# version 11
# baseline (speedup 1.0000x reference)
import numpy as np
from contextlib import ExitStack

import concourse.bass as bass
import concourse.mybir as mybir
from concourse.bass_utils import run_bass_kernel_spmd
from concourse.aot_env import allow_ici_all_to_all

F32 = mybir.dt.float32
BF16 = mybir.dt.bfloat16
AF = mybir.ActivationFunctionType
ALU = mybir.AluOpType
AX = mybir.AxisListType

B, S, D = 8, 2048, 1024
NT = 16
NC = 8
FF = 4096
NE = 8
MASK_L = 30720.0

C_EPS, C_INV1024, C_NEGINV1024, C_INV32, C_INV8, C_NEG1, C_NEGBIG, C_NEGHALF, C_ZERO, C_ONE = range(10)


class P:
    def __init__(self, nc):
        self.nc = nc
        self.cnt = {}
        self.sems = {}

    def sem(self, stack, name):
        self.sems[name] = stack.enter_context(self.nc.semaphore(name))
        self.cnt[name] = 0

    def inc(self, instr, name, n=1):
        instr.then_inc(self.sems[name], n)
        self.cnt[name] += n
        return self.cnt[name]

    def v(self, name):
        return self.cnt[name]

    def w(self, eng, name, val):
        if val > 0:
            eng.wait_ge(self.sems[name], val)

    def gsync(self, g):
        self.inc(g.engine_nop(), "ge")
        self.w(g, "ge", self.v("ge"))

    def wt(self, eng, tup):
        if tup and tup[1] > 0:
            eng.wait_ge(self.sems[tup[0]], tup[1])

    def dma(self, q, stream, slot, out, in_, slow=False):
        sem = f"{stream}{slot}"
        v = self.inc(q.dma_start(out=out, in_=in_, allow_slow_non_contiguous=slow), sem, 16)
        return (sem, v)


def build_program(parts="AB"):
    nc = bass.Bass("TRN2", target_bir_lowering=False)

    x_in = nc.dram_tensor("x", [S, D], F32, kind="ExternalInput")
    wvT_d = nc.dram_tensor("wvT", [128, NC, D], F32, kind="ExternalInput")
    wqkT_d = nc.dram_tensor("wqkT", [2, NC, 128, D], F32, kind="ExternalInput")
    woT_d = nc.dram_tensor("woT", [128, NC, D], F32, kind="ExternalInput")
    inqkT_d = nc.dram_tensor("inqkT", [16, 128, D], F32, kind="ExternalInput")
    invT_d = nc.dram_tensor("invT", [128, NC, D], F32, kind="ExternalInput")
    inb_d = nc.dram_tensor("inb", [3 * D], F32, kind="ExternalInput")
    inqkb_d = nc.dram_tensor("inqkb", [2, 128, NC], F32, kind="ExternalInput")
    outwT_d = nc.dram_tensor("outwT", [128, NC, D], F32, kind="ExternalInput")
    outb_d = nc.dram_tensor("outb", [D], F32, kind="ExternalInput")
    gwT_d = nc.dram_tensor("gwT", [128, NC, NE], F32, kind="ExternalInput")
    gb_d = nc.dram_tensor("gb", [NE, 1], F32, kind="ExternalInput")
    w1T_d = nc.dram_tensor("w1T", [NE, 32, 128, D], F32, kind="ExternalInput")
    b1_d = nc.dram_tensor("b1", [NE, 128, 32], F32, kind="ExternalInput")
    w2T_d = nc.dram_tensor("w2T", [NE, FF, D], F32, kind="ExternalInput")
    b2_d = nc.dram_tensor("b2", [NE, D], F32, kind="ExternalInput")
    lng_d = nc.dram_tensor("lng", [4, D], F32, kind="ExternalInput")

    w2bf_d = nc.dram_tensor("w2bf", [NE, FF, D], BF16)
    wqkbf_d = nc.dram_tensor("wqkbf", [2, NC, 128, D], BF16)
    inqkbf_d = nc.dram_tensor("inqkbf", [16, 128, D], BF16)
    invbf_d = nc.dram_tensor("invbf", [128, NC, D], BF16)
    outwbf_d = nc.dram_tensor("outwbf", [128, NC, D], BF16)
    a2a_in = nc.dram_tensor("a2a_in", [S, D], F32)
    a2a_out = nc.dram_tensor("a2a_out", [S, D], F32)
    x1p = nc.dram_tensor("x1p", [S, D], F32)
    tokspill = nc.dram_tensor("tokspill", [S, D], F32)
    y_out = nc.dram_tensor("y", [S, D], F32, kind="ExternalOutput")

    p = P(nc)

    with ExitStack() as outer:
        def sb(st, name, shape, dt=F32):
            return st.enter_context(nc.sbuf_tensor(name, shape, dt))

        ident = sb(outer, "ident", [128, 128], BF16)
        consts = sb(outer, "consts", [128, 10], F32)
        stat = sb(outer, "stat", [128, 12], F32)
        negmax = sb(outer, "negmax", [128, 2], F32)
        rowsum = sb(outer, "rowsum", [128, 2], F32)
        recipA = sb(outer, "recipA", [128, NT], F32)
        nmr2 = sb(outer, "nmr2", [128, NT], F32)
        rstd2 = sb(outer, "rstd2", [128, NT], F32)
        sqscr = sb(outer, "sqscr", [128, 1024], F32)
        magic = sb(outer, "magic", [128, 1], mybir.dt.int32)

        ps_s = outer.enter_context(nc.psum_tensor("ps_s", [128, 2048], F32))
        ps_a = outer.enter_context(nc.psum_tensor("ps_a", [128, 1024], F32))
        pst = outer.enter_context(nc.psum_tensor("pst", [128, 2048], BF16))

        for s_ in ("te", "ve", "se", "ge", "cc", "bar1", "bar2",
                   "xld0", "xld1", "xs0", "xs1", "x1p0", "wA0",
                   "wqk0", "wqk1", "wqk2", "wqk3",
                   "lds0", "vst0", "bx0", "bx1", "sp0", "sp1", "wV0",
                   "iqa0", "iqa1", "iqa2", "iqa3", "iqb0", "iqb1", "iqb2", "iqb3",
                   "gwb0", "b10",
                   "w1a0", "w1a1", "w1a2", "w1a3", "w1b0", "w1b1", "w1b2", "w1b3",
                   "w20", "fl0", "fl1", "st0", "st1",
                   "wc0", "wc1", "wc2", "wc3", "wc4", "wc5", "wc6", "wc7",
                   "wqc0", "wqc1", "iqc0", "ivc0"):
            p.sem(outer, s_)

        def ccol(i):
            return consts[:, i:i + 1]

        blk = nc.Block()
        blk.__enter__()

        @blk.gpsimd
        def _(g):
            g.memset(ident[:, :], 0.0)
            p.gsync(g)
            g.affine_select(out=ident[:, :], in_=ident[:, :], compare_op=ALU.not_equal,
                            fill=1.0, base=0, pattern=[[-1, 128]], channel_multiplier=1)
            for idx, v_ in [(C_EPS, 1e-5), (C_INV1024, 1.0 / 1024.0),
                            (C_NEGINV1024, -1.0 / 1024.0), (C_INV32, 1.0 / 32.0),
                            (C_INV8, 1.0 / 8.0), (C_NEG1, -1.0), (C_NEGBIG, -1e30),
                            (C_NEGHALF, -0.5), (C_ZERO, 0.0), (C_ONE, 1.0)]:
                g.memset(consts[:, idx:idx + 1], v_)
            g.memset(magic[:, :], 0x5f3759df)
            g.memset(stat[:, :], 0.0)
            p.inc(g.engine_nop(), "ge")
        GE_SETUP = p.v("ge")

        blk.__exit__(None, None, None)

        def ln_stats(blk_, src_ap, m, wait_ld=None, nmr_dst=None, rstd_dst=None, wait_ve=0, wait_se=0):
            musum = stat[:, m:m + 1]
            negmu = stat[:, 2 + m:3 + m]
            sd = stat[:, 4 + m:5 + m]
            rstd = rstd_dst if rstd_dst is not None else stat[:, 6 + m:7 + m]
            nmr = nmr_dst if nmr_dst is not None else stat[:, 8 + m:9 + m]

            @blk_.vector
            def _(v):
                p.wt(v, wait_ld)
                p.w(v, "ve", wait_ve)
                p.w(v, "se", wait_se)
                _v = v.reduce_sum(musum, src_ap, axis=AX.X)
                v.drain()
                _v = v.tensor_scalar_mul(negmu, musum, ccol(C_NEGINV1024))
                p.inc(_v, "ve")
            ve_req = p.v("ve")

            @blk_.scalar
            def _(s):
                p.w(s, "ve", ve_req)
                _a = s.activation(sqscr[:, :], src_ap, AF.Square, bias=negmu,
                             accum_out=musum)
                s.drain()
                _a = s.activation(sd, musum, AF.Ln, bias=ccol(C_EPS), scale=ccol(C_INV1024))
                s.drain()
                _a = s.activation(rstd, sd, AF.Exp, bias=ccol(C_ZERO), scale=ccol(C_NEGHALF))
                s.drain()
                _a = s.activation(nmr, negmu, AF.Identity, bias=ccol(C_ZERO), scale=rstd)
                p.inc(_a, "se")
                s.drain()
            return nmr, rstd, ve_req

        def barrier(blk_, name):
            for dec, sem_ in ((blk_.vector, "ve"), (blk_.scalar, "se"), (blk_.gpsimd, "ge")):
                @dec
                def _(e, sem_=sem_):
                    p.inc(e.engine_nop() if sem_ != "se" else e.drain(), sem_)
            snap = {k: v for k, v in p.cnt.items() if v > 0 and k not in ("bar1", "bar2")}
            for dec in (blk_.tensor, blk_.vector, blk_.scalar, blk_.gpsimd, blk_.sync):
                @dec
                def _(e):
                    for k, v in snap.items():
                        p.w(e, k, v)
                    p.inc(e.nop(), name)
            tgt = p.v(name)
            for dec in (blk_.tensor, blk_.vector, blk_.scalar, blk_.gpsimd, blk_.sync):
                @dec
                def _(e):
                    p.w(e, name, tgt)

        pa = ExitStack()
        q8_buf = sb(pa, "q8_buf", [128, NC, NC, 128], BF16)
        wv_sb = sb(pa, "wv_sb", [128, NC, 1024], BF16)
        wo_sb = sb(pa, "wo_sb", [128, NC, 1024], BF16)
        x_fm = sb(pa, "x_fm", [128, NC, 2048], BF16)
        K_sb = sb(pa, "K_sb", [128, NC, 2048], BF16)
        V_sb = sb(pa, "V_sb", [128, NT, 1024], BF16)
        Q_sb = sb(pa, "Q_sb", [128, NC, 512], BF16)
        P_sb = sb(pa, "P_sb", [128, 1, 2048], BF16)
        PT_sb = sb(pa, "PT_sb", [128, 16, 512], BF16)
        CT_sb = sb(pa, "CT_sb", [128, NC, 512], BF16)
        xin = sb(pa, "xin", [128, 2, 1024], F32)
        xbf = sb(pa, "xbf", [128, 2, 1024], BF16)
        xin2 = xin
        x1buf = sb(pa, "x1buf", [128, 2, 1024], F32)

        blkA = nc.Block()
        blkA.__enter__()

        @blkA.gpsimd
        def _(g):
            for wsb, src in ((wv_sb, wvT_d[:, :, :]), (wo_sb, woT_d[:, :, :])):
                p.dma(g, "wA", 0, wsb[:, :, :], src)
        LDW_A = ("wA0", p.v("wA0"))
        def load_wqk8(wsel, sem_slot, wait_tup):
            @blkA.sync
            def _(s, wsel=wsel, sem_slot=sem_slot, wait_tup=wait_tup):
                p.wt(s, wait_tup)
                p.w(s, "wqc%d" % wsel, 128)
                for ot in range(NC):
                    p.dma(s, "wqk", sem_slot, q8_buf[:, ot, :, :],
                          wqkbf_d[wsel, ot].rearrange("p (c o) -> p c o", o=128))
            return ("wqk%d" % sem_slot, p.v("wqk%d" % sem_slot))

        pst_rel = [0, 0]

        @blkA.vector
        def _(v):
            p.w(v, "ge", GE_SETUP)

        @blkA.scalar
        def _(s):
            p.w(s, "ge", GE_SETUP)

        se_norm = [0] * NT
        te_a1 = [0] * NT
        evV = [0] * NT
        for i in range(NT):
            m = i % 2

            @blkA.sync
            def _(s, i=i, m=m):
                if i >= 2:
                    p.w(s, "se", se_norm[i - 2])
                p.dma(s, "xld", m, xin[:, m, :], x_in[128 * i:128 * (i + 1), :])
            ldxv = ("xld%d" % m, p.v("xld%d" % m))

            nmr, rstd, ve_ln = ln_stats(blkA, xin[:, m, :], m, wait_ld=ldxv,
                                        wait_se=(se_norm[i - 2] if i >= 2 else 0))

            @blkA.scalar
            def _(s, i=i, m=m, nmr=nmr, rstd=rstd, ve_ln=ve_ln):
                p.w(s, "ve", ve_ln)
                if i >= 2:
                    p.w(s, "te", te_a1[i - 2])
                _a = s.activation(xbf[:, m, :], xin[:, m, :], AF.Identity, bias=nmr, scale=rstd)
                p.inc(_a, "se")
            se_norm[i] = p.v("se")

            @blkA.gpsimd
            def _(g, i=i, sev=se_norm[i]):
                p.w(g, "se", sev)
                wsel, ot = (1, i) if i < 8 else (0, i - 8)
                p.dma(g, "wqc", wsel, wqkbf_d[wsel, ot], wqkT_d[wsel, ot])

            @blkA.tensor
            def _(t, i=i, m=m, pr=pst_rel[m]):
                p.w(t, "se", se_norm[i])
                if i == 0:
                    p.w(t, "ge", GE_SETUP)
                p.w(t, "ve", pr)
                for j in range(8):
                    _mm = t.transpose(pst[:, 1024 * m + 128 * j:1024 * m + 128 * (j + 1)],
                                xbf[:, m, 128 * j:128 * (j + 1)], ident[:, :])
                p.inc(_mm, "te")
            te_tr = p.v("te")
            te_a1[i] = te_tr

            @blkA.vector
            def _(v, i=i, m=m, te_tr=te_tr):
                p.w(v, "te", te_tr)
                _v = v.tensor_copy(x_fm[:, 0:8, 128 * i:128 * (i + 1)],
                              pst[:, 1024 * m:1024 * (m + 1)].rearrange("p (c t) -> p c t", t=128))
                p.inc(_v, "ve")
            pst_rel[m] = p.v("ve")

            @blkA.tensor
            def _(t, tt=i, xrdy=pst_rel[m]):
                if tt == 0:
                    p.wt(t, LDW_A)
                p.w(t, "ve", xrdy)
                if tt >= 1:
                    p.w(t, "se", evV[tt - 1])
                for c in range(NC):
                    for half in range(2):
                        _mm = t.matmul(ps_a[:, 512 * half:512 * (half + 1)],
                                 x_fm[:, c, 128 * tt:128 * (tt + 1)],
                                 wv_sb[:, c, 512 * half:512 * (half + 1)],
                                 start=(c == 0), stop=(c == NC - 1))
                p.inc(_mm, "te")
            teV = p.v("te")

            @blkA.scalar
            def _(s, tt=i, teV=teV):
                p.w(s, "te", teV)
                _a = s.copy(V_sb[:, tt, :], ps_a[:, :])
                p.inc(_a, "se")
            evV[i] = p.v("se")
        VE_XFM = p.v("ve")

        if parts == "A1x":
            @blkA.gpsimd
            def _(g):
                p.w(g, "ve", VE_XFM)
                p.dma(g, "xs", 0, y_out[0:128, :], xbf[:, 0, :])
                p.dma(g, "xs", 0, y_out[128:256, :], xbf[:, 1, :])
                p.dma(g, "xs", 0, y_out[256:384, :], xin[:, 0, :])
                p.dma(g, "xs", 0, y_out[384:512, :], xin[:, 1, :])
                p.dma(g, "xs", 0, y_out[512:640, 0:12], stat[:, :])

            @blkA.sync
            def _(s):
                p.w(s, "xs0", p.v("xs0"))
            blkA.__exit__(None, None, None)
            pa.close()
            return nc

        if parts == "A1":
            @blkA.gpsimd
            def _(g):
                p.w(g, "ve", VE_XFM)
                p.dma(g, "xs", 0, y_out[:, :].flatten().rearrange("(q c t) -> q c t", q=128, c=8),
                      x_fm[:, :, :])

            @blkA.sync
            def _(s):
                p.w(s, "xs0", p.v("xs0"))
            blkA.__exit__(None, None, None)
            pa.close()
            return nc

        LDK_A2 = load_wqk8(1, 0, None)
        evK = [0] * NC
        for ot in range(NC):
            @blkA.tensor
            def _(t, ot=ot):
                if ot == 0:
                    p.wt(t, LDK_A2)
                    p.w(t, "ve", VE_XFM)
                if ot >= 1:
                    p.w(t, "ve", evK[ot - 1])
                for c in range(NC):
                    for nb in range(4):
                        _mm = t.matmul(ps_s[:, 512 * nb:512 * (nb + 1)],
                                 q8_buf[:, ot, c, :],
                                 x_fm[:, c, 512 * nb:512 * (nb + 1)],
                                 start=(c == 0), stop=(c == NC - 1))
                p.inc(_mm, "te")
            teK = p.v("te")

            @blkA.vector
            def _(v, ot=ot, teK=teK):
                p.w(v, "te", teK)
                _v = v.tensor_copy(K_sb[:, ot, :], ps_s[:, :])
                p.inc(_v, "ve")
            evK[ot] = p.v("ve")

        if parts in ("A2K", "A2V"):
            src_t = K_sb if parts == "A2K" else V_sb
            @blkA.gpsimd
            def _(g, src_t=src_t):
                p.w(g, "ve", evK[NC - 1])
                p.w(g, "se", evV[NT - 1])
                shp = src_t.shape
                p.dma(g, "xs", 0, y_out[:, :].flatten().rearrange("(q c t) -> q c t", q=128,
                                                                  c=shp[1]),
                      src_t[:, :, :])

            @blkA.sync
            def _(s):
                p.w(s, "xs0", p.v("xs0"))
            blkA.__exit__(None, None, None)
            pa.close()
            return nc

        te_scores_done = 0
        te_ctx_done = 0
        ve_outp_done = 0
        ve_x1rel = [0, 0]
        ldx_x1st = [0, 0]
        LDQ_A3 = load_wqk8(0, 1, ("te", p.v("te")))
        for g_ in range(4):
            seQ = [0] * NC
            for ot in range(NC):
                @blkA.tensor
                def _(t, g_=g_, ot=ot):
                    if g_ == 0 and ot == 0:
                        p.wt(t, LDQ_A3)
                    if g_ == 0 and ot < 2:
                        p.w(t, "se", evV[NT - 1])
                    if ot >= 2:
                        p.w(t, "se", seQ[ot - 2])
                    if g_ > 0 and ot < 2:
                        p.w(t, "ve", ve_outp_done)
                    for c in range(NC):
                        _mm = t.matmul(ps_a[:, 512 * (ot % 2):512 * (ot % 2 + 1)],
                                 q8_buf[:, ot, c, :],
                                 x_fm[:, c, 512 * g_:512 * (g_ + 1)],
                                 start=(c == 0), stop=(c == NC - 1))
                    p.inc(_mm, "te")
                teq = p.v("te")

                ck = 8 * (g_ // 2) + ot
                if g_ % 2 == 0:
                    @blkA.gpsimd
                    def _(g, teq=teq, ck=ck):
                        p.w(g, "te", teq)
                        if ck < 8:
                            p.dma(g, "ivc", 0, invbf_d[:, ck, :], invT_d[:, ck, :])
                        else:
                            p.dma(g, "ivc", 0, outwbf_d[:, ck - 8, :],
                                  outwT_d[:, ck - 8, :])
                else:
                    @blkA.gpsimd
                    def _(g, teq=teq, ck=ck):
                        p.w(g, "te", teq)
                        p.dma(g, "iqc", 0, inqkbf_d[ck], inqkT_d[ck])

                @blkA.scalar
                def _(s, ot=ot, teq=teq, g_=g_):
                    p.w(s, "te", teq)
                    if g_ > 0 and ot == 0:
                        p.w(s, "te", te_scores_done)
                    _a = s.activation(Q_sb[:, ot, :], ps_a[:, 512 * (ot % 2):512 * (ot % 2 + 1)],
                                 AF.Identity, bias=ccol(C_ZERO), scale=ccol(C_INV32))
                    p.inc(_a, "se")
                seQ[ot] = p.v("se")

            se_exp = [0] * 4
            ve_recip = [0] * 4
            te_ptr = [0] * 4
            for qt in range(4):
                m = qt % 2
                tq = 4 * g_ + qt

                @blkA.tensor
                def _(t, qt=qt, g_=g_):
                    if qt == 0:
                        p.w(t, "se", seQ[NC - 1])
                        if g_ == 0:
                            p.w(t, "ve", evK[NC - 1])
                    if qt >= 1:
                        p.w(t, "se", se_exp[qt - 1])
                    for c in range(NC):
                        for nb in range(4):
                            _mm = t.matmul(ps_s[:, 512 * nb:512 * (nb + 1)],
                                     Q_sb[:, c, 128 * qt:128 * (qt + 1)],
                                     K_sb[:, c, 512 * nb:512 * (nb + 1)],
                                     start=(c == 0), stop=(c == NC - 1))
                    p.inc(_mm, "te")
                te_sc = p.v("te")
                if qt == 3:
                    te_scores_done = te_sc

                @blkA.vector
                def _(v, m=m, te_sc=te_sc):
                    p.w(v, "te", te_sc)
                    _v = v.reduce_max(negmax[:, m:m + 1], ps_s[:, :], axis=AX.X, negate=True)
                    p.inc(_v, "ve")
                ve_mx = p.v("ve")

                @blkA.scalar
                def _(s, m=m, ve_mx=ve_mx, qt=qt, ve_rp=(ve_recip[qt - 2] if qt >= 2 else 0)):
                    p.w(s, "ve", ve_mx)
                    p.w(s, "ve", ve_rp)
                    if qt >= 1:
                        p.w(s, "te", te_ptr[qt - 1])
                    _a = s.activation(P_sb[:, 0, :], ps_s[:, :], AF.Exp,
                                 bias=negmax[:, m:m + 1], scale=1.0,
                                 accum_out=rowsum[:, m:m + 1])
                    p.inc(_a, "se")
                se_exp[qt] = p.v("se")

                @blkA.vector
                def _(v, m=m, tq=tq, se_e=se_exp[qt]):
                    p.w(v, "se", se_e)
                    _v = v.reciprocal(recipA[:, tq:tq + 1], rowsum[:, m:m + 1])
                    p.inc(_v, "ve")
                ve_recip[qt] = p.v("ve")

                for half in range(2):
                    @blkA.tensor
                    def _(t, m=m, half=half, se_e=se_exp[qt], pr=pst_rel[half]):
                        p.w(t, "se", se_e)
                        p.w(t, "ve", pr)
                        for kk in range(8):
                            kc = 8 * half + kk
                            _mm = t.transpose(pst[:, 1024 * half + 128 * kk:1024 * half + 128 * (kk + 1)],
                                        P_sb[:, 0, 128 * kc:128 * (kc + 1)], ident[:, :])
                        p.inc(_mm, "te")
                    te_ptr[qt] = p.v("te")

                    @blkA.vector
                    def _(v, half=half, qt=qt, te_p=te_ptr[qt]):
                        p.w(v, "te", te_p)
                        _v = v.tensor_copy(PT_sb[:, 8 * half:8 * (half + 1), 128 * qt:128 * (qt + 1)],
                                      pst[:, 1024 * half:1024 * (half + 1)]
                                      .rearrange("p (c t) -> p c t", t=128))
                        p.inc(_v, "ve")
                    pst_rel[half] = p.v("ve")

            se_ctx = [0] * NC
            for dt_ in range(NC):
                @blkA.tensor
                def _(t, dt_=dt_, pr0=pst_rel[0], pr1=pst_rel[1]):
                    if dt_ == 0:
                        p.w(t, "ve", pr0)
                        p.w(t, "ve", pr1)
                    if dt_ >= 2:
                        p.w(t, "se", se_ctx[dt_ - 2])
                    for kc in range(16):
                        _mm = t.matmul(ps_a[:, 512 * (dt_ % 2):512 * (dt_ % 2 + 1)],
                                 V_sb[:, kc, 128 * dt_:128 * (dt_ + 1)],
                                 PT_sb[:, kc, :],
                                 start=(kc == 0), stop=(kc == 15))
                    p.inc(_mm, "te")
                te_c = p.v("te")
                if dt_ == NC - 1:
                    te_ctx_done = te_c

                @blkA.scalar
                def _(s, dt_=dt_, te_c=te_c):
                    p.w(s, "te", te_c)
                    _a = s.copy(CT_sb[:, dt_, :], ps_a[:, 512 * (dt_ % 2):512 * (dt_ % 2 + 1)])
                    p.inc(_a, "se")
                se_ctx[dt_] = p.v("se")

            ve_outp = [0] * 4
            for tt in range(4):
                tq = 4 * g_ + tt
                m = tt % 2
                first = (g_ == 0 and tt < 2)

                @blkA.sync
                def _(s, tq=tq, m=m, first=first):
                    if first:
                        p.w(s, "se", se_norm[14 + m])
                    else:
                        p.w(s, "ve", ve_x1rel[m])
                    p.dma(s, "xld", m, xin2[:, m, :], x_in[128 * tq:128 * (tq + 1), :])
                ldx2 = ("xld%d" % m, p.v("xld%d" % m))

                @blkA.tensor
                def _(t, tt=tt, g_=g_):
                    if tt == 0:
                        p.w(t, "se", se_ctx[NC - 1])
                        if g_ == 0:
                            p.wt(t, LDW_A)
                    if tt >= 1:
                        p.w(t, "ve", ve_outp[tt - 1])
                    for c in range(NC):
                        for half in range(2):
                            _mm = t.matmul(ps_a[:, 512 * half:512 * (half + 1)],
                                     CT_sb[:, c, 128 * tt:128 * (tt + 1)],
                                     wo_sb[:, c, 512 * half:512 * (half + 1)],
                                     start=(c == 0), stop=(c == NC - 1))
                    p.inc(_mm, "te")
                te_o = p.v("te")

                @blkA.vector
                def _(v, tq=tq, m=m, te_o=te_o, ldx2=ldx2, first=first, ve_r=ve_recip[3]):
                    p.w(v, "te", te_o)
                    p.wt(v, ldx2)
                    p.w(v, "ve", ve_r)
                    if not first:
                        p.wt(v, ldx_x1st[m])
                    _v = v.scalar_tensor_tensor(x1buf[:, m, :], ps_a[:, :], recipA[:, tq:tq + 1],
                                           xin2[:, m, :], ALU.mult, ALU.add)
                    p.inc(_v, "ve")
                ve_outp[tt] = p.v("ve")
                ve_x1rel[m] = ve_outp[tt]

                @blkA.sync
                def _(s, tq=tq, m=m, ve_o=ve_outp[tt]):
                    p.w(s, "ve", ve_o)
                    dst = y_out if parts == "A" else a2a_in
                    p.dma(s, "xs", m, dst[128 * tq:128 * (tq + 1), :], x1buf[:, m, :])
                ldx_x1st[m] = ("xs%d" % m, p.v("xs%d" % m))
            ve_outp_done = ve_outp[3]

        if parts == "A":
            @blkA.sync
            def _(s):
                p.wt(s, ldx_x1st[0])
                p.wt(s, ldx_x1st[1])
            blkA.__exit__(None, None, None)
            pa.close()
            return nc

        @blkA.gpsimd
        def _(g):
            p.wt(g, ldx_x1st[0])
            p.wt(g, ldx_x1st[1])
            with allow_ici_all_to_all():
                p.inc(g.collective_compute(
                    "AllToAll", ALU.bypass, replica_groups=[list(range(8))],
                    ins=[a2a_in.ap().opt()], outs=[a2a_out.ap().opt()]), "cc")
        CC1 = p.v("cc")

        @blkA.sync
        def _(s):
            p.w(s, "cc", CC1)
            for sh in range(8):
                p.dma(s, "x1p", 0, x1p[:, :].rearrange("(q s) d -> s q d", s=8)[sh],
                      a2a_out[256 * sh:256 * (sh + 1), :])
        LDX_X1P = ("x1p0", p.v("x1p0"))

        barrier(blkA, "bar1")
        blkA.__exit__(None, None, None)
        pa.close()
        pst_rel = [0, 0]

        pbP = ExitStack()
        tok_fm = sb(pbP, "tok_fm", [128, NC, 2048], BF16)
        gw_sb = sb(pbP, "gw_sb", [128, NC, 8], BF16)
        gb_sb = sb(pbP, "gb_sb", [8, 1], F32)
        b2_sb = sb(pbP, "b2_sb", [8, 1024], BF16)
        qb8 = sb(pbP, "qb8", [128, NC], F32)
        kbc = sb(pbP, "kbc", [128, NC], F32)
        combine = sb(pbP, "combine", [128, NT, 8], F32)
        combine_bf = sb(pbP, "combine_bf", [128, NT, 8], BF16)
        combT_sb = sb(pbP, "combT_sb", [8, NT, 128], BF16)
        zeros_bf = sb(pbP, "zeros_bf", [128, 1024], BF16)
        mask17 = sb(pbP, "mask17", [17, 256], BF16)
        ident32 = sb(pbP, "ident32", [128, 128], F32)
        onesrow = sb(pbP, "onesrow", [1, 128], F32)

        blkP = nc.Block()
        blkP.__enter__()

        @blkP.gpsimd
        def _(g):
            p.w(g, "bar1", p.v("bar1"))
            g.memset(ident32[:, :], 0.0)
            p.gsync(g)
            g.affine_select(out=ident32[:, :], in_=ident32[:, :], compare_op=ALU.not_equal,
                            fill=1.0, base=0, pattern=[[-1, 128]], channel_multiplier=1)
            g.memset(mask17[:, :], 0.0)
            for half, scl in ((0, 1.0), (128, MASK_L)):
                p.gsync(g)
                g.affine_select(out=mask17[0:16, half:half + 128], in_=mask17[0:16, half:half + 128],
                                compare_op=ALU.is_gt, fill=scl, base=-7,
                                pattern=[[1, 128]], channel_multiplier=-8)
                p.gsync(g)
                g.affine_select(out=mask17[0:16, half:half + 128], in_=mask17[0:16, half:half + 128],
                                compare_op=ALU.is_ge, fill=0.0, base=0,
                                pattern=[[1, 128]], channel_multiplier=-8)
            p.gsync(g)
            g.affine_select(out=mask17[:, 0:128], in_=mask17[:, 0:128],
                            compare_op=ALU.is_ge, fill=1.0, base=15,
                            pattern=[[0, 128]], channel_multiplier=-1)
            g.affine_select(out=mask17[:, 128:256], in_=mask17[:, 128:256],
                            compare_op=ALU.is_ge, fill=-MASK_L, base=15,
                            pattern=[[0, 128]], channel_multiplier=-1)
            g.memset(onesrow[:, :], 1.0)
            g.memset(zeros_bf[:, :], 0.0)
            p.dma(g, "gwb", 0, gw_sb[:, :, :], gwT_d[:, :, :])
            p.dma(g, "gwb", 0, b2_sb[:, :], b2_d[:, :])
            p.inc(g.engine_nop(), "ge")
        GE_PB = p.v("ge")

        @blkP.sync
        def _(s):
            p.dma(s, "lds", 0, gb_sb[:, :], gb_d[:, :])
            p.dma(s, "lds", 0, qb8[:, :], inqkb_d[0])
            p.dma(s, "lds", 0, kbc[:, :], inqkb_d[1])
        LDX_PB = ("lds0", p.v("lds0"))

        blkP.__exit__(None, None, None)

        pb1 = ExitStack()
        inqk_buf = sb(pb1, "inqk_buf", [128, 2, NC, 128], BF16)
        inv_sb = sb(pb1, "inv_sb", [128, NC, 1024], BF16)
        outwT_sb = sb(pb1, "outwT_sb", [128, NC, 1024], BF16)
        src_fm = sb(pb1, "src_fm", [128, NC, 1024], BF16)
        q_fm = sb(pb1, "q_fm", [128, NC, 1024], BF16)
        k_fm = sb(pb1, "k_fm", [128, NC, 1024], BF16)
        v_tm = sb(pb1, "v_tm", [128, 8, 1024], BF16)
        E_sb = sb(pb1, "E_sb", [128, 2, 2048], BF16)
        A_sb = E_sb
        AT_sb = sb(pb1, "AT_sb", [128, 16, 128], BF16)
        ctx_t = sb(pb1, "ctx_t", [128, 1, NC, 128], BF16)
        bxin = sb(pb1, "bxin", [128, 1, 1024], F32)
        srcb = sb(pb1, "srcb", [128, 2, 1024], F32)
        tokf = sb(pb1, "tokf", [128, 2, 1024], F32)
        tokb = sb(pb1, "tokb", [128, 2, 1024], BF16)
        sbf = sb(pb1, "sbf", [128, 1, 1024], BF16)
        bc_outb = sb(pb1, "bc_outb", [128, 1024], F32)
        bc_vb = sb(pb1, "bc_vb", [128, 1024], F32)
        bc_g1 = sb(pb1, "bc_g1", [128, 1024], F32)
        bc_b1 = sb(pb1, "bc_b1", [128, 1024], F32)
        seg16 = sb(pb1, "seg16", [128, 16], F32)
        vstage1 = sb(pb1, "vstage1", [1, 1024], F32)
        rec16 = sb(pb1, "rec16", [128, 16], F32)
        rbuf = sb(pb1, "rbuf", [128, 8], F32)

        blkB = nc.Block()
        blkB.__enter__()

        @blkB.sync
        def _(s):
            p.w(s, "bar1", p.v("bar1"))
            p.w(s, "ivc0", 256)
            p.dma(s, "wV", 0, inv_sb[:, :, :], invbf_d[:, :, :])
            p.dma(s, "wV", 0, outwT_sb[:, :, :], outwbf_d[:, :, :])
        LDW_INV = ("wV0", p.v("wV0"))

        w2cast_k = [0]

        def issue_w2cast(blk_, wait_sem, wait_val):
            k = w2cast_k[0]
            if k >= 64:
                return
            w2cast_k[0] = k + 1

            @blk_.gpsimd
            def _(g, k=k, wait_sem=wait_sem, wait_val=wait_val):
                p.w(g, wait_sem, wait_val)
                e_, j = divmod(k, 8)
                p.dma(g, "wc", e_, w2bf_d[e_, 512 * j:512 * (j + 1), :],
                      w2T_d[e_, 512 * j:512 * (j + 1), :])

        inqk_rel = [0, 0]

        def load_inqk(idx, slot):
            res = []

            @blkB.sync
            def _(s, idx=idx, slot=slot, rel=inqk_rel[slot]):
                p.w(s, "te", rel)
                if idx == 0:
                    p.w(s, "iqc0", 256)
                res.append(p.dma(s, "iqa", slot,
                                 inqk_buf[:, slot, :, :],
                                 inqkbf_d[idx].rearrange("p (c o) -> p c o", o=128)))
            return res

        def build_bcasts(blk_, specs, stage, wait0_ve=0):
            te_prev = 0
            ve_prev = 0
            out_ves = []
            for j, (vec_ap, bct) in enumerate(specs):
                @blk_.sync
                def _(s, j=j, vec_ap=vec_ap, te_prev=te_prev, stage=stage):
                    if j == 0:
                        p.w(s, "bar1", p.v("bar1"))
                    p.w(s, "te", te_prev)
                    p.dma(s, "vst", 0, stage[0:1, :], vec_ap)
                ldx_v = ("vst0", p.v("vst0"))

                @blk_.tensor
                def _(t, j=j, ldx_v=ldx_v, ve_prev=ve_prev, wait0_ve=wait0_ve, stage=stage):
                    p.wt(t, ldx_v)
                    if j == 0 and wait0_ve:
                        p.w(t, "ve", wait0_ve)
                    if j >= 1:
                        p.w(t, "ve", ve_prev)
                    for half in range(2):
                        _mm = t.matmul(ps_a[:, 512 * half:512 * (half + 1)], onesrow[:, :],
                                 stage[0:1, 512 * half:512 * (half + 1)],
                                 start=True, stop=True)
                    p.inc(_mm, "te")
                te_prev = p.v("te")

                @blk_.vector
                def _(v, bct=bct, te_prev=te_prev):
                    p.w(v, "te", te_prev)
                    _v = v.tensor_copy(bct[:, :], ps_a[:, :])
                    p.inc(_v, "ve")
                ve_prev = p.v("ve")
                out_ves.append(ve_prev)
            return ve_prev

        _r1 = lambda apx: apx.rearrange("(a d) -> a d", a=1)
        VE_BC = build_bcasts(blkB, [
            (_r1(outb_d[:]), bc_outb),
            (_r1(inb_d[2048:3072]), bc_vb),
            (_r1(lng_d[0]), bc_g1),
            (_r1(lng_d[1]), bc_b1),
        ], vstage1)

        @blkB.vector
        def _(v):
            p.wt(v, LDX_PB)
            _v = v.tensor_scalar_mul(qb8[:, :], qb8[:, :], ccol(C_INV8))
            p.inc(_v, "ve")
        VE_QB8 = p.v("ve")

        ve_tok_ready = [0] * NT
        se_tokb = [0] * NT
        ve_tokev = [0] * NT
        te_toktr = [0] * NT
        ldx_spill = [0] * NT
        toktr_queue = []

        se_exp_b = {}
        ve_an = {}
        te_at = {}
        ve_atv = {}
        ve_ctxev = {}
        ve_z1 = {}
        ve_outev = {}
        te_op_prev = {}
        se_n1_prev = {}
        ldx_at = {}
        se_srcb = {}
        H0_ATTN_TE = 0

        def emit_tok_transpose(ti):
            m = ti % 2
            bank = ti % 2

            @blkB.tensor
            def _(t, ti=ti, m=m, bank=bank, pr=pst_rel[bank]):
                p.w(t, "se", se_tokb[ti])
                p.w(t, "ve", pr)
                for j in range(8):
                    _mm = t.transpose(pst[:, 1024 * bank + 128 * j:1024 * bank + 128 * (j + 1)],
                                tokb[:, m, 128 * j:128 * (j + 1)], ident[:, :])
                p.inc(_mm, "te")
            te_toktr[ti] = p.v("te")

            @blkB.vector
            def _(v, ti=ti, bank=bank, te_t=te_toktr[ti]):
                p.w(v, "te", te_t)
                _v = v.tensor_copy(tok_fm[:, 0:8, 128 * ti:128 * (ti + 1)],
                              pst[:, 1024 * bank:1024 * (bank + 1)]
                              .rearrange("p (c t) -> p c t", t=128))
                p.inc(_v, "ve")
            pst_rel[bank] = p.v("ve")
            ve_tokev[ti] = p.v("ve")

        for H in range(2):
            tiles = list(range(8 * H, 8 * H + 8))

            se_nrm = {}
            te_bln = {}
            for ti in tiles:
                m = ti % 2

                @blkB.sync
                def _(s, ti=ti, m=m, H=H):
                    if ti == 0:
                        p.wt(s, LDX_X1P)
                    if ti >= 8 * H + 1:
                        p.w(s, "se", se_nrm[ti - 1])
                    if H == 1 and ti == 8:
                        p.w(s, "se", se_srcb.get(7, 0))
                    p.dma(s, "bx", 0, bxin[:, 0, :], x1p[128 * ti:128 * (ti + 1), :])
                ldxv = ("bx0", p.v("bx0"))

                nmr, rstd, ve_ln = ln_stats(blkB, bxin[:, 0, :], m, wait_ld=ldxv,
                                        nmr_dst=nmr2[:, ti:ti + 1], rstd_dst=rstd2[:, ti:ti + 1],
                                        wait_se=se_nrm.get(ti - 2, 0))

                @blkB.scalar
                def _(s, ti=ti, m=m, nmr=nmr, rstd=rstd, H=H, ve_ln=ve_ln):
                    p.w(s, "ve", ve_ln)
                    if ti >= 8 * H + 1:
                        p.w(s, "te", te_bln[ti - 1])
                    _a = s.activation(sbf[:, 0, :], bxin[:, 0, :], AF.Identity, bias=nmr, scale=rstd)
                    p.inc(_a, "se")
                se_nrm[ti] = p.v("se")
                issue_w2cast(blkB, "se", se_nrm[ti])

                @blkB.tensor
                def _(t, ti=ti, m=m, pr=pst_rel[m]):
                    p.w(t, "se", se_nrm[ti])
                    if ti == 0:
                        p.w(t, "ve", VE_BC)
                        p.w(t, "ge", GE_PB)
                    p.w(t, "ve", pr)
                    for j in range(8):
                        _mm = t.transpose(pst[:, 1024 * m + 128 * j:1024 * m + 128 * (j + 1)],
                                    sbf[:, 0, 128 * j:128 * (j + 1)], ident[:, :])
                    p.inc(_mm, "te")
                te_s = p.v("te")
                te_bln[ti] = te_s

                @blkB.vector
                def _(v, ti=ti, m=m, te_s=te_s, H=H):
                    p.w(v, "te", te_s)
                    if H == 1 and ti == 8:
                        p.w(v, "te", H0_ATTN_TE)
                    tl = ti - 8 * H
                    _v = v.tensor_copy(src_fm[:, 0:8, 128 * tl:128 * (tl + 1)],
                                  pst[:, 1024 * m:1024 * (m + 1)].rearrange("p (c t) -> p c t", t=128))
                    p.inc(_v, "ve")
                pst_rel[m] = p.v("ve")
            VE_SRC = p.v("ve")

            se_qk = {}
            for idx in range(16):
                isq = idx < 8
                ot = idx % 8
                reg = idx % 2
                slot = idx % 2
                ldw_qk = load_inqk(idx, slot)

                @blkB.tensor
                def _(t, idx=idx, reg=reg, slot=slot, H=H, ldw_qk=ldw_qk):
                    for tup in ldw_qk:
                        p.wt(t, tup)
                    if idx == 0:
                        p.w(t, "ve", VE_SRC)
                        if H == 1:
                            p.w(t, "se", se_exp_b.get(7, 0))
                    if idx >= 2:
                        p.w(t, "se", se_qk[idx - 2])
                    for c in range(NC):
                        for half in range(2):
                            _mm = t.matmul(ps_s[:, 1024 * reg + 512 * half:1024 * reg + 512 * (half + 1)],
                                     inqk_buf[:, slot, c, :],
                                     src_fm[:, c, 512 * half:512 * (half + 1)],
                                     start=(c == 0), stop=(c == NC - 1))
                    p.inc(_mm, "te")
                te_q = p.v("te")
                inqk_rel[slot] = te_q

                @blkB.scalar
                def _(s, isq=isq, ot=ot, reg=reg, te_q=te_q, idx=idx, H=H):
                    p.w(s, "te", te_q)
                    if idx == 0:
                        p.w(s, "ve", VE_QB8)
                        if H == 1:
                            p.w(s, "te", H0_ATTN_TE)
                    dst = (q_fm if isq else k_fm)[:, ot, :]
                    if isq:
                        _a = s.activation(dst, ps_s[:, 1024 * reg:1024 * (reg + 1)], AF.Identity,
                                     bias=qb8[:, ot:ot + 1], scale=ccol(C_INV8))
                    else:
                        _a = s.activation(dst, ps_s[:, 1024 * reg:1024 * (reg + 1)], AF.Identity,
                                     bias=kbc[:, ot:ot + 1], scale=1.0)
                    p.inc(_a, "se")
                se_qk[idx] = p.v("se")
                issue_w2cast(blkB, "se", se_qk[idx])

            ve_v = {}
            for tl in range(8):
                @blkB.tensor
                def _(t, tl=tl):
                    if tl == 0:
                        p.w(t, "ve", VE_SRC)
                        p.wt(t, LDW_INV)
                    if tl >= 1:
                        p.w(t, "ve", ve_v[tl - 1])
                    for c in range(NC):
                        for half in range(2):
                            _mm = t.matmul(ps_a[:, 512 * half:512 * (half + 1)],
                                     src_fm[:, c, 128 * tl:128 * (tl + 1)],
                                     inv_sb[:, c, 512 * half:512 * (half + 1)],
                                     start=(c == 0), stop=(c == NC - 1))
                    p.inc(_mm, "te")
                te_v = p.v("te")

                @blkB.vector
                def _(v, tl=tl, te_v=te_v, H=H):
                    p.w(v, "te", te_v)
                    if H == 1 and tl == 0:
                        p.w(v, "te", H0_ATTN_TE)
                    _v = v.tensor_add(v_tm[:, tl, :], ps_a[:, :], bc_vb[:, :])
                    p.inc(_v, "ve")
                ve_v[tl] = p.v("ve")
            VE_IN_V = p.v("ve")

            def emit_bx(ti, first):
                @blkB.sync
                def _(s, ti=ti, first=first, H=H):
                    if first:
                        p.w(s, "se", se_nrm[8 * H + 7])
                    else:
                        p.w(s, "se", se_srcb.get(ti - 1, 0))
                    p.dma(s, "bx", 0, bxin[:, 0, :], x1p[128 * ti:128 * (ti + 1), :])
                ldx_at[ti] = ("bx0", p.v("bx0"))

            def emit_head(ti):
                tl = ti - 8 * H
                par = ti % 2

                @blkB.tensor
                def _(t, tl=tl, ti=ti):
                    if tl == 0:
                        p.w(t, "se", se_qk[15])
                        p.w(t, "ve", VE_IN_V)
                    if ti >= 1:
                        p.w(t, "se", se_exp_b.get(ti - 1, 0))
                    for h in range(16):
                        po = 64 * (h % 2)
                        cp = h // 2
                        col = 512 * (h // 4) + 128 * (h % 4)
                        _mm = t.matmul(ps_s[:, col:col + 128],
                                 q_fm[po:po + 64, cp, 128 * tl:128 * (tl + 1)],
                                 k_fm[po:po + 64, cp, 128 * tl:128 * (tl + 1)],
                                 start=True, stop=False)
                        _mm = t.matmul(ps_s[:, col:col + 128], mask17[:, 0:128], mask17[:, 128:256],
                                 start=False, stop=True)
                    p.inc(_mm, "te")
                te_sc = p.v("te")

                @blkB.scalar
                def _(s, te_sc=te_sc, ti=ti, par=par):
                    p.w(s, "te", te_sc)
                    if ti >= 2:
                        p.w(s, "te", te_at.get(ti - 2, 0))
                    for g2 in range(4):
                        _a = s.activation(E_sb[:, par, 512 * g2:512 * (g2 + 1)],
                                     ps_s[:, 512 * g2:512 * (g2 + 1)],
                                     AF.Exp, bias=0.0, scale=1.0)
                    p.inc(_a, "se")
                se_exp_b[ti] = p.v("se")

                @blkB.vector
                def _(v, se_e=se_exp_b[ti], ti=ti, par=par):
                    p.w(v, "se", se_e)
                    _v = v.tensor_reduce(seg16[:, :],
                                    E_sb[:, par, :].rearrange("p (h t) -> p h t", t=128),
                                    axis=AX.X, op=ALU.add)
                    v.drain()
                    _v = v.reciprocal(rec16[:, :], seg16[:, :])
                    v.drain()
                    _v = v.tensor_tensor(A_sb[:, par, :].rearrange("p (h t) -> p h t", t=128),
                                    E_sb[:, par, :].rearrange("p (h t) -> p h t", t=128),
                                    rec16[:, :].unsqueeze(-1).broadcast_to([128, 16, 128]),
                                    ALU.mult)
                    p.inc(_v, "ve")
                ve_an[ti] = p.v("ve")

            def emit_tail(ti):
                tl = ti - 8 * H
                m = ti % 2
                par = ti % 2
                for half in range(2):
                    @blkB.tensor
                    def _(t, half=half, ve_a=ve_an[ti], pr=pst_rel[half], par=par):
                        p.w(t, "ve", ve_a)
                        p.w(t, "ve", pr)
                        for kk in range(8):
                            hh = 8 * half + kk
                            _mm = t.transpose(pst[:, 1024 * half + 128 * kk:1024 * half + 128 * (kk + 1)],
                                        A_sb[:, par, 128 * hh:128 * (hh + 1)], ident[:, :])
                        p.inc(_mm, "te")
                    te_at[ti] = p.v("te")

                    @blkB.vector
                    def _(v, half=half, te_a=te_at[ti]):
                        p.w(v, "te", te_a)
                        _v = v.tensor_copy(AT_sb[:, 8 * half:8 * (half + 1), :],
                                      pst[:, 1024 * half:1024 * (half + 1)]
                                      .rearrange("p (c t) -> p c t", t=128))
                        p.inc(_v, "ve")
                    pst_rel[half] = p.v("ve")
                    ve_atv[(ti, half)] = p.v("ve")

                @blkB.tensor
                def _(t, tl=tl, ti=ti, m=m):
                    p.w(t, "ve", ve_atv[(ti, 1)])
                    if ti >= 1:
                        p.w(t, "ve", ve_ctxev.get(ti - 1, 0))
                        p.w(t, "ve", ve_z1.get(ti - 1, 0))
                    for h in range(16):
                        po = 64 * (h % 2)
                        col = 512 * (h // 8) + 128 * ((h // 2) % 4)
                        _mm = t.matmul(ps_a[po:po + 64, col:col + 128],
                                 v_tm[:, tl, 64 * h:64 * (h + 1)],
                                 AT_sb[:, h, :],
                                 start=True, stop=True)
                    p.inc(_mm, "te")
                te_cx = p.v("te")

                @blkB.vector
                def _(v, ti=ti, te_cx=te_cx):
                    p.w(v, "te", te_cx)
                    if ti >= 1:
                        p.w(v, "te", te_op_prev.get(ti - 1, 0))
                    for half in range(2):
                        _v = v.tensor_copy(ctx_t[:, 0, 4 * half:4 * (half + 1), :],
                                      ps_a[:, 512 * half:512 * (half + 1)]
                                      .rearrange("p (c t) -> p c t", t=128))
                    p.inc(_v, "ve")
                ve_ctxev[ti] = p.v("ve")

                @blkB.tensor
                def _(t, m=m, ve_c=ve_ctxev[ti]):
                    p.w(t, "ve", ve_c)
                    for c in range(NC):
                        for half in range(2):
                            _mm = t.matmul(ps_a[:, 512 * half:512 * (half + 1)],
                                     ctx_t[:, 0, c, :],
                                     outwT_sb[:, c, 512 * half:512 * (half + 1)],
                                     start=(c == 0), stop=(c == NC - 1))
                    p.inc(_mm, "te")
                te_op = p.v("te")
                te_op_prev[ti] = te_op

                @blkB.scalar
                def _(s, ti=ti, m=m, ldx_v=ldx_at[ti]):
                    p.wt(s, ldx_v)
                    _a = s.activation(srcb[:, m, :], bxin[:, 0, :], AF.Identity,
                                 bias=nmr2[:, ti:ti + 1], scale=rstd2[:, ti:ti + 1])
                    p.inc(_a, "se")
                se_srcb[ti] = p.v("se")
                issue_w2cast(blkB, "se", se_srcb[ti])
                if ti + 1 <= tiles[-1]:
                    emit_bx(ti + 1, False)

                @blkB.vector
                def _(v, m=m, te_op=te_op, se_sr=se_srcb[ti]):
                    p.w(v, "te", te_op)
                    p.w(v, "se", se_sr)
                    _v = v.tensor_add(srcb[:, m, :], srcb[:, m, :], ps_a[:, 0:1024])
                    v.drain()
                    _v = v.tensor_add(srcb[:, m, :], srcb[:, m, :], bc_outb[:, :])
                    p.inc(_v, "ve")
                ve_z1[ti] = p.v("ve")
                ve_outev[ti] = ve_z1[ti]

                nmr, rstd, ve_ln = ln_stats(blkB, srcb[:, m, :], m, wait_ve=ve_z1[ti],
                                        wait_se=se_n1_prev.get(ti - 2, 0))

                @blkB.scalar
                def _(s, m=m, nmr=nmr, rstd=rstd, ti=ti, ve_ln=ve_ln):
                    p.w(s, "ve", ve_ln)
                    if ti >= 2:
                        p.wt(s, ldx_spill[ti - 2])
                    _a = s.activation(tokf[:, m, :], srcb[:, m, :], AF.Identity, bias=nmr, scale=rstd)
                    p.inc(_a, "se")
                se_n1 = p.v("se")
                se_n1_prev[ti] = se_n1

                @blkB.vector
                def _(v, m=m, se_n1=se_n1, ti=ti):
                    p.w(v, "se", se_n1)
                    _v = v.tensor_mul(tokf[:, m, :], tokf[:, m, :], bc_g1[:, :])
                    v.drain()
                    _v = v.tensor_add(tokf[:, m, :], tokf[:, m, :], bc_b1[:, :])
                    p.inc(_v, "ve")
                ve_tok_ready[ti] = p.v("ve")

                @blkB.scalar
                def _(s, m=m, ti=ti, ve_t=ve_tok_ready[ti]):
                    p.w(s, "ve", ve_t)
                    if ti >= 2:
                        p.w(s, "te", te_toktr[ti - 2])
                    _a = s.copy(tokb[:, m, :], tokf[:, m, :])
                    p.inc(_a, "se")
                se_tokb[ti] = p.v("se")

                @blkB.sync
                def _(s, ti=ti, m=m, ve_t=ve_tok_ready[ti]):
                    p.w(s, "ve", ve_t)
                    p.dma(s, "sp", m, tokspill[128 * ti:128 * (ti + 1), :], tokf[:, m, :])
                ldx_spill[ti] = ("sp%d" % m, p.v("sp%d" % m))

                toktr_queue.append(ti)

            emit_bx(tiles[0], True)
            for ti in tiles:
                if toktr_queue:
                    emit_tok_transpose(toktr_queue.pop(0))
                emit_head(ti)
                if ti > tiles[0]:
                    emit_tail(ti - 1)
            emit_tail(tiles[-1])

            if H == 0:
                H0_ATTN_TE = p.v("te")

        while toktr_queue:
            emit_tok_transpose(toktr_queue.pop(0))
        VE_TOKFM = p.v("ve")

        barrier(blkB, "bar2")
        blkB.__exit__(None, None, None)
        pb1.close()

        pbG = ExitStack()
        lfm = sb(pbG, "lfm", [8, 2048], F32)
        ltm = sb(pbG, "ltm", [128, NT, 8], F32)
        gsc = sb(pbG, "gsc", [128, 66], F32)
        blkG = nc.Block()
        blkG.__enter__()

        se_lfm = [0] * 4
        for nb in range(4):
            @blkG.tensor
            def _(t, nb=nb):
                if nb == 0:
                    p.w(t, "bar2", p.v("bar2"))
                    p.w(t, "gwb0", p.v("gwb0"))
                if nb >= 2:
                    p.w(t, "se", se_lfm[nb - 2])
                for c in range(NC):
                    _mm = t.matmul(ps_a[0:8, 512 * (nb % 2):512 * (nb % 2) + 512],
                             gw_sb[:, c, :], tok_fm[:, c, 512 * nb:512 * (nb + 1)],
                             start=(c == 0), stop=(c == NC - 1))
                p.inc(_mm, "te")
            te_g = p.v("te")

            @blkG.scalar
            def _(s, nb=nb, te_g=te_g):
                p.w(s, "te", te_g)
                if nb == 0:
                    p.wt(s, LDX_PB)
                _a = s.activation(lfm[0:8, 512 * nb:512 * (nb + 1)],
                             ps_a[0:8, 512 * (nb % 2):512 * (nb % 2) + 512],
                             AF.Identity, bias=gb_sb[:, 0:1], scale=1.0)
                p.inc(_a, "se")
            se_lfm[nb] = p.v("se")

        @blkG.tensor
        def _(t):
            p.w(t, "se", se_lfm[3])
            for i in range(NT):
                _mm = t.transpose(ps_s[:, 8 * i:8 * (i + 1)], lfm[0:8, 128 * i:128 * (i + 1)],
                            ident32[0:8, 0:8])
            p.inc(_mm, "te")
        TE_LT = p.v("te")

        @blkG.vector
        def _(v):
            p.w(v, "te", TE_LT)
            _v = v.tensor_copy(ltm[:, :, :], ps_s[:, 0:128].rearrange("p (i e) -> p i e", e=8))
            p.inc(_v, "ve")
        VE_LTM = p.v("ve")

        for i in range(NT):
            @blkG.vector
            def _(v, i=i):
                if i == 0:
                    p.w(v, "ve", VE_LTM)
                l = ltm[:, i, :]
                _v = v.tensor_max(gsc[:, 0:4], l[:, 0:4], l[:, 4:8])
                v.drain()
                _v = v.tensor_max(gsc[:, 4:6], gsc[:, 0:2], gsc[:, 2:4])
                v.drain()
                _v = v.tensor_max(gsc[:, 6:7], gsc[:, 4:5], gsc[:, 5:6])
                v.drain()
                _v = v.tensor_scalar_mul(gsc[:, 7:8], gsc[:, 6:7], ccol(C_NEG1))
                _v = v.tensor_single_scalar(gsc[:, 8:16], l, gsc[:, 6:7], ALU.is_equal)
                v.drain()
                _v = v.scalar_tensor_tensor(gsc[:, 16:24], gsc[:, 8:16], ccol(C_NEGBIG), l,
                                       ALU.mult, ALU.add)
                v.drain()
                _v = v.tensor_max(gsc[:, 24:28], gsc[:, 16:20], gsc[:, 20:24])
                v.drain()
                _v = v.tensor_max(gsc[:, 28:30], gsc[:, 24:26], gsc[:, 26:28])
                v.drain()
                _v = v.tensor_max(gsc[:, 30:31], gsc[:, 28:29], gsc[:, 29:30])
                v.drain()
                _v = v.tensor_single_scalar(gsc[:, 32:40], l, gsc[:, 30:31], ALU.is_ge)
                p.inc(_v, "ve")
            ve_rt = p.v("ve")

            @blkG.scalar
            def _(s, i=i, ve_rt=ve_rt):
                p.w(s, "ve", ve_rt)
                _a = s.activation(gsc[:, 40:48], ltm[:, i, :], AF.Exp, bias=gsc[:, 7:8], scale=1.0)
                p.inc(_a, "se")
            se_e8 = p.v("se")

            @blkG.vector
            def _(v, se_e8=se_e8):
                p.w(v, "se", se_e8)
                _v = v.tensor_mul(gsc[:, 48:56], gsc[:, 40:48], gsc[:, 32:40])
                v.drain()
                _v = v.tensor_add(gsc[:, 56:60], gsc[:, 48:52], gsc[:, 52:56])
                v.drain()
                _v = v.tensor_add(gsc[:, 60:62], gsc[:, 56:58], gsc[:, 58:60])
                v.drain()
                _v = v.tensor_add(gsc[:, 62:63], gsc[:, 60:61], gsc[:, 61:62])
                v.drain()
                _v = v.reciprocal(gsc[:, 64:65], gsc[:, 62:63])
                v.drain()
                _v = v.tensor_scalar_mul(combine[:, i, :], gsc[:, 48:56], gsc[:, 64:65])
                v.drain()
                _v = v.tensor_copy(combine_bf[:, i, :], combine[:, i, :])
                p.inc(_v, "ve")
        VE_COMB = p.v("ve")

        @blkG.tensor
        def _(t, pr0=pst_rel[0], pr1=pst_rel[1]):
            p.w(t, "ve", VE_COMB)
            p.w(t, "ve", pr0)
            p.w(t, "ve", pr1)
            for i in range(NT):
                _mm = t.transpose(pst[0:8, 128 * i:128 * (i + 1)], combine_bf[:, i, :], ident[:, :])
            p.inc(_mm, "te")
        TE_CT = p.v("te")

        @blkG.vector
        def _(v):
            p.w(v, "te", TE_CT)
            _v = v.tensor_copy(combT_sb[:, :, :], pst[0:8, 0:2048].rearrange("p (i t) -> p i t", t=128))
            p.inc(_v, "ve")
        VE_CT = p.v("ve")
        pst_rel = [VE_CT, VE_CT]

        barrier(blkG, "bar2")
        blkG.__exit__(None, None, None)
        pbG.close()

        pb2 = ExitStack()
        h_sb = sb(pb2, "h_sb", [128, 16, 1024], BF16)
        w2sb = sb(pb2, "w2sb", [128, 16, 1024], BF16)
        moe_acc = sb(pb2, "moe_acc", [128, 2, 8, 1024], F32)
        w1buf = sb(pb2, "w1buf", [128, 4, NC, 128], BF16)
        b1c = sb(pb2, "b1c", [128, 32], F32)
        fxb = sb(pb2, "fxb", [128, 1, 1024], F32)
        fnb = sb(pb2, "fnb", [128, 1, 1024], F32)
        ftk = sb(pb2, "ftk", [128, 1, 1024], F32)
        bc_g2 = sb(pb2, "bc_g2", [128, 1024], F32)
        bc_b2 = sb(pb2, "bc_b2", [128, 1024], F32)
        vstage2 = sb(pb2, "vstage2", [1, 1024], F32)

        blkM = nc.Block()
        blkM.__enter__()

        VE_BC2 = build_bcasts(blkM, [
            (_r1(lng_d[2]), bc_g2),
            (_r1(lng_d[3]), bc_b2),
        ], vstage2, wait0_ve=VE_CT)

        prev_y_te = 0
        prev_y_ve = 0
        prev_hev14 = None
        fin_se_prev = [0]

        W1SEQ = [(e_, 16 * fh_ + ft_) for _bk in range(2) for e_ in range(NE)
                 for fh_ in range(2) for ft_ in range(16)]
        w1_gi = [0]
        w1_ld = {}

        def issue_w1(gate_te, n=1):
            gis = list(range(w1_gi[0], min(w1_gi[0] + n, len(W1SEQ))))
            if not gis:
                return
            w1_gi[0] = gis[-1] + 1

            @blkM.gpsimd
            def _(g, gate_te=gate_te, gis=gis):
                p.w(g, "te", gate_te)
                if gis[0] == 0:
                    p.w(g, "bar2", p.v("bar2"))
                for gi in gis:
                    e_, t_ = W1SEQ[gi]
                    src = w1T_d[e_, t_].rearrange("p (c o) -> p c o", o=128)
                    s_ = gi % 4
                    w1_ld[gi] = [
                        p.dma(g, "w1a", s_, w1buf[:, s_, 0:4, :], src[:, 0:4, :]),
                        p.dma(g, "w1b", s_, w1buf[:, s_, 4:8, :], src[:, 4:8, :]),
                    ]
        sto_rel = [None, None]
        last_spill = [ldx_spill[14], ldx_spill[15]]
        if last_spill[0][0] != "sp0":
            last_spill = [ldx_spill[15], ldx_spill[14]]

        def emit_final(blki, tt8):
            ti = 8 * blki + tt8
            m = tt8 % 2
            vm = ve_moe[tt8]

            @blkM.sync
            def _(s, ti=ti, m=m, sr=sto_rel[0], spv=last_spill[m]):
                p.wt(s, sr)
                p.wt(s, spv)
                p.dma(s, "fl", 0, ftk[:, 0, :], tokspill[128 * ti:128 * (ti + 1), :])
                p.dma(s, "fl", 0, fxb[:, 0, :], x1p[128 * ti:128 * (ti + 1), :])
            ldx_f = ("fl0", p.v("fl0"))

            @blkM.vector
            def _(v, blki=blki, tt8=tt8, ldx_f=ldx_f, vm=vm):
                p.wt(v, ldx_f)
                p.w(v, "ve", vm)
                _v = v.tensor_add(ftk[:, 0, :], ftk[:, 0, :], moe_acc[:, blki, tt8, :])
                _v = v.tensor_add(fxb[:, 0, :], fxb[:, 0, :], bc_b2[:, :])
                p.inc(_v, "ve")
            ve_fz = p.v("ve")

            nmr, rstd, ve_ln = ln_stats(blkM, ftk[:, 0, :], m, wait_ve=ve_fz,
                                        wait_se=fin_se_prev[0])

            @blkM.scalar
            def _(s, nmr=nmr, rstd=rstd, ve_ln=ve_ln):
                p.w(s, "ve", ve_ln)
                _a = s.activation(fnb[:, 0, :], ftk[:, 0, :], AF.Identity, bias=nmr, scale=rstd)
                p.inc(_a, "se")
            se_fn = p.v("se")
            fin_se_prev[0] = se_fn

            @blkM.vector
            def _(v, se_fn=se_fn):
                p.w(v, "se", se_fn)
                _v = v.tensor_mul(fnb[:, 0, :], fnb[:, 0, :], bc_g2[:, :])
                v.drain()
                _v = v.tensor_add(fnb[:, 0, :], fnb[:, 0, :], fxb[:, 0, :])
                p.inc(_v, "ve")
            ve_fo = p.v("ve")

            @blkM.sync
            def _(s, ti=ti, ve_fo=ve_fo):
                p.w(s, "ve", ve_fo)
                p.dma(s, "st", 0, y_out[128 * ti:128 * (ti + 1), :], fnb[:, 0, :])
            sto_rel[0] = ("st0", p.v("st0"))

        for blki in range(2):
            ve_ini = {}
            for tt8 in range(8):
                @blkM.tensor
                def _(t, tt8=tt8, blki=blki):
                    if tt8 == 0:
                        p.w(t, "bar2", p.v("bar2"))
                        p.w(t, "ve", VE_BC2)
                        p.w(t, "gwb0", p.v("gwb0"))
                        if blki == 1:
                            p.w(t, "ve", prev_y_ve)
                    if tt8 >= 1:
                        p.w(t, "ve", ve_ini[tt8 - 1])
                    for half in range(2):
                        _mm = t.matmul(ps_a[:, 512 * half:512 * (half + 1)],
                                 combT_sb[:, 8 * blki + tt8, :],
                                 b2_sb[:, 512 * half:512 * (half + 1)],
                                 start=True, stop=True)
                    p.inc(_mm, "te")
                te_i = p.v("te")

                @blkM.vector
                def _(v, tt8=tt8, te_i=te_i, blki=blki):
                    p.w(v, "te", te_i)
                    _v = v.tensor_copy(moe_acc[:, blki, tt8, :], ps_a[:, :])
                    p.inc(_v, "ve")
                ve_ini[tt8] = p.v("ve")
            VE_INI = p.v("ve")

            ve_moe = {t_: VE_INI for t_ in range(8)}

            if blki == 0:
                issue_w1(0, 4)

            for e in range(NE):
                for fh in range(2):
                    gi0 = ((blki * NE + e) * 2 + fh) * 16
                    te_h = {}
                    hev_rel = {}

                    if fh == 0:
                        @blkM.gpsimd
                        def _(g, e=e, blki=blki, pyt=prev_y_te):
                            if e == 0 and blki == 0:
                                p.w(g, "bar2", p.v("bar2"))
                            p.w(g, "te", pyt)
                            p.dma(g, "b1", 0, b1c[:, :], b1_d[e])

                    @blkM.sync
                    def _(s, e=e, fh=fh, blki=blki, pyt=prev_y_te):
                        if e == 0 and fh == 0 and blki == 0:
                            p.w(s, "bar2", p.v("bar2"))
                        p.w(s, "te", pyt)
                        p.w(s, "wc%d" % e, 128)
                        for ftt in range(16):
                            p.dma(s, "w2", 0, w2sb[:, ftt, :],
                                  w2bf_d[e, 2048 * fh + 128 * ftt:
                                         2048 * fh + 128 * (ftt + 1), :])
                    ldw2seg = ("w20", p.v("w20"))

                    for ft in range(16):
                        gi = gi0 + ft

                        @blkM.tensor
                        def _(t, ft=ft, gi=gi, blki=blki, ldwv=w1_ld[gi], pyv=prev_y_ve,
                              b1v=("b10", p.v("b10")), ph=prev_hev14):
                            for tup in ldwv:
                                p.wt(t, tup)
                            if ft == 0:
                                p.wt(t, b1v)
                                if ph is not None:
                                    p.w(t, ph[0], ph[1])
                            if ft == 1:
                                p.w(t, "ve", pyv)
                            if ft >= 2:
                                rel = hev_rel[ft - 2]
                                p.w(t, rel[0], rel[1])
                            r_ = (ft + 1) % 2
                            for c in range(NC):
                                for sub in range(2):
                                    _mm = t.matmul(ps_s[:, 1024 * r_ + 512 * sub:
                                                  1024 * r_ + 512 * (sub + 1)],
                                             w1buf[:, gi % 4, c, :],
                                             tok_fm[:, c, 1024 * blki + 512 * sub:
                                                    1024 * blki + 512 * (sub + 1)],
                                             start=(c == 0), stop=(c == NC - 1))
                            p.inc(_mm, "te")
                        te_h[ft] = p.v("te")
                        issue_w1(te_h[ft])

                        r_ = (ft + 1) % 2

                        @blkM.vector
                        def _(v, fh=fh, ft=ft, r_=r_, te_v=te_h[ft], pyt=prev_y_te):
                            p.w(v, "te", te_v)
                            p.w(v, "te", pyt)
                            _v = v.scalar_tensor_tensor(h_sb[:, ft, :],
                                                   ps_s[:, 1024 * r_:1024 * (r_ + 1)],
                                                   b1c[:, 16 * fh + ft:16 * fh + ft + 1],
                                                   zeros_bf[:, :], ALU.add, ALU.max)
                            p.inc(_v, "ve")
                        hev_rel[ft] = ("ve", p.v("ve"))

                    ve_yev = {}
                    for tt8 in range(8):
                        @blkM.tensor
                        def _(t, tt8=tt8, h14=hev_rel[14], h15=hev_rel[15], ldwv=ldw2seg):
                            if tt8 == 0:
                                p.w(t, h14[0], h14[1])
                                p.w(t, h15[0], h15[1])
                                p.wt(t, ldwv)
                            if tt8 >= 2:
                                p.w(t, "ve", ve_yev[tt8 - 2])
                            reg = ps_a if tt8 % 2 == 0 else ps_s
                            for fs in range(16):
                                for half in range(2):
                                    _mm = t.matmul(reg[:, 512 * half:512 * (half + 1)],
                                             h_sb[:, fs, 128 * tt8:128 * (tt8 + 1)],
                                             w2sb[:, fs, 512 * half:512 * (half + 1)],
                                             start=(fs == 0), stop=(fs == 15))
                            p.inc(_mm, "te")
                        te_y = p.v("te")

                        @blkM.vector
                        def _(v, e=e, tt8=tt8, blki=blki, te_y=te_y, vm=ve_moe[tt8]):
                            p.w(v, "te", te_y)
                            p.w(v, "ve", vm)
                            reg = ps_a if tt8 % 2 == 0 else ps_s
                            _v = v.scalar_tensor_tensor(moe_acc[:, blki, tt8, :], reg[:, 0:1024],
                                                   combine[:, 8 * blki + tt8, e:e + 1],
                                                   moe_acc[:, blki, tt8, :], ALU.mult, ALU.add)
                            p.inc(_v, "ve")
                        ve_yev[tt8] = p.v("ve")
                        ve_moe[tt8] = p.v("ve")
                        if e == NE - 1 and fh == 1:
                            emit_final(blki, tt8)

                    prev_y_te = p.v("te")
                    prev_y_ve = ve_yev[7]
                    prev_hev14 = hev_rel[14]

        @blkM.sync
        def _(s):
            p.wt(s, sto_rel[0])
        blkM.__exit__(None, None, None)
        pb2.close()
        pbP.close()

    return nc


def _prep_inputs(inputs):
    W_qkv = np.asarray(inputs["L0_attn_W_qkv"], np.float32)
    share = {
        "wvT": np.ascontiguousarray(W_qkv[2::3].T.reshape(8, 128, D).transpose(1, 0, 2)),
        "wqkT": np.ascontiguousarray(np.stack([W_qkv[i::3].T for i in range(2)])
                                     .reshape(2, D, 8, 128).transpose(0, 2, 1, 3)
                                     .reshape(2, 8, 8, 128, 128).transpose(0, 1, 3, 2, 4)
                                     .reshape(2, 8, 128, D)),
        "woT": np.ascontiguousarray(np.asarray(inputs["L0_attn_W_out"], np.float32).T
                                    .reshape(8, 128, D).transpose(1, 0, 2)),
        "inqkT": np.ascontiguousarray(np.asarray(inputs["L0_ffn_in_proj_w"], np.float32).T[:, 0:2048]
                                      .reshape(D, 16, 128).transpose(1, 0, 2)
                                      .reshape(16, 8, 128, 128).transpose(0, 2, 1, 3)
                                      .reshape(16, 128, D)),
        "invT": np.ascontiguousarray(np.asarray(inputs["L0_ffn_in_proj_w"], np.float32).T[:, 2048:3072]
                                     .reshape(8, 128, D).transpose(1, 0, 2)),
        "inb": np.ascontiguousarray(np.asarray(inputs["L0_ffn_in_proj_b"], np.float32)),
        "inqkb": np.ascontiguousarray(
            np.asarray(inputs["L0_ffn_in_proj_b"], np.float32)[0:2048]
            .reshape(2, 8, 128).transpose(0, 2, 1)),
        "outwT": np.ascontiguousarray(np.asarray(inputs["L0_ffn_out_w"], np.float32).T
                                      .reshape(8, 128, D).transpose(1, 0, 2)),
        "outb": np.ascontiguousarray(np.asarray(inputs["L0_ffn_out_b"], np.float32)),
        "gwT": np.ascontiguousarray(np.asarray(inputs["L0_ffn_gate_w"], np.float32).T
                                    .reshape(8, 128, NE).transpose(1, 0, 2)),
        "gb": np.ascontiguousarray(np.asarray(inputs["L0_ffn_gate_b"], np.float32).reshape(NE, 1)),
        "w1T": np.ascontiguousarray(
            np.asarray(inputs["L0_ffn_w1"], np.float32).transpose(0, 2, 1)
            .reshape(NE, D, 32, 128).transpose(0, 2, 1, 3)
            .reshape(NE, 32, 8, 128, 128).transpose(0, 1, 3, 2, 4)
            .reshape(NE, 32, 128, D)),
        "b1": np.ascontiguousarray(np.asarray(inputs["L0_ffn_b1"], np.float32)
                                   .reshape(NE, 32, 128).transpose(0, 2, 1)),
        "w2T": np.ascontiguousarray(np.asarray(inputs["L0_ffn_w2"], np.float32).transpose(0, 2, 1)),
        "b2": np.ascontiguousarray(np.asarray(inputs["L0_ffn_b2"], np.float32)),
        "lng": np.ascontiguousarray(np.stack([
            np.asarray(inputs["L0_ffn_ln1_g"], np.float32),
            np.asarray(inputs["L0_ffn_ln1_b"], np.float32),
            np.asarray(inputs["L0_ffn_ln2_g"], np.float32),
            np.asarray(inputs["L0_ffn_ln2_b"], np.float32)])),
    }
    x = np.asarray(inputs["L0_attn_x"], np.float32)
    in_maps = []
    for c in range(8):
        m = dict(share)
        m["x"] = np.ascontiguousarray(x[c])
        in_maps.append(m)
    return in_maps


def kernel(**inputs):
    in_maps = _prep_inputs(inputs)
    nc = build_program("AB")
    res = run_bass_kernel_spmd(nc, in_maps, core_ids=list(range(8)))
    out = np.empty((B, S, D), np.float32)
    for c in range(8):
        yc = np.asarray(res.results[c]["y"], np.float32)
        out[:, 256 * c:256 * (c + 1), :] = yc.reshape(256, 8, D).transpose(1, 0, 2)
    return out


# version 12
# speedup vs baseline: 1.0421x; 1.0421x over previous
import numpy as np
from contextlib import ExitStack

import concourse.bass as bass
import concourse.mybir as mybir
from concourse.bass_utils import run_bass_kernel_spmd
from concourse.aot_env import allow_ici_all_to_all

F32 = mybir.dt.float32
BF16 = mybir.dt.bfloat16
AF = mybir.ActivationFunctionType
ALU = mybir.AluOpType
AX = mybir.AxisListType

B, S, D = 8, 2048, 1024
NT = 16
NC = 8
FF = 4096
NE = 8
MASK_L = 30720.0

C_EPS, C_INV1024, C_NEGINV1024, C_INV32, C_INV8, C_NEG1, C_NEGBIG, C_NEGHALF, C_ZERO, C_ONE = range(10)


class P:
    def __init__(self, nc):
        self.nc = nc
        self.cnt = {}
        self.sems = {}

    def sem(self, stack, name):
        self.sems[name] = stack.enter_context(self.nc.semaphore(name))
        self.cnt[name] = 0

    def inc(self, instr, name, n=1):
        instr.then_inc(self.sems[name], n)
        self.cnt[name] += n
        return self.cnt[name]

    def v(self, name):
        return self.cnt[name]

    def w(self, eng, name, val):
        if val > 0:
            eng.wait_ge(self.sems[name], val)

    def gsync(self, g):
        self.inc(g.engine_nop(), "ge")
        self.w(g, "ge", self.v("ge"))

    def wt(self, eng, tup):
        if tup and tup[1] > 0:
            eng.wait_ge(self.sems[tup[0]], tup[1])

    def dma(self, q, stream, slot, out, in_, slow=False):
        sem = f"{stream}{slot}"
        v = self.inc(q.dma_start(out=out, in_=in_, allow_slow_non_contiguous=slow), sem, 16)
        return (sem, v)


def build_program(parts="AB"):
    nc = bass.Bass("TRN2", target_bir_lowering=False)

    x_in = nc.dram_tensor("x", [S, D], F32, kind="ExternalInput")
    wvT_d = nc.dram_tensor("wvT", [128, NC, D], F32, kind="ExternalInput")
    wqkT_d = nc.dram_tensor("wqkT", [2, NC, 128, D], F32, kind="ExternalInput")
    woT_d = nc.dram_tensor("woT", [128, NC, D], F32, kind="ExternalInput")
    inqkT_d = nc.dram_tensor("inqkT", [16, 128, D], F32, kind="ExternalInput")
    invT_d = nc.dram_tensor("invT", [128, NC, D], F32, kind="ExternalInput")
    inb_d = nc.dram_tensor("inb", [3 * D], F32, kind="ExternalInput")
    inqkb_d = nc.dram_tensor("inqkb", [2, 128, NC], F32, kind="ExternalInput")
    outwT_d = nc.dram_tensor("outwT", [128, NC, D], F32, kind="ExternalInput")
    outb_d = nc.dram_tensor("outb", [D], F32, kind="ExternalInput")
    gwT_d = nc.dram_tensor("gwT", [128, NC, NE], F32, kind="ExternalInput")
    gb_d = nc.dram_tensor("gb", [NE, 1], F32, kind="ExternalInput")
    w1T_d = nc.dram_tensor("w1T", [NE, 32, 128, D], F32, kind="ExternalInput")
    b1_d = nc.dram_tensor("b1", [NE, 128, 32], F32, kind="ExternalInput")
    w2T_d = nc.dram_tensor("w2T", [NE, FF, D], F32, kind="ExternalInput")
    b2_d = nc.dram_tensor("b2", [NE, D], F32, kind="ExternalInput")
    lng_d = nc.dram_tensor("lng", [4, D], F32, kind="ExternalInput")

    w2bf_d = nc.dram_tensor("w2bf", [NE, FF, D], BF16)
    wqkbf_d = nc.dram_tensor("wqkbf", [2, NC, 128, D], BF16)
    inqkbf_d = nc.dram_tensor("inqkbf", [16, 128, D], BF16)
    invbf_d = nc.dram_tensor("invbf", [128, NC, D], BF16)
    outwbf_d = nc.dram_tensor("outwbf", [128, NC, D], BF16)
    a2a_in = nc.dram_tensor("a2a_in", [S, D], F32)
    a2a_out = nc.dram_tensor("a2a_out", [S, D], F32)
    x1p = nc.dram_tensor("x1p", [S, D], F32)
    tokspill = nc.dram_tensor("tokspill", [S, D], F32)
    y_out = nc.dram_tensor("y", [S, D], F32, kind="ExternalOutput")

    p = P(nc)

    with ExitStack() as outer:
        def sb(st, name, shape, dt=F32):
            return st.enter_context(nc.sbuf_tensor(name, shape, dt))

        ident = sb(outer, "ident", [128, 128], BF16)
        consts = sb(outer, "consts", [128, 10], F32)
        stat = sb(outer, "stat", [128, 12], F32)
        negmax = sb(outer, "negmax", [128, 2], F32)
        rowsum = sb(outer, "rowsum", [128, 2], F32)
        recipA = sb(outer, "recipA", [128, NT], F32)
        nmr2 = sb(outer, "nmr2", [128, NT], F32)
        rstd2 = sb(outer, "rstd2", [128, NT], F32)
        sqscr = sb(outer, "sqscr", [128, 1024], F32)
        magic = sb(outer, "magic", [128, 1], mybir.dt.int32)

        ps_s = outer.enter_context(nc.psum_tensor("ps_s", [128, 2048], F32))
        ps_a = outer.enter_context(nc.psum_tensor("ps_a", [128, 1024], F32))
        pst = outer.enter_context(nc.psum_tensor("pst", [128, 2048], BF16))

        for s_ in ("te", "ve", "se", "ge", "cc", "bar1", "bar2",
                   "xld0", "xld1", "xs0", "xs1", "x1p0", "wA0",
                   "wqk0", "wqk1", "wqk2", "wqk3",
                   "lds0", "vst0", "bx0", "bx1", "sp0", "sp1", "wV0",
                   "iqa0", "iqa1", "iqa2", "iqa3", "iqb0", "iqb1", "iqb2", "iqb3",
                   "gwb0", "b10",
                   "w1a0", "w1a1", "w1a2", "w1a3", "w1b0", "w1b1", "w1b2", "w1b3",
                   "w20", "fl0", "fl1", "st0", "st1",
                   "wc0", "wc1", "wc2", "wc3", "wc4", "wc5", "wc6", "wc7",
                   "wqc0", "wqc1", "iqc0", "ivc0"):
            p.sem(outer, s_)

        def ccol(i):
            return consts[:, i:i + 1]

        blk = nc.Block()
        blk.__enter__()

        @blk.gpsimd
        def _(g):
            g.memset(ident[:, :], 0.0)
            p.gsync(g)
            g.affine_select(out=ident[:, :], in_=ident[:, :], compare_op=ALU.not_equal,
                            fill=1.0, base=0, pattern=[[-1, 128]], channel_multiplier=1)
            for idx, v_ in [(C_EPS, 1e-5), (C_INV1024, 1.0 / 1024.0),
                            (C_NEGINV1024, -1.0 / 1024.0), (C_INV32, 1.0 / 32.0),
                            (C_INV8, 1.0 / 8.0), (C_NEG1, -1.0), (C_NEGBIG, -1e30),
                            (C_NEGHALF, -0.5), (C_ZERO, 0.0), (C_ONE, 1.0)]:
                g.memset(consts[:, idx:idx + 1], v_)
            g.memset(magic[:, :], 0x5f3759df)
            g.memset(stat[:, :], 0.0)
            p.inc(g.engine_nop(), "ge")
        GE_SETUP = p.v("ge")

        blk.__exit__(None, None, None)

        def ln_stats(blk_, src_ap, m, wait_ld=None, nmr_dst=None, rstd_dst=None, wait_ve=0, wait_se=0):
            musum = stat[:, m:m + 1]
            negmu = stat[:, 2 + m:3 + m]
            sd = stat[:, 4 + m:5 + m]
            rstd = rstd_dst if rstd_dst is not None else stat[:, 6 + m:7 + m]
            nmr = nmr_dst if nmr_dst is not None else stat[:, 8 + m:9 + m]

            @blk_.vector
            def _(v):
                p.wt(v, wait_ld)
                p.w(v, "ve", wait_ve)
                p.w(v, "se", wait_se)
                _v = v.reduce_sum(musum, src_ap, axis=AX.X)
                v.drain()
                _v = v.tensor_scalar_mul(negmu, musum, ccol(C_NEGINV1024))
                p.inc(_v, "ve")
            ve_req = p.v("ve")

            @blk_.scalar
            def _(s):
                p.w(s, "ve", ve_req)
                _a = s.activation(sqscr[:, :], src_ap, AF.Square, bias=negmu,
                             accum_out=musum)
                s.drain()
                _a = s.activation(sd, musum, AF.Ln, bias=ccol(C_EPS), scale=ccol(C_INV1024))
                s.drain()
                _a = s.activation(rstd, sd, AF.Exp, bias=ccol(C_ZERO), scale=ccol(C_NEGHALF))
                s.drain()
                _a = s.activation(nmr, negmu, AF.Identity, bias=ccol(C_ZERO), scale=rstd)
                p.inc(_a, "se")
                s.drain()
            return nmr, rstd, ve_req

        def barrier(blk_, name):
            for dec, sem_ in ((blk_.vector, "ve"), (blk_.scalar, "se"), (blk_.gpsimd, "ge")):
                @dec
                def _(e, sem_=sem_):
                    p.inc(e.engine_nop() if sem_ != "se" else e.drain(), sem_)
            snap = {k: v for k, v in p.cnt.items() if v > 0 and k not in ("bar1", "bar2")}
            for dec in (blk_.tensor, blk_.vector, blk_.scalar, blk_.gpsimd, blk_.sync):
                @dec
                def _(e):
                    for k, v in snap.items():
                        p.w(e, k, v)
                    p.inc(e.nop(), name)
            tgt = p.v(name)
            for dec in (blk_.tensor, blk_.vector, blk_.scalar, blk_.gpsimd, blk_.sync):
                @dec
                def _(e):
                    p.w(e, name, tgt)

        pa = ExitStack()
        q8_buf = sb(pa, "q8_buf", [128, NC, NC, 128], BF16)
        wv_sb = sb(pa, "wv_sb", [128, NC, 1024], BF16)
        wo_sb = sb(pa, "wo_sb", [128, NC, 1024], BF16)
        x_fm = sb(pa, "x_fm", [128, NC, 2048], BF16)
        K_sb = sb(pa, "K_sb", [128, NC, 2048], BF16)
        V_sb = sb(pa, "V_sb", [128, NT, 1024], BF16)
        Q_sb = sb(pa, "Q_sb", [128, NC, 512], BF16)
        P_sb = sb(pa, "P_sb", [128, 1, 2048], BF16)
        PT_sb = sb(pa, "PT_sb", [128, 16, 512], BF16)
        CT_sb = sb(pa, "CT_sb", [128, NC, 512], BF16)
        xin = sb(pa, "xin", [128, 2, 1024], F32)
        xbf = sb(pa, "xbf", [128, 2, 1024], BF16)
        xin2 = xin
        x1buf = sb(pa, "x1buf", [128, 2, 1024], F32)

        blkA = nc.Block()
        blkA.__enter__()

        @blkA.gpsimd
        def _(g):
            for wsb, src in ((wv_sb, wvT_d[:, :, :]), (wo_sb, woT_d[:, :, :])):
                p.dma(g, "wA", 0, wsb[:, :, :], src)
        LDW_A = ("wA0", p.v("wA0"))
        def load_wqk8(wsel, sem_slot, wait_tup):
            @blkA.sync
            def _(s, wsel=wsel, sem_slot=sem_slot, wait_tup=wait_tup):
                p.wt(s, wait_tup)
                p.w(s, "wqc%d" % wsel, 128)
                for ot in range(NC):
                    p.dma(s, "wqk", sem_slot, q8_buf[:, ot, :, :],
                          wqkbf_d[wsel, ot].rearrange("p (c o) -> p c o", o=128))
            return ("wqk%d" % sem_slot, p.v("wqk%d" % sem_slot))

        pst_rel = [0, 0]

        @blkA.vector
        def _(v):
            p.w(v, "ge", GE_SETUP)

        @blkA.scalar
        def _(s):
            p.w(s, "ge", GE_SETUP)

        se_norm = [0] * NT
        te_a1 = [0] * NT
        for i in range(NT):
            m = i % 2

            @blkA.sync
            def _(s, i=i, m=m):
                if i >= 2:
                    p.w(s, "se", se_norm[i - 2])
                p.dma(s, "xld", m, xin[:, m, :], x_in[128 * i:128 * (i + 1), :])
            ldxv = ("xld%d" % m, p.v("xld%d" % m))

            nmr, rstd, ve_ln = ln_stats(blkA, xin[:, m, :], m, wait_ld=ldxv,
                                        wait_se=(se_norm[i - 2] if i >= 2 else 0))

            @blkA.scalar
            def _(s, i=i, m=m, nmr=nmr, rstd=rstd, ve_ln=ve_ln):
                p.w(s, "ve", ve_ln)
                if i >= 2:
                    p.w(s, "te", te_a1[i - 2])
                _a = s.activation(xbf[:, m, :], xin[:, m, :], AF.Identity, bias=nmr, scale=rstd)
                p.inc(_a, "se")
            se_norm[i] = p.v("se")

            @blkA.gpsimd
            def _(g, i=i, sev=se_norm[i]):
                p.w(g, "se", sev)
                wsel, ot = (1, i) if i < 8 else (0, i - 8)
                p.dma(g, "wqc", wsel, wqkbf_d[wsel, ot], wqkT_d[wsel, ot])

            @blkA.tensor
            def _(t, i=i, m=m, pr=pst_rel[m]):
                p.w(t, "se", se_norm[i])
                if i == 0:
                    p.w(t, "ge", GE_SETUP)
                p.w(t, "ve", pr)
                for j in range(8):
                    _mm = t.transpose(pst[:, 1024 * m + 128 * j:1024 * m + 128 * (j + 1)],
                                xbf[:, m, 128 * j:128 * (j + 1)], ident[:, :])
                p.inc(_mm, "te")
            te_tr = p.v("te")
            te_a1[i] = te_tr

            @blkA.vector
            def _(v, i=i, m=m, te_tr=te_tr):
                p.w(v, "te", te_tr)
                _v = v.tensor_copy(x_fm[:, 0:8, 128 * i:128 * (i + 1)],
                              pst[:, 1024 * m:1024 * (m + 1)].rearrange("p (c t) -> p c t", t=128))
                p.inc(_v, "ve")
            pst_rel[m] = p.v("ve")
        VE_XFM = p.v("ve")

        if parts == "A1x":
            @blkA.gpsimd
            def _(g):
                p.w(g, "ve", VE_XFM)
                p.dma(g, "xs", 0, y_out[0:128, :], xbf[:, 0, :])
                p.dma(g, "xs", 0, y_out[128:256, :], xbf[:, 1, :])
                p.dma(g, "xs", 0, y_out[256:384, :], xin[:, 0, :])
                p.dma(g, "xs", 0, y_out[384:512, :], xin[:, 1, :])
                p.dma(g, "xs", 0, y_out[512:640, 0:12], stat[:, :])

            @blkA.sync
            def _(s):
                p.w(s, "xs0", p.v("xs0"))
            blkA.__exit__(None, None, None)
            pa.close()
            return nc

        if parts == "A1":
            @blkA.gpsimd
            def _(g):
                p.w(g, "ve", VE_XFM)
                p.dma(g, "xs", 0, y_out[:, :].flatten().rearrange("(q c t) -> q c t", q=128, c=8),
                      x_fm[:, :, :])

            @blkA.sync
            def _(s):
                p.w(s, "xs0", p.v("xs0"))
            blkA.__exit__(None, None, None)
            pa.close()
            return nc

        LDK_A2 = load_wqk8(1, 0, None)
        evK = [0] * NC
        evV = [0] * NT
        for r in range(NT):
            if r % 2 == 0:
                ot = r // 2

                @blkA.tensor
                def _(t, ot=ot):
                    if ot == 0:
                        p.wt(t, LDK_A2)
                        p.w(t, "ve", VE_XFM)
                    if ot >= 1:
                        p.w(t, "ve", evK[ot - 1])
                    for c in range(NC):
                        for nb in range(4):
                            _mm = t.matmul(ps_s[:, 512 * nb:512 * (nb + 1)],
                                     q8_buf[:, ot, c, :],
                                     x_fm[:, c, 512 * nb:512 * (nb + 1)],
                                     start=(c == 0), stop=(c == NC - 1))
                    p.inc(_mm, "te")
                teK = p.v("te")

                @blkA.vector
                def _(v, ot=ot, teK=teK):
                    p.w(v, "te", teK)
                    _v = v.tensor_copy(K_sb[:, ot, :], ps_s[:, :])
                    p.inc(_v, "ve")
                evK[ot] = p.v("ve")

            tt = r

            @blkA.gpsimd
            def _(g, tt=tt, sev=evV[tt - 1] if tt >= 1 else 0):
                p.w(g, "se", sev)
                p.dma(g, "iqc", 0, inqkbf_d[tt], inqkT_d[tt])

            @blkA.tensor
            def _(t, tt=tt):
                if tt == 0:
                    p.wt(t, LDW_A)
                    p.w(t, "ve", VE_XFM)
                if tt >= 1:
                    p.w(t, "se", evV[tt - 1])
                for c in range(NC):
                    for half in range(2):
                        _mm = t.matmul(ps_a[:, 512 * half:512 * (half + 1)],
                                 x_fm[:, c, 128 * tt:128 * (tt + 1)],
                                 wv_sb[:, c, 512 * half:512 * (half + 1)],
                                 start=(c == 0), stop=(c == NC - 1))
                p.inc(_mm, "te")
            teV = p.v("te")

            @blkA.scalar
            def _(s, tt=tt, teV=teV):
                p.w(s, "te", teV)
                _a = s.copy(V_sb[:, tt, :], ps_a[:, :])
                p.inc(_a, "se")
            evV[tt] = p.v("se")

        if parts in ("A2K", "A2V"):
            src_t = K_sb if parts == "A2K" else V_sb
            @blkA.gpsimd
            def _(g, src_t=src_t):
                p.w(g, "ve", evK[NC - 1])
                p.w(g, "se", evV[NT - 1])
                shp = src_t.shape
                p.dma(g, "xs", 0, y_out[:, :].flatten().rearrange("(q c t) -> q c t", q=128,
                                                                  c=shp[1]),
                      src_t[:, :, :])

            @blkA.sync
            def _(s):
                p.w(s, "xs0", p.v("xs0"))
            blkA.__exit__(None, None, None)
            pa.close()
            return nc

        te_scores_done = 0
        te_ctx_done = 0
        ve_outp_done = 0
        ve_x1rel = [0, 0]
        ldx_x1st = [0, 0]
        LDQ_A3 = load_wqk8(0, 1, ("te", p.v("te")))
        for g_ in range(4):
            seQ = [0] * NC
            for ot in range(NC):
                @blkA.tensor
                def _(t, g_=g_, ot=ot):
                    if g_ == 0 and ot == 0:
                        p.wt(t, LDQ_A3)
                    if g_ == 0 and ot < 2:
                        p.w(t, "se", evV[NT - 1])
                    if ot >= 2:
                        p.w(t, "se", seQ[ot - 2])
                    if g_ > 0 and ot < 2:
                        p.w(t, "ve", ve_outp_done)
                    for c in range(NC):
                        _mm = t.matmul(ps_a[:, 512 * (ot % 2):512 * (ot % 2 + 1)],
                                 q8_buf[:, ot, c, :],
                                 x_fm[:, c, 512 * g_:512 * (g_ + 1)],
                                 start=(c == 0), stop=(c == NC - 1))
                    p.inc(_mm, "te")
                teq = p.v("te")

                ck = 8 * (g_ // 2) + ot
                if g_ % 2 == 0:
                    @blkA.gpsimd
                    def _(g, teq=teq, ck=ck):
                        p.w(g, "te", teq)
                        if ck < 8:
                            p.dma(g, "ivc", 0, invbf_d[:, ck, :], invT_d[:, ck, :])
                        else:
                            p.dma(g, "ivc", 0, outwbf_d[:, ck - 8, :],
                                  outwT_d[:, ck - 8, :])

                @blkA.scalar
                def _(s, ot=ot, teq=teq, g_=g_):
                    p.w(s, "te", teq)
                    if g_ > 0 and ot == 0:
                        p.w(s, "te", te_scores_done)
                    _a = s.activation(Q_sb[:, ot, :], ps_a[:, 512 * (ot % 2):512 * (ot % 2 + 1)],
                                 AF.Identity, bias=ccol(C_ZERO), scale=ccol(C_INV32))
                    p.inc(_a, "se")
                seQ[ot] = p.v("se")

            se_exp = [0] * 4
            ve_recip = [0] * 4
            te_ptr = [0] * 4
            for qt in range(4):
                m = qt % 2
                tq = 4 * g_ + qt

                @blkA.tensor
                def _(t, qt=qt, g_=g_):
                    if qt == 0:
                        p.w(t, "se", seQ[NC - 1])
                        if g_ == 0:
                            p.w(t, "ve", evK[NC - 1])
                    if qt >= 1:
                        p.w(t, "se", se_exp[qt - 1])
                    for c in range(NC):
                        for nb in range(4):
                            _mm = t.matmul(ps_s[:, 512 * nb:512 * (nb + 1)],
                                     Q_sb[:, c, 128 * qt:128 * (qt + 1)],
                                     K_sb[:, c, 512 * nb:512 * (nb + 1)],
                                     start=(c == 0), stop=(c == NC - 1))
                    p.inc(_mm, "te")
                te_sc = p.v("te")
                if qt == 3:
                    te_scores_done = te_sc

                @blkA.vector
                def _(v, m=m, te_sc=te_sc):
                    p.w(v, "te", te_sc)
                    _v = v.reduce_max(negmax[:, m:m + 1], ps_s[:, :], axis=AX.X, negate=True)
                    p.inc(_v, "ve")
                ve_mx = p.v("ve")

                @blkA.scalar
                def _(s, m=m, ve_mx=ve_mx, qt=qt, ve_rp=(ve_recip[qt - 2] if qt >= 2 else 0)):
                    p.w(s, "ve", ve_mx)
                    p.w(s, "ve", ve_rp)
                    if qt >= 1:
                        p.w(s, "te", te_ptr[qt - 1])
                    _a = s.activation(P_sb[:, 0, :], ps_s[:, :], AF.Exp,
                                 bias=negmax[:, m:m + 1], scale=1.0,
                                 accum_out=rowsum[:, m:m + 1])
                    p.inc(_a, "se")
                se_exp[qt] = p.v("se")

                @blkA.vector
                def _(v, m=m, tq=tq, se_e=se_exp[qt]):
                    p.w(v, "se", se_e)
                    _v = v.reciprocal(recipA[:, tq:tq + 1], rowsum[:, m:m + 1])
                    p.inc(_v, "ve")
                ve_recip[qt] = p.v("ve")

                for half in range(2):
                    @blkA.tensor
                    def _(t, m=m, half=half, se_e=se_exp[qt], pr=pst_rel[half]):
                        p.w(t, "se", se_e)
                        p.w(t, "ve", pr)
                        for kk in range(8):
                            kc = 8 * half + kk
                            _mm = t.transpose(pst[:, 1024 * half + 128 * kk:1024 * half + 128 * (kk + 1)],
                                        P_sb[:, 0, 128 * kc:128 * (kc + 1)], ident[:, :])
                        p.inc(_mm, "te")
                    te_ptr[qt] = p.v("te")

                    @blkA.vector
                    def _(v, half=half, qt=qt, te_p=te_ptr[qt]):
                        p.w(v, "te", te_p)
                        _v = v.tensor_copy(PT_sb[:, 8 * half:8 * (half + 1), 128 * qt:128 * (qt + 1)],
                                      pst[:, 1024 * half:1024 * (half + 1)]
                                      .rearrange("p (c t) -> p c t", t=128))
                        p.inc(_v, "ve")
                    pst_rel[half] = p.v("ve")

            se_ctx = [0] * NC
            for dt_ in range(NC):
                @blkA.tensor
                def _(t, dt_=dt_, pr0=pst_rel[0], pr1=pst_rel[1]):
                    if dt_ == 0:
                        p.w(t, "ve", pr0)
                        p.w(t, "ve", pr1)
                    if dt_ >= 2:
                        p.w(t, "se", se_ctx[dt_ - 2])
                    for kc in range(16):
                        _mm = t.matmul(ps_a[:, 512 * (dt_ % 2):512 * (dt_ % 2 + 1)],
                                 V_sb[:, kc, 128 * dt_:128 * (dt_ + 1)],
                                 PT_sb[:, kc, :],
                                 start=(kc == 0), stop=(kc == 15))
                    p.inc(_mm, "te")
                te_c = p.v("te")
                if dt_ == NC - 1:
                    te_ctx_done = te_c

                @blkA.scalar
                def _(s, dt_=dt_, te_c=te_c):
                    p.w(s, "te", te_c)
                    _a = s.copy(CT_sb[:, dt_, :], ps_a[:, 512 * (dt_ % 2):512 * (dt_ % 2 + 1)])
                    p.inc(_a, "se")
                se_ctx[dt_] = p.v("se")

            ve_outp = [0] * 4
            for tt in range(4):
                tq = 4 * g_ + tt
                m = tt % 2
                first = (g_ == 0 and tt < 2)

                @blkA.sync
                def _(s, tq=tq, m=m, first=first):
                    if first:
                        p.w(s, "se", se_norm[14 + m])
                    else:
                        p.w(s, "ve", ve_x1rel[m])
                    p.dma(s, "xld", m, xin2[:, m, :], x_in[128 * tq:128 * (tq + 1), :])
                ldx2 = ("xld%d" % m, p.v("xld%d" % m))

                @blkA.tensor
                def _(t, tt=tt, g_=g_):
                    if tt == 0:
                        p.w(t, "se", se_ctx[NC - 1])
                        if g_ == 0:
                            p.wt(t, LDW_A)
                    if tt >= 1:
                        p.w(t, "ve", ve_outp[tt - 1])
                    for c in range(NC):
                        for half in range(2):
                            _mm = t.matmul(ps_a[:, 512 * half:512 * (half + 1)],
                                     CT_sb[:, c, 128 * tt:128 * (tt + 1)],
                                     wo_sb[:, c, 512 * half:512 * (half + 1)],
                                     start=(c == 0), stop=(c == NC - 1))
                    p.inc(_mm, "te")
                te_o = p.v("te")

                @blkA.vector
                def _(v, tq=tq, m=m, te_o=te_o, ldx2=ldx2, first=first, ve_r=ve_recip[3]):
                    p.w(v, "te", te_o)
                    p.wt(v, ldx2)
                    p.w(v, "ve", ve_r)
                    if not first:
                        p.wt(v, ldx_x1st[m])
                    _v = v.scalar_tensor_tensor(x1buf[:, m, :], ps_a[:, :], recipA[:, tq:tq + 1],
                                           xin2[:, m, :], ALU.mult, ALU.add)
                    p.inc(_v, "ve")
                ve_outp[tt] = p.v("ve")
                ve_x1rel[m] = ve_outp[tt]

                @blkA.sync
                def _(s, tq=tq, m=m, ve_o=ve_outp[tt]):
                    p.w(s, "ve", ve_o)
                    dst = y_out if parts == "A" else a2a_in
                    p.dma(s, "xs", m, dst[128 * tq:128 * (tq + 1), :], x1buf[:, m, :])
                ldx_x1st[m] = ("xs%d" % m, p.v("xs%d" % m))
            ve_outp_done = ve_outp[3]

        if parts == "A":
            @blkA.sync
            def _(s):
                p.wt(s, ldx_x1st[0])
                p.wt(s, ldx_x1st[1])
            blkA.__exit__(None, None, None)
            pa.close()
            return nc

        @blkA.gpsimd
        def _(g):
            p.wt(g, ldx_x1st[0])
            p.wt(g, ldx_x1st[1])
            with allow_ici_all_to_all():
                p.inc(g.collective_compute(
                    "AllToAll", ALU.bypass, replica_groups=[list(range(8))],
                    ins=[a2a_in.ap().opt()], outs=[a2a_out.ap().opt()]), "cc")
        CC1 = p.v("cc")

        @blkA.sync
        def _(s):
            p.w(s, "cc", CC1)
            for sh in range(8):
                p.dma(s, "x1p", 0, x1p[:, :].rearrange("(q s) d -> s q d", s=8)[sh],
                      a2a_out[256 * sh:256 * (sh + 1), :])
        LDX_X1P = ("x1p0", p.v("x1p0"))

        barrier(blkA, "bar1")
        blkA.__exit__(None, None, None)
        pa.close()
        pst_rel = [0, 0]

        pbP = ExitStack()
        tok_fm = sb(pbP, "tok_fm", [128, NC, 2048], BF16)
        gw_sb = sb(pbP, "gw_sb", [128, NC, 8], BF16)
        gb_sb = sb(pbP, "gb_sb", [8, 1], F32)
        b2_sb = sb(pbP, "b2_sb", [8, 1024], BF16)
        qb8 = sb(pbP, "qb8", [128, NC], F32)
        kbc = sb(pbP, "kbc", [128, NC], F32)
        combine = sb(pbP, "combine", [128, NT, 8], F32)
        combine_bf = sb(pbP, "combine_bf", [128, NT, 8], BF16)
        combT_sb = sb(pbP, "combT_sb", [8, NT, 128], BF16)
        zeros_bf = sb(pbP, "zeros_bf", [128, 1024], BF16)
        mask17 = sb(pbP, "mask17", [17, 256], BF16)
        ident32 = sb(pbP, "ident32", [128, 128], F32)
        onesrow = sb(pbP, "onesrow", [1, 128], F32)

        blkP = nc.Block()
        blkP.__enter__()

        @blkP.gpsimd
        def _(g):
            p.w(g, "bar1", p.v("bar1"))
            g.memset(ident32[:, :], 0.0)
            p.gsync(g)
            g.affine_select(out=ident32[:, :], in_=ident32[:, :], compare_op=ALU.not_equal,
                            fill=1.0, base=0, pattern=[[-1, 128]], channel_multiplier=1)
            g.memset(mask17[:, :], 0.0)
            for half, scl in ((0, 1.0), (128, MASK_L)):
                p.gsync(g)
                g.affine_select(out=mask17[0:16, half:half + 128], in_=mask17[0:16, half:half + 128],
                                compare_op=ALU.is_gt, fill=scl, base=-7,
                                pattern=[[1, 128]], channel_multiplier=-8)
                p.gsync(g)
                g.affine_select(out=mask17[0:16, half:half + 128], in_=mask17[0:16, half:half + 128],
                                compare_op=ALU.is_ge, fill=0.0, base=0,
                                pattern=[[1, 128]], channel_multiplier=-8)
            p.gsync(g)
            g.affine_select(out=mask17[:, 0:128], in_=mask17[:, 0:128],
                            compare_op=ALU.is_ge, fill=1.0, base=15,
                            pattern=[[0, 128]], channel_multiplier=-1)
            g.affine_select(out=mask17[:, 128:256], in_=mask17[:, 128:256],
                            compare_op=ALU.is_ge, fill=-MASK_L, base=15,
                            pattern=[[0, 128]], channel_multiplier=-1)
            g.memset(onesrow[:, :], 1.0)
            g.memset(zeros_bf[:, :], 0.0)
            p.dma(g, "gwb", 0, gw_sb[:, :, :], gwT_d[:, :, :])
            p.dma(g, "gwb", 0, b2_sb[:, :], b2_d[:, :])
            p.inc(g.engine_nop(), "ge")
        GE_PB = p.v("ge")

        @blkP.sync
        def _(s):
            p.dma(s, "lds", 0, gb_sb[:, :], gb_d[:, :])
            p.dma(s, "lds", 0, qb8[:, :], inqkb_d[0])
            p.dma(s, "lds", 0, kbc[:, :], inqkb_d[1])
        LDX_PB = ("lds0", p.v("lds0"))

        blkP.__exit__(None, None, None)

        pb1 = ExitStack()
        inqk_buf = sb(pb1, "inqk_buf", [128, 2, NC, 128], BF16)
        inv_sb = sb(pb1, "inv_sb", [128, NC, 1024], BF16)
        outwT_sb = sb(pb1, "outwT_sb", [128, NC, 1024], BF16)
        src_fm = sb(pb1, "src_fm", [128, NC, 1024], BF16)
        q_fm = sb(pb1, "q_fm", [128, NC, 1024], BF16)
        k_fm = sb(pb1, "k_fm", [128, NC, 1024], BF16)
        v_tm = sb(pb1, "v_tm", [128, 8, 1024], BF16)
        E_sb = sb(pb1, "E_sb", [128, 2, 2048], BF16)
        A_sb = E_sb
        AT_sb = sb(pb1, "AT_sb", [128, 16, 128], BF16)
        ctx_t = sb(pb1, "ctx_t", [128, 1, NC, 128], BF16)
        bxin = sb(pb1, "bxin", [128, 1, 1024], F32)
        srcb = sb(pb1, "srcb", [128, 2, 1024], F32)
        tokf = sb(pb1, "tokf", [128, 2, 1024], F32)
        tokb = sb(pb1, "tokb", [128, 2, 1024], BF16)
        sbf = sb(pb1, "sbf", [128, 1, 1024], BF16)
        bc_outb = sb(pb1, "bc_outb", [128, 1024], F32)
        bc_vb = sb(pb1, "bc_vb", [128, 1024], F32)
        bc_g1 = sb(pb1, "bc_g1", [128, 1024], F32)
        bc_b1 = sb(pb1, "bc_b1", [128, 1024], F32)
        seg16 = sb(pb1, "seg16", [128, 16], F32)
        vstage1 = sb(pb1, "vstage1", [1, 1024], F32)
        rec16 = sb(pb1, "rec16", [128, 16], F32)
        rbuf = sb(pb1, "rbuf", [128, 8], F32)

        blkB = nc.Block()
        blkB.__enter__()

        @blkB.sync
        def _(s):
            p.w(s, "bar1", p.v("bar1"))
            p.w(s, "ivc0", 256)
            p.dma(s, "wV", 0, inv_sb[:, :, :], invbf_d[:, :, :])
            p.dma(s, "wV", 0, outwT_sb[:, :, :], outwbf_d[:, :, :])
        LDW_INV = ("wV0", p.v("wV0"))

        w2cast_k = [0]

        def issue_w2cast(blk_, wait_sem, wait_val):
            k = w2cast_k[0]
            if k >= 64:
                return
            w2cast_k[0] = k + 1

            @blk_.gpsimd
            def _(g, k=k, wait_sem=wait_sem, wait_val=wait_val):
                p.w(g, wait_sem, wait_val)
                e_, j = divmod(k, 8)
                p.dma(g, "wc", e_, w2bf_d[e_, 512 * j:512 * (j + 1), :],
                      w2T_d[e_, 512 * j:512 * (j + 1), :])

        inqk_rel = [0, 0]

        def load_inqk(idx, slot):
            res = []

            @blkB.sync
            def _(s, idx=idx, slot=slot, rel=inqk_rel[slot]):
                p.w(s, "te", rel)
                if idx == 0:
                    p.w(s, "iqc0", 256)
                res.append(p.dma(s, "iqa", slot,
                                 inqk_buf[:, slot, :, :],
                                 inqkbf_d[idx].rearrange("p (c o) -> p c o", o=128)))
            return res

        def build_bcasts(blk_, specs, stage, wait0_ve=0):
            te_prev = 0
            ve_prev = 0
            out_ves = []
            for j, (vec_ap, bct) in enumerate(specs):
                @blk_.sync
                def _(s, j=j, vec_ap=vec_ap, te_prev=te_prev, stage=stage):
                    if j == 0:
                        p.w(s, "bar1", p.v("bar1"))
                    p.w(s, "te", te_prev)
                    p.dma(s, "vst", 0, stage[0:1, :], vec_ap)
                ldx_v = ("vst0", p.v("vst0"))

                @blk_.tensor
                def _(t, j=j, ldx_v=ldx_v, ve_prev=ve_prev, wait0_ve=wait0_ve, stage=stage):
                    p.wt(t, ldx_v)
                    if j == 0 and wait0_ve:
                        p.w(t, "ve", wait0_ve)
                    if j >= 1:
                        p.w(t, "ve", ve_prev)
                    for half in range(2):
                        _mm = t.matmul(ps_a[:, 512 * half:512 * (half + 1)], onesrow[:, :],
                                 stage[0:1, 512 * half:512 * (half + 1)],
                                 start=True, stop=True)
                    p.inc(_mm, "te")
                te_prev = p.v("te")

                @blk_.vector
                def _(v, bct=bct, te_prev=te_prev):
                    p.w(v, "te", te_prev)
                    _v = v.tensor_copy(bct[:, :], ps_a[:, :])
                    p.inc(_v, "ve")
                ve_prev = p.v("ve")
                out_ves.append(ve_prev)
            return ve_prev

        _r1 = lambda apx: apx.rearrange("(a d) -> a d", a=1)
        VE_BC = build_bcasts(blkB, [
            (_r1(outb_d[:]), bc_outb),
            (_r1(inb_d[2048:3072]), bc_vb),
            (_r1(lng_d[0]), bc_g1),
            (_r1(lng_d[1]), bc_b1),
        ], vstage1)

        @blkB.vector
        def _(v):
            p.wt(v, LDX_PB)
            _v = v.tensor_scalar_mul(qb8[:, :], qb8[:, :], ccol(C_INV8))
            p.inc(_v, "ve")
        VE_QB8 = p.v("ve")

        ve_tok_ready = [0] * NT
        se_tokb = [0] * NT
        ve_tokev = [0] * NT
        te_toktr = [0] * NT
        ldx_spill = [0] * NT
        toktr_queue = []

        se_exp_b = {}
        ve_an = {}
        te_at = {}
        ve_atv = {}
        ve_ctxev = {}
        ve_z1 = {}
        ve_outev = {}
        te_op_prev = {}
        se_n1_prev = {}
        ldx_at = {}
        se_srcb = {}
        H0_ATTN_TE = 0

        def emit_tok_transpose(ti):
            m = ti % 2
            bank = ti % 2

            @blkB.tensor
            def _(t, ti=ti, m=m, bank=bank, pr=pst_rel[bank]):
                p.w(t, "se", se_tokb[ti])
                p.w(t, "ve", pr)
                for j in range(8):
                    _mm = t.transpose(pst[:, 1024 * bank + 128 * j:1024 * bank + 128 * (j + 1)],
                                tokb[:, m, 128 * j:128 * (j + 1)], ident[:, :])
                p.inc(_mm, "te")
            te_toktr[ti] = p.v("te")

            @blkB.vector
            def _(v, ti=ti, bank=bank, te_t=te_toktr[ti]):
                p.w(v, "te", te_t)
                _v = v.tensor_copy(tok_fm[:, 0:8, 128 * ti:128 * (ti + 1)],
                              pst[:, 1024 * bank:1024 * (bank + 1)]
                              .rearrange("p (c t) -> p c t", t=128))
                p.inc(_v, "ve")
            pst_rel[bank] = p.v("ve")
            ve_tokev[ti] = p.v("ve")

        for H in range(2):
            tiles = list(range(8 * H, 8 * H + 8))

            se_nrm = {}
            te_bln = {}
            for ti in tiles:
                m = ti % 2

                @blkB.sync
                def _(s, ti=ti, m=m, H=H):
                    if ti == 0:
                        p.wt(s, LDX_X1P)
                    if ti >= 8 * H + 1:
                        p.w(s, "se", se_nrm[ti - 1])
                    if H == 1 and ti == 8:
                        p.w(s, "se", se_srcb.get(7, 0))
                    p.dma(s, "bx", 0, bxin[:, 0, :], x1p[128 * ti:128 * (ti + 1), :])
                ldxv = ("bx0", p.v("bx0"))

                nmr, rstd, ve_ln = ln_stats(blkB, bxin[:, 0, :], m, wait_ld=ldxv,
                                        nmr_dst=nmr2[:, ti:ti + 1], rstd_dst=rstd2[:, ti:ti + 1],
                                        wait_se=se_nrm.get(ti - 2, 0))

                @blkB.scalar
                def _(s, ti=ti, m=m, nmr=nmr, rstd=rstd, H=H, ve_ln=ve_ln):
                    p.w(s, "ve", ve_ln)
                    if ti >= 8 * H + 1:
                        p.w(s, "te", te_bln[ti - 1])
                    _a = s.activation(sbf[:, 0, :], bxin[:, 0, :], AF.Identity, bias=nmr, scale=rstd)
                    p.inc(_a, "se")
                se_nrm[ti] = p.v("se")
                issue_w2cast(blkB, "se", se_nrm[ti])

                @blkB.tensor
                def _(t, ti=ti, m=m, pr=pst_rel[m]):
                    p.w(t, "se", se_nrm[ti])
                    if ti == 0:
                        p.w(t, "ve", VE_BC)
                        p.w(t, "ge", GE_PB)
                    p.w(t, "ve", pr)
                    for j in range(8):
                        _mm = t.transpose(pst[:, 1024 * m + 128 * j:1024 * m + 128 * (j + 1)],
                                    sbf[:, 0, 128 * j:128 * (j + 1)], ident[:, :])
                    p.inc(_mm, "te")
                te_s = p.v("te")
                te_bln[ti] = te_s

                @blkB.vector
                def _(v, ti=ti, m=m, te_s=te_s, H=H):
                    p.w(v, "te", te_s)
                    if H == 1 and ti == 8:
                        p.w(v, "te", H0_ATTN_TE)
                    tl = ti - 8 * H
                    _v = v.tensor_copy(src_fm[:, 0:8, 128 * tl:128 * (tl + 1)],
                                  pst[:, 1024 * m:1024 * (m + 1)].rearrange("p (c t) -> p c t", t=128))
                    p.inc(_v, "ve")
                pst_rel[m] = p.v("ve")
            VE_SRC = p.v("ve")

            se_qk = {}
            for idx in range(16):
                isq = idx < 8
                ot = idx % 8
                reg = idx % 2
                slot = idx % 2
                ldw_qk = load_inqk(idx, slot)

                @blkB.tensor
                def _(t, idx=idx, reg=reg, slot=slot, H=H, ldw_qk=ldw_qk):
                    for tup in ldw_qk:
                        p.wt(t, tup)
                    if idx == 0:
                        p.w(t, "ve", VE_SRC)
                        if H == 1:
                            p.w(t, "se", se_exp_b.get(7, 0))
                    if idx >= 2:
                        p.w(t, "se", se_qk[idx - 2])
                    for c in range(NC):
                        for half in range(2):
                            _mm = t.matmul(ps_s[:, 1024 * reg + 512 * half:1024 * reg + 512 * (half + 1)],
                                     inqk_buf[:, slot, c, :],
                                     src_fm[:, c, 512 * half:512 * (half + 1)],
                                     start=(c == 0), stop=(c == NC - 1))
                    p.inc(_mm, "te")
                te_q = p.v("te")
                inqk_rel[slot] = te_q

                @blkB.scalar
                def _(s, isq=isq, ot=ot, reg=reg, te_q=te_q, idx=idx, H=H):
                    p.w(s, "te", te_q)
                    if idx == 0:
                        p.w(s, "ve", VE_QB8)
                        if H == 1:
                            p.w(s, "te", H0_ATTN_TE)
                    dst = (q_fm if isq else k_fm)[:, ot, :]
                    if isq:
                        _a = s.activation(dst, ps_s[:, 1024 * reg:1024 * (reg + 1)], AF.Identity,
                                     bias=qb8[:, ot:ot + 1], scale=ccol(C_INV8))
                    else:
                        _a = s.activation(dst, ps_s[:, 1024 * reg:1024 * (reg + 1)], AF.Identity,
                                     bias=kbc[:, ot:ot + 1], scale=1.0)
                    p.inc(_a, "se")
                se_qk[idx] = p.v("se")
                issue_w2cast(blkB, "se", se_qk[idx])

            ve_v = {}
            for tl in range(8):
                @blkB.tensor
                def _(t, tl=tl):
                    if tl == 0:
                        p.w(t, "ve", VE_SRC)
                        p.wt(t, LDW_INV)
                    if tl >= 1:
                        p.w(t, "ve", ve_v[tl - 1])
                    for c in range(NC):
                        for half in range(2):
                            _mm = t.matmul(ps_a[:, 512 * half:512 * (half + 1)],
                                     src_fm[:, c, 128 * tl:128 * (tl + 1)],
                                     inv_sb[:, c, 512 * half:512 * (half + 1)],
                                     start=(c == 0), stop=(c == NC - 1))
                    p.inc(_mm, "te")
                te_v = p.v("te")

                @blkB.vector
                def _(v, tl=tl, te_v=te_v, H=H):
                    p.w(v, "te", te_v)
                    if H == 1 and tl == 0:
                        p.w(v, "te", H0_ATTN_TE)
                    _v = v.tensor_add(v_tm[:, tl, :], ps_a[:, :], bc_vb[:, :])
                    p.inc(_v, "ve")
                ve_v[tl] = p.v("ve")
            VE_IN_V = p.v("ve")

            def emit_bx(ti, first):
                @blkB.sync
                def _(s, ti=ti, first=first, H=H):
                    if first:
                        p.w(s, "se", se_nrm[8 * H + 7])
                    else:
                        p.w(s, "se", se_srcb.get(ti - 1, 0))
                    p.dma(s, "bx", 0, bxin[:, 0, :], x1p[128 * ti:128 * (ti + 1), :])
                ldx_at[ti] = ("bx0", p.v("bx0"))

            def emit_head1(ti):
                tl = ti - 8 * H
                par = ti % 2

                @blkB.tensor
                def _(t, tl=tl, ti=ti):
                    if tl == 0:
                        p.w(t, "se", se_qk[15])
                        p.w(t, "ve", VE_IN_V)
                    if ti >= 1:
                        p.w(t, "se", se_exp_b.get(ti - 1, 0))
                    for h in range(16):
                        po = 64 * (h % 2)
                        cp = h // 2
                        col = 512 * (h // 4) + 128 * (h % 4)
                        _mm = t.matmul(ps_s[:, col:col + 128],
                                 q_fm[po:po + 64, cp, 128 * tl:128 * (tl + 1)],
                                 k_fm[po:po + 64, cp, 128 * tl:128 * (tl + 1)],
                                 start=True, stop=False)
                        _mm = t.matmul(ps_s[:, col:col + 128], mask17[:, 0:128], mask17[:, 128:256],
                                 start=False, stop=True)
                    p.inc(_mm, "te")
                te_sc = p.v("te")

                @blkB.scalar
                def _(s, te_sc=te_sc, ti=ti, par=par):
                    p.w(s, "te", te_sc)
                    if ti >= 2:
                        p.w(s, "te", te_at.get(ti - 2, 0))
                    for g2 in range(4):
                        _a = s.activation(E_sb[:, par, 512 * g2:512 * (g2 + 1)],
                                     ps_s[:, 512 * g2:512 * (g2 + 1)],
                                     AF.Exp, bias=0.0, scale=1.0)
                    p.inc(_a, "se")
                se_exp_b[ti] = p.v("se")

            def emit_head2(ti):
                par = ti % 2

                @blkB.vector
                def _(v, se_e=se_exp_b[ti], ti=ti, par=par):
                    p.w(v, "se", se_e)
                    _v = v.tensor_reduce(seg16[:, :],
                                    E_sb[:, par, :].rearrange("p (h t) -> p h t", t=128),
                                    axis=AX.X, op=ALU.add)
                    v.drain()
                    _v = v.reciprocal(rec16[:, :], seg16[:, :])
                    v.drain()
                    _v = v.tensor_tensor(A_sb[:, par, :].rearrange("p (h t) -> p h t", t=128),
                                    E_sb[:, par, :].rearrange("p (h t) -> p h t", t=128),
                                    rec16[:, :].unsqueeze(-1).broadcast_to([128, 16, 128]),
                                    ALU.mult)
                    p.inc(_v, "ve")
                ve_an[ti] = p.v("ve")

            def emit_tail1(ti):
                tl = ti - 8 * H
                m = ti % 2
                par = ti % 2
                for half in range(2):
                    @blkB.tensor
                    def _(t, half=half, ve_a=ve_an[ti], pr=pst_rel[half], par=par):
                        p.w(t, "ve", ve_a)
                        p.w(t, "ve", pr)
                        for kk in range(8):
                            hh = 8 * half + kk
                            _mm = t.transpose(pst[:, 1024 * half + 128 * kk:1024 * half + 128 * (kk + 1)],
                                        A_sb[:, par, 128 * hh:128 * (hh + 1)], ident[:, :])
                        p.inc(_mm, "te")
                    te_at[ti] = p.v("te")

                    @blkB.vector
                    def _(v, half=half, te_a=te_at[ti]):
                        p.w(v, "te", te_a)
                        _v = v.tensor_copy(AT_sb[:, 8 * half:8 * (half + 1), :],
                                      pst[:, 1024 * half:1024 * (half + 1)]
                                      .rearrange("p (c t) -> p c t", t=128))
                        p.inc(_v, "ve")
                    pst_rel[half] = p.v("ve")
                    ve_atv[(ti, half)] = p.v("ve")

                @blkB.tensor
                def _(t, tl=tl, ti=ti, m=m):
                    p.w(t, "ve", ve_atv[(ti, 1)])
                    if ti >= 1:
                        p.w(t, "ve", ve_ctxev.get(ti - 1, 0))
                        p.w(t, "ve", ve_z1.get(ti - 1, 0))
                    for h in range(16):
                        po = 64 * (h % 2)
                        col = 512 * (h // 8) + 128 * ((h // 2) % 4)
                        _mm = t.matmul(ps_a[po:po + 64, col:col + 128],
                                 v_tm[:, tl, 64 * h:64 * (h + 1)],
                                 AT_sb[:, h, :],
                                 start=True, stop=True)
                    p.inc(_mm, "te")
                te_cx = p.v("te")

                @blkB.vector
                def _(v, ti=ti, te_cx=te_cx):
                    p.w(v, "te", te_cx)
                    if ti >= 1:
                        p.w(v, "te", te_op_prev.get(ti - 1, 0))
                    for half in range(2):
                        _v = v.tensor_copy(ctx_t[:, 0, 4 * half:4 * (half + 1), :],
                                      ps_a[:, 512 * half:512 * (half + 1)]
                                      .rearrange("p (c t) -> p c t", t=128))
                    p.inc(_v, "ve")
                ve_ctxev[ti] = p.v("ve")

                @blkB.tensor
                def _(t, m=m, ve_c=ve_ctxev[ti]):
                    p.w(t, "ve", ve_c)
                    for c in range(NC):
                        for half in range(2):
                            _mm = t.matmul(ps_a[:, 512 * half:512 * (half + 1)],
                                     ctx_t[:, 0, c, :],
                                     outwT_sb[:, c, 512 * half:512 * (half + 1)],
                                     start=(c == 0), stop=(c == NC - 1))
                    p.inc(_mm, "te")
                te_op_prev[ti] = p.v("te")

            def emit_tail2(ti):
                m = ti % 2

                @blkB.scalar
                def _(s, ti=ti, m=m, ldx_v=ldx_at[ti]):
                    p.wt(s, ldx_v)
                    _a = s.activation(srcb[:, m, :], bxin[:, 0, :], AF.Identity,
                                 bias=nmr2[:, ti:ti + 1], scale=rstd2[:, ti:ti + 1])
                    p.inc(_a, "se")
                se_srcb[ti] = p.v("se")
                issue_w2cast(blkB, "se", se_srcb[ti])
                if ti + 1 <= tiles[-1]:
                    emit_bx(ti + 1, False)

                @blkB.vector
                def _(v, m=m, te_op=te_op_prev[ti], se_sr=se_srcb[ti]):
                    p.w(v, "te", te_op)
                    p.w(v, "se", se_sr)
                    _v = v.tensor_add(srcb[:, m, :], srcb[:, m, :], ps_a[:, 0:1024])
                    v.drain()
                    _v = v.tensor_add(srcb[:, m, :], srcb[:, m, :], bc_outb[:, :])
                    p.inc(_v, "ve")
                ve_z1[ti] = p.v("ve")
                ve_outev[ti] = ve_z1[ti]

                nmr, rstd, ve_ln = ln_stats(blkB, srcb[:, m, :], m, wait_ve=ve_z1[ti],
                                        wait_se=se_n1_prev.get(ti - 2, 0))

                @blkB.scalar
                def _(s, m=m, nmr=nmr, rstd=rstd, ti=ti, ve_ln=ve_ln):
                    p.w(s, "ve", ve_ln)
                    if ti >= 2:
                        p.wt(s, ldx_spill[ti - 2])
                    _a = s.activation(tokf[:, m, :], srcb[:, m, :], AF.Identity, bias=nmr, scale=rstd)
                    p.inc(_a, "se")
                se_n1 = p.v("se")
                se_n1_prev[ti] = se_n1

                @blkB.vector
                def _(v, m=m, se_n1=se_n1, ti=ti):
                    p.w(v, "se", se_n1)
                    _v = v.tensor_mul(tokf[:, m, :], tokf[:, m, :], bc_g1[:, :])
                    v.drain()
                    _v = v.tensor_add(tokf[:, m, :], tokf[:, m, :], bc_b1[:, :])
                    p.inc(_v, "ve")
                ve_tok_ready[ti] = p.v("ve")

                @blkB.scalar
                def _(s, m=m, ti=ti, ve_t=ve_tok_ready[ti]):
                    p.w(s, "ve", ve_t)
                    if ti >= 2:
                        p.w(s, "te", te_toktr[ti - 2])
                    _a = s.copy(tokb[:, m, :], tokf[:, m, :])
                    p.inc(_a, "se")
                se_tokb[ti] = p.v("se")

                @blkB.sync
                def _(s, ti=ti, m=m, ve_t=ve_tok_ready[ti]):
                    p.w(s, "ve", ve_t)
                    p.dma(s, "sp", m, tokspill[128 * ti:128 * (ti + 1), :], tokf[:, m, :])
                ldx_spill[ti] = ("sp%d" % m, p.v("sp%d" % m))

                toktr_queue.append(ti)

            emit_bx(tiles[0], True)
            for ti in tiles:
                if len(toktr_queue) >= 2:
                    emit_tok_transpose(toktr_queue.pop(0))
                emit_head1(ti)
                if ti > tiles[0]:
                    emit_tail1(ti - 1)
                emit_head2(ti)
                if ti > tiles[0]:
                    emit_tail2(ti - 1)
            if toktr_queue:
                emit_tok_transpose(toktr_queue.pop(0))
            emit_tail1(tiles[-1])
            emit_tail2(tiles[-1])

            if H == 0:
                H0_ATTN_TE = p.v("te")

        while toktr_queue:
            emit_tok_transpose(toktr_queue.pop(0))
        VE_TOKFM = p.v("ve")

        barrier(blkB, "bar2")
        blkB.__exit__(None, None, None)
        pb1.close()

        pbG = ExitStack()
        lfm = sb(pbG, "lfm", [8, 2048], F32)
        ltm = sb(pbG, "ltm", [128, NT, 8], F32)
        gsc = sb(pbG, "gsc", [128, 66], F32)
        blkG = nc.Block()
        blkG.__enter__()

        se_lfm = [0] * 4
        for nb in range(4):
            @blkG.tensor
            def _(t, nb=nb):
                if nb == 0:
                    p.w(t, "bar2", p.v("bar2"))
                    p.w(t, "gwb0", p.v("gwb0"))
                if nb >= 2:
                    p.w(t, "se", se_lfm[nb - 2])
                for c in range(NC):
                    _mm = t.matmul(ps_a[0:8, 512 * (nb % 2):512 * (nb % 2) + 512],
                             gw_sb[:, c, :], tok_fm[:, c, 512 * nb:512 * (nb + 1)],
                             start=(c == 0), stop=(c == NC - 1))
                p.inc(_mm, "te")
            te_g = p.v("te")

            @blkG.scalar
            def _(s, nb=nb, te_g=te_g):
                p.w(s, "te", te_g)
                if nb == 0:
                    p.wt(s, LDX_PB)
                _a = s.activation(lfm[0:8, 512 * nb:512 * (nb + 1)],
                             ps_a[0:8, 512 * (nb % 2):512 * (nb % 2) + 512],
                             AF.Identity, bias=gb_sb[:, 0:1], scale=1.0)
                p.inc(_a, "se")
            se_lfm[nb] = p.v("se")

        @blkG.tensor
        def _(t):
            p.w(t, "se", se_lfm[3])
            for i in range(NT):
                _mm = t.transpose(ps_s[:, 8 * i:8 * (i + 1)], lfm[0:8, 128 * i:128 * (i + 1)],
                            ident32[0:8, 0:8])
            p.inc(_mm, "te")
        TE_LT = p.v("te")

        @blkG.vector
        def _(v):
            p.w(v, "te", TE_LT)
            _v = v.tensor_copy(ltm[:, :, :], ps_s[:, 0:128].rearrange("p (i e) -> p i e", e=8))
            p.inc(_v, "ve")
        VE_LTM = p.v("ve")

        for i in range(NT):
            @blkG.vector
            def _(v, i=i):
                if i == 0:
                    p.w(v, "ve", VE_LTM)
                l = ltm[:, i, :]
                _v = v.tensor_max(gsc[:, 0:4], l[:, 0:4], l[:, 4:8])
                v.drain()
                _v = v.tensor_max(gsc[:, 4:6], gsc[:, 0:2], gsc[:, 2:4])
                v.drain()
                _v = v.tensor_max(gsc[:, 6:7], gsc[:, 4:5], gsc[:, 5:6])
                v.drain()
                _v = v.tensor_scalar_mul(gsc[:, 7:8], gsc[:, 6:7], ccol(C_NEG1))
                _v = v.tensor_single_scalar(gsc[:, 8:16], l, gsc[:, 6:7], ALU.is_equal)
                v.drain()
                _v = v.scalar_tensor_tensor(gsc[:, 16:24], gsc[:, 8:16], ccol(C_NEGBIG), l,
                                       ALU.mult, ALU.add)
                v.drain()
                _v = v.tensor_max(gsc[:, 24:28], gsc[:, 16:20], gsc[:, 20:24])
                v.drain()
                _v = v.tensor_max(gsc[:, 28:30], gsc[:, 24:26], gsc[:, 26:28])
                v.drain()
                _v = v.tensor_max(gsc[:, 30:31], gsc[:, 28:29], gsc[:, 29:30])
                v.drain()
                _v = v.tensor_single_scalar(gsc[:, 32:40], l, gsc[:, 30:31], ALU.is_ge)
                p.inc(_v, "ve")
            ve_rt = p.v("ve")

            @blkG.scalar
            def _(s, i=i, ve_rt=ve_rt):
                p.w(s, "ve", ve_rt)
                _a = s.activation(gsc[:, 40:48], ltm[:, i, :], AF.Exp, bias=gsc[:, 7:8], scale=1.0)
                p.inc(_a, "se")
            se_e8 = p.v("se")

            @blkG.vector
            def _(v, se_e8=se_e8):
                p.w(v, "se", se_e8)
                _v = v.tensor_mul(gsc[:, 48:56], gsc[:, 40:48], gsc[:, 32:40])
                v.drain()
                _v = v.tensor_add(gsc[:, 56:60], gsc[:, 48:52], gsc[:, 52:56])
                v.drain()
                _v = v.tensor_add(gsc[:, 60:62], gsc[:, 56:58], gsc[:, 58:60])
                v.drain()
                _v = v.tensor_add(gsc[:, 62:63], gsc[:, 60:61], gsc[:, 61:62])
                v.drain()
                _v = v.reciprocal(gsc[:, 64:65], gsc[:, 62:63])
                v.drain()
                _v = v.tensor_scalar_mul(combine[:, i, :], gsc[:, 48:56], gsc[:, 64:65])
                v.drain()
                _v = v.tensor_copy(combine_bf[:, i, :], combine[:, i, :])
                p.inc(_v, "ve")
        VE_COMB = p.v("ve")

        @blkG.tensor
        def _(t, pr0=pst_rel[0], pr1=pst_rel[1]):
            p.w(t, "ve", VE_COMB)
            p.w(t, "ve", pr0)
            p.w(t, "ve", pr1)
            for i in range(NT):
                _mm = t.transpose(pst[0:8, 128 * i:128 * (i + 1)], combine_bf[:, i, :], ident[:, :])
            p.inc(_mm, "te")
        TE_CT = p.v("te")

        @blkG.vector
        def _(v):
            p.w(v, "te", TE_CT)
            _v = v.tensor_copy(combT_sb[:, :, :], pst[0:8, 0:2048].rearrange("p (i t) -> p i t", t=128))
            p.inc(_v, "ve")
        VE_CT = p.v("ve")
        pst_rel = [VE_CT, VE_CT]

        barrier(blkG, "bar2")
        blkG.__exit__(None, None, None)
        pbG.close()

        pb2 = ExitStack()
        h_sb = sb(pb2, "h_sb", [128, 16, 1024], BF16)
        w2sb = sb(pb2, "w2sb", [128, 16, 1024], BF16)
        moe_acc = sb(pb2, "moe_acc", [128, 2, 8, 1024], F32)
        w1buf = sb(pb2, "w1buf", [128, 4, NC, 128], BF16)
        b1c = sb(pb2, "b1c", [128, 32], F32)
        fxb = sb(pb2, "fxb", [128, 1, 1024], F32)
        fnb = sb(pb2, "fnb", [128, 1, 1024], F32)
        ftk = sb(pb2, "ftk", [128, 1, 1024], F32)
        bc_g2 = sb(pb2, "bc_g2", [128, 1024], F32)
        bc_b2 = sb(pb2, "bc_b2", [128, 1024], F32)
        vstage2 = sb(pb2, "vstage2", [1, 1024], F32)

        blkM = nc.Block()
        blkM.__enter__()

        VE_BC2 = build_bcasts(blkM, [
            (_r1(lng_d[2]), bc_g2),
            (_r1(lng_d[3]), bc_b2),
        ], vstage2, wait0_ve=VE_CT)

        prev_y_te = 0
        prev_y_ve = 0
        prev_hev14 = None
        fin_se_prev = [0]

        W1SEQ = [(e_, 16 * fh_ + ft_) for _bk in range(2) for e_ in range(NE)
                 for fh_ in range(2) for ft_ in range(16)]
        w1_gi = [0]
        w1_ld = {}

        def issue_w1(gate_te, n=1):
            gis = list(range(w1_gi[0], min(w1_gi[0] + n, len(W1SEQ))))
            if not gis:
                return
            w1_gi[0] = gis[-1] + 1

            @blkM.gpsimd
            def _(g, gate_te=gate_te, gis=gis):
                p.w(g, "te", gate_te)
                if gis[0] == 0:
                    p.w(g, "bar2", p.v("bar2"))
                for gi in gis:
                    e_, t_ = W1SEQ[gi]
                    src = w1T_d[e_, t_].rearrange("p (c o) -> p c o", o=128)
                    s_ = gi % 4
                    w1_ld[gi] = [
                        p.dma(g, "w1a", s_, w1buf[:, s_, 0:4, :], src[:, 0:4, :]),
                        p.dma(g, "w1b", s_, w1buf[:, s_, 4:8, :], src[:, 4:8, :]),
                    ]
        sto_rel = [None, None]
        last_spill = [ldx_spill[14], ldx_spill[15]]
        if last_spill[0][0] != "sp0":
            last_spill = [ldx_spill[15], ldx_spill[14]]

        def emit_final(blki, tt8):
            ti = 8 * blki + tt8
            m = tt8 % 2
            vm = ve_moe[tt8]

            @blkM.sync
            def _(s, ti=ti, m=m, sr=sto_rel[0], spv=last_spill[m]):
                p.wt(s, sr)
                p.wt(s, spv)
                p.dma(s, "fl", 0, ftk[:, 0, :], tokspill[128 * ti:128 * (ti + 1), :])
                p.dma(s, "fl", 0, fxb[:, 0, :], x1p[128 * ti:128 * (ti + 1), :])
            ldx_f = ("fl0", p.v("fl0"))

            @blkM.vector
            def _(v, blki=blki, tt8=tt8, ldx_f=ldx_f, vm=vm):
                p.wt(v, ldx_f)
                p.w(v, "ve", vm)
                _v = v.tensor_add(ftk[:, 0, :], ftk[:, 0, :], moe_acc[:, blki, tt8, :])
                _v = v.tensor_add(fxb[:, 0, :], fxb[:, 0, :], bc_b2[:, :])
                p.inc(_v, "ve")
            ve_fz = p.v("ve")

            nmr, rstd, ve_ln = ln_stats(blkM, ftk[:, 0, :], m, wait_ve=ve_fz,
                                        wait_se=fin_se_prev[0])

            @blkM.scalar
            def _(s, nmr=nmr, rstd=rstd, ve_ln=ve_ln):
                p.w(s, "ve", ve_ln)
                _a = s.activation(fnb[:, 0, :], ftk[:, 0, :], AF.Identity, bias=nmr, scale=rstd)
                p.inc(_a, "se")
            se_fn = p.v("se")
            fin_se_prev[0] = se_fn

            @blkM.vector
            def _(v, se_fn=se_fn):
                p.w(v, "se", se_fn)
                _v = v.tensor_mul(fnb[:, 0, :], fnb[:, 0, :], bc_g2[:, :])
                v.drain()
                _v = v.tensor_add(fnb[:, 0, :], fnb[:, 0, :], fxb[:, 0, :])
                p.inc(_v, "ve")
            ve_fo = p.v("ve")

            @blkM.sync
            def _(s, ti=ti, ve_fo=ve_fo):
                p.w(s, "ve", ve_fo)
                p.dma(s, "st", 0, y_out[128 * ti:128 * (ti + 1), :], fnb[:, 0, :])
            sto_rel[0] = ("st0", p.v("st0"))

        for blki in range(2):
            ve_ini = {}
            for tt8 in range(8):
                @blkM.tensor
                def _(t, tt8=tt8, blki=blki):
                    if tt8 == 0:
                        p.w(t, "bar2", p.v("bar2"))
                        p.w(t, "ve", VE_BC2)
                        p.w(t, "gwb0", p.v("gwb0"))
                        if blki == 1:
                            p.w(t, "ve", prev_y_ve)
                    if tt8 >= 1:
                        p.w(t, "ve", ve_ini[tt8 - 1])
                    for half in range(2):
                        _mm = t.matmul(ps_a[:, 512 * half:512 * (half + 1)],
                                 combT_sb[:, 8 * blki + tt8, :],
                                 b2_sb[:, 512 * half:512 * (half + 1)],
                                 start=True, stop=True)
                    p.inc(_mm, "te")
                te_i = p.v("te")

                @blkM.vector
                def _(v, tt8=tt8, te_i=te_i, blki=blki):
                    p.w(v, "te", te_i)
                    _v = v.tensor_copy(moe_acc[:, blki, tt8, :], ps_a[:, :])
                    p.inc(_v, "ve")
                ve_ini[tt8] = p.v("ve")
            VE_INI = p.v("ve")

            ve_moe = {t_: VE_INI for t_ in range(8)}

            if blki == 0:
                issue_w1(0, 4)

            for e in range(NE):
                for fh in range(2):
                    gi0 = ((blki * NE + e) * 2 + fh) * 16
                    te_h = {}
                    hev_rel = {}

                    if fh == 0:
                        @blkM.gpsimd
                        def _(g, e=e, blki=blki, pyt=prev_y_te):
                            if e == 0 and blki == 0:
                                p.w(g, "bar2", p.v("bar2"))
                            p.w(g, "te", pyt)
                            p.dma(g, "b1", 0, b1c[:, :], b1_d[e])

                    @blkM.sync
                    def _(s, e=e, fh=fh, blki=blki, pyt=prev_y_te):
                        if e == 0 and fh == 0 and blki == 0:
                            p.w(s, "bar2", p.v("bar2"))
                        p.w(s, "te", pyt)
                        p.w(s, "wc%d" % e, 128)
                        for ftt in range(16):
                            p.dma(s, "w2", 0, w2sb[:, ftt, :],
                                  w2bf_d[e, 2048 * fh + 128 * ftt:
                                         2048 * fh + 128 * (ftt + 1), :])
                    ldw2seg = ("w20", p.v("w20"))

                    for ft in range(16):
                        gi = gi0 + ft

                        @blkM.tensor
                        def _(t, ft=ft, gi=gi, blki=blki, ldwv=w1_ld[gi], pyv=prev_y_ve,
                              b1v=("b10", p.v("b10")), ph=prev_hev14):
                            for tup in ldwv:
                                p.wt(t, tup)
                            if ft == 0:
                                p.wt(t, b1v)
                                if ph is not None:
                                    p.w(t, ph[0], ph[1])
                            if ft == 1:
                                p.w(t, "ve", pyv)
                            if ft >= 2:
                                rel = hev_rel[ft - 2]
                                p.w(t, rel[0], rel[1])
                            r_ = (ft + 1) % 2
                            for c in range(NC):
                                for sub in range(2):
                                    _mm = t.matmul(ps_s[:, 1024 * r_ + 512 * sub:
                                                  1024 * r_ + 512 * (sub + 1)],
                                             w1buf[:, gi % 4, c, :],
                                             tok_fm[:, c, 1024 * blki + 512 * sub:
                                                    1024 * blki + 512 * (sub + 1)],
                                             start=(c == 0), stop=(c == NC - 1))
                            p.inc(_mm, "te")
                        te_h[ft] = p.v("te")
                        issue_w1(te_h[ft])

                        r_ = (ft + 1) % 2

                        @blkM.vector
                        def _(v, fh=fh, ft=ft, r_=r_, te_v=te_h[ft], pyt=prev_y_te):
                            p.w(v, "te", te_v)
                            p.w(v, "te", pyt)
                            _v = v.scalar_tensor_tensor(h_sb[:, ft, :],
                                                   ps_s[:, 1024 * r_:1024 * (r_ + 1)],
                                                   b1c[:, 16 * fh + ft:16 * fh + ft + 1],
                                                   zeros_bf[:, :], ALU.add, ALU.max)
                            p.inc(_v, "ve")
                        hev_rel[ft] = ("ve", p.v("ve"))

                    ve_yev = {}
                    for tt8 in range(8):
                        @blkM.tensor
                        def _(t, tt8=tt8, h14=hev_rel[14], h15=hev_rel[15], ldwv=ldw2seg):
                            if tt8 == 0:
                                p.w(t, h14[0], h14[1])
                                p.w(t, h15[0], h15[1])
                                p.wt(t, ldwv)
                            if tt8 >= 2:
                                p.w(t, "ve", ve_yev[tt8 - 2])
                            reg = ps_a if tt8 % 2 == 0 else ps_s
                            for fs in range(16):
                                for half in range(2):
                                    _mm = t.matmul(reg[:, 512 * half:512 * (half + 1)],
                                             h_sb[:, fs, 128 * tt8:128 * (tt8 + 1)],
                                             w2sb[:, fs, 512 * half:512 * (half + 1)],
                                             start=(fs == 0), stop=(fs == 15))
                            p.inc(_mm, "te")
                        te_y = p.v("te")

                        @blkM.vector
                        def _(v, e=e, tt8=tt8, blki=blki, te_y=te_y, vm=ve_moe[tt8]):
                            p.w(v, "te", te_y)
                            p.w(v, "ve", vm)
                            reg = ps_a if tt8 % 2 == 0 else ps_s
                            _v = v.scalar_tensor_tensor(moe_acc[:, blki, tt8, :], reg[:, 0:1024],
                                                   combine[:, 8 * blki + tt8, e:e + 1],
                                                   moe_acc[:, blki, tt8, :], ALU.mult, ALU.add)
                            p.inc(_v, "ve")
                        ve_yev[tt8] = p.v("ve")
                        ve_moe[tt8] = p.v("ve")
                        if e == NE - 1 and fh == 1:
                            emit_final(blki, tt8)

                    prev_y_te = p.v("te")
                    prev_y_ve = ve_yev[7]
                    prev_hev14 = hev_rel[14]

        @blkM.sync
        def _(s):
            p.wt(s, sto_rel[0])
        blkM.__exit__(None, None, None)
        pb2.close()
        pbP.close()

    return nc


def _prep_inputs(inputs):
    W_qkv = np.asarray(inputs["L0_attn_W_qkv"], np.float32)
    share = {
        "wvT": np.ascontiguousarray(W_qkv[2::3].T.reshape(8, 128, D).transpose(1, 0, 2)),
        "wqkT": np.ascontiguousarray(np.stack([W_qkv[i::3].T for i in range(2)])
                                     .reshape(2, D, 8, 128).transpose(0, 2, 1, 3)
                                     .reshape(2, 8, 8, 128, 128).transpose(0, 1, 3, 2, 4)
                                     .reshape(2, 8, 128, D)),
        "woT": np.ascontiguousarray(np.asarray(inputs["L0_attn_W_out"], np.float32).T
                                    .reshape(8, 128, D).transpose(1, 0, 2)),
        "inqkT": np.ascontiguousarray(np.asarray(inputs["L0_ffn_in_proj_w"], np.float32).T[:, 0:2048]
                                      .reshape(D, 16, 128).transpose(1, 0, 2)
                                      .reshape(16, 8, 128, 128).transpose(0, 2, 1, 3)
                                      .reshape(16, 128, D)),
        "invT": np.ascontiguousarray(np.asarray(inputs["L0_ffn_in_proj_w"], np.float32).T[:, 2048:3072]
                                     .reshape(8, 128, D).transpose(1, 0, 2)),
        "inb": np.ascontiguousarray(np.asarray(inputs["L0_ffn_in_proj_b"], np.float32)),
        "inqkb": np.ascontiguousarray(
            np.asarray(inputs["L0_ffn_in_proj_b"], np.float32)[0:2048]
            .reshape(2, 8, 128).transpose(0, 2, 1)),
        "outwT": np.ascontiguousarray(np.asarray(inputs["L0_ffn_out_w"], np.float32).T
                                      .reshape(8, 128, D).transpose(1, 0, 2)),
        "outb": np.ascontiguousarray(np.asarray(inputs["L0_ffn_out_b"], np.float32)),
        "gwT": np.ascontiguousarray(np.asarray(inputs["L0_ffn_gate_w"], np.float32).T
                                    .reshape(8, 128, NE).transpose(1, 0, 2)),
        "gb": np.ascontiguousarray(np.asarray(inputs["L0_ffn_gate_b"], np.float32).reshape(NE, 1)),
        "w1T": np.ascontiguousarray(
            np.asarray(inputs["L0_ffn_w1"], np.float32).transpose(0, 2, 1)
            .reshape(NE, D, 32, 128).transpose(0, 2, 1, 3)
            .reshape(NE, 32, 8, 128, 128).transpose(0, 1, 3, 2, 4)
            .reshape(NE, 32, 128, D)),
        "b1": np.ascontiguousarray(np.asarray(inputs["L0_ffn_b1"], np.float32)
                                   .reshape(NE, 32, 128).transpose(0, 2, 1)),
        "w2T": np.ascontiguousarray(np.asarray(inputs["L0_ffn_w2"], np.float32).transpose(0, 2, 1)),
        "b2": np.ascontiguousarray(np.asarray(inputs["L0_ffn_b2"], np.float32)),
        "lng": np.ascontiguousarray(np.stack([
            np.asarray(inputs["L0_ffn_ln1_g"], np.float32),
            np.asarray(inputs["L0_ffn_ln1_b"], np.float32),
            np.asarray(inputs["L0_ffn_ln2_g"], np.float32),
            np.asarray(inputs["L0_ffn_ln2_b"], np.float32)])),
    }
    x = np.asarray(inputs["L0_attn_x"], np.float32)
    in_maps = []
    for c in range(8):
        m = dict(share)
        m["x"] = np.ascontiguousarray(x[c])
        in_maps.append(m)
    return in_maps


def kernel(**inputs):
    in_maps = _prep_inputs(inputs)
    nc = build_program("AB")
    res = run_bass_kernel_spmd(nc, in_maps, core_ids=list(range(8)))
    out = np.empty((B, S, D), np.float32)
    for c in range(8):
        yc = np.asarray(res.results[c]["y"], np.float32)
        out[:, 256 * c:256 * (c + 1), :] = yc.reshape(256, 8, D).transpose(1, 0, 2)
    return out


# version 14
# speedup vs baseline: 1.0614x; 1.0185x over previous
import numpy as np
from contextlib import ExitStack

import concourse.bass as bass
import concourse.mybir as mybir
from concourse.bass_utils import run_bass_kernel_spmd
from concourse.aot_env import allow_ici_all_to_all

F32 = mybir.dt.float32
BF16 = mybir.dt.bfloat16
AF = mybir.ActivationFunctionType
ALU = mybir.AluOpType
AX = mybir.AxisListType

B, S, D = 8, 2048, 1024
NT = 16
NC = 8
FF = 4096
NE = 8
MASK_L = 30720.0

C_EPS, C_INV1024, C_NEGINV1024, C_INV32, C_INV8, C_NEG1, C_NEGBIG, C_NEGHALF, C_ZERO, C_ONE = range(10)


class P:
    def __init__(self, nc):
        self.nc = nc
        self.cnt = {}
        self.sems = {}

    def sem(self, stack, name):
        self.sems[name] = stack.enter_context(self.nc.semaphore(name))
        self.cnt[name] = 0

    def inc(self, instr, name, n=1):
        instr.then_inc(self.sems[name], n)
        self.cnt[name] += n
        return self.cnt[name]

    def v(self, name):
        return self.cnt[name]

    def w(self, eng, name, val):
        if val > 0:
            eng.wait_ge(self.sems[name], val)

    def gsync(self, g):
        self.inc(g.engine_nop(), "ge")
        self.w(g, "ge", self.v("ge"))

    def wt(self, eng, tup):
        if tup and tup[1] > 0:
            eng.wait_ge(self.sems[tup[0]], tup[1])

    def dma(self, q, stream, slot, out, in_, slow=False):
        sem = f"{stream}{slot}"
        v = self.inc(q.dma_start(out=out, in_=in_, allow_slow_non_contiguous=slow), sem, 16)
        return (sem, v)


def build_program(parts="AB"):
    nc = bass.Bass("TRN2", target_bir_lowering=False)

    x_in = nc.dram_tensor("x", [S, D], F32, kind="ExternalInput")
    wvT_d = nc.dram_tensor("wvT", [128, NC, D], F32, kind="ExternalInput")
    wqkT_d = nc.dram_tensor("wqkT", [2, NC, 128, D], F32, kind="ExternalInput")
    woT_d = nc.dram_tensor("woT", [128, NC, D], F32, kind="ExternalInput")
    inqkT_d = nc.dram_tensor("inqkT", [16, 128, D], F32, kind="ExternalInput")
    invT_d = nc.dram_tensor("invT", [128, NC, D], F32, kind="ExternalInput")
    inb_d = nc.dram_tensor("inb", [3 * D], F32, kind="ExternalInput")
    inqkb_d = nc.dram_tensor("inqkb", [2, 128, NC], F32, kind="ExternalInput")
    outwT_d = nc.dram_tensor("outwT", [128, NC, D], F32, kind="ExternalInput")
    outb_d = nc.dram_tensor("outb", [D], F32, kind="ExternalInput")
    gwT_d = nc.dram_tensor("gwT", [128, NC, NE], F32, kind="ExternalInput")
    gb_d = nc.dram_tensor("gb", [NE, 1], F32, kind="ExternalInput")
    w1T_d = nc.dram_tensor("w1T", [NE, 32, 128, D], F32, kind="ExternalInput")
    b1_d = nc.dram_tensor("b1", [NE, 128, 32], F32, kind="ExternalInput")
    w2T_d = nc.dram_tensor("w2T", [NE, FF, D], F32, kind="ExternalInput")
    b2_d = nc.dram_tensor("b2", [NE, D], F32, kind="ExternalInput")
    lng_d = nc.dram_tensor("lng", [4, D], F32, kind="ExternalInput")

    w2bf_d = nc.dram_tensor("w2bf", [NE, FF, D], BF16)
    wqkbf_d = nc.dram_tensor("wqkbf", [2, NC, 128, D], BF16)
    inqkbf_d = nc.dram_tensor("inqkbf", [16, 128, D], BF16)
    invbf_d = nc.dram_tensor("invbf", [128, NC, D], BF16)
    outwbf_d = nc.dram_tensor("outwbf", [128, NC, D], BF16)
    a2a_in = nc.dram_tensor("a2a_in", [S, D], F32)
    a2a_out = nc.dram_tensor("a2a_out", [S, D], F32)
    x1p = nc.dram_tensor("x1p", [S, D], F32)
    tokspill = nc.dram_tensor("tokspill", [S, D], F32)
    y_out = nc.dram_tensor("y", [S, D], F32, kind="ExternalOutput")

    p = P(nc)

    with ExitStack() as outer:
        def sb(st, name, shape, dt=F32):
            return st.enter_context(nc.sbuf_tensor(name, shape, dt))

        ident = sb(outer, "ident", [128, 128], BF16)
        consts = sb(outer, "consts", [128, 10], F32)
        stat = sb(outer, "stat", [128, 12], F32)
        negmax = sb(outer, "negmax", [128, 2], F32)
        rowsum = sb(outer, "rowsum", [128, 2], F32)
        recipA = sb(outer, "recipA", [128, NT], F32)
        nmr2 = sb(outer, "nmr2", [128, NT], F32)
        rstd2 = sb(outer, "rstd2", [128, NT], F32)
        sqscr = sb(outer, "sqscr", [128, 1024], F32)
        magic = sb(outer, "magic", [128, 1], mybir.dt.int32)

        ps_s = outer.enter_context(nc.psum_tensor("ps_s", [128, 2048], F32))
        ps_a = outer.enter_context(nc.psum_tensor("ps_a", [128, 1024], F32))
        pst = outer.enter_context(nc.psum_tensor("pst", [128, 2048], BF16))

        for s_ in ("te", "ve", "se", "ge", "cc", "bar1", "bar2",
                   "xld0", "xld1", "xs0", "xs1", "x1p0", "wA0",
                   "wqk0", "wqk1", "wqk2", "wqk3",
                   "lds0", "vst0", "bx0", "bx1", "sp0", "sp1", "wV0",
                   "iqa0", "iqa1", "iqa2", "iqa3", "iqb0", "iqb1", "iqb2", "iqb3",
                   "gwb0", "b10",
                   "w1a0", "w1a1", "w1a2", "w1a3", "w1b0", "w1b1", "w1b2", "w1b3",
                   "w20", "fl0", "fl1", "st0", "st1",
                   "wc0", "wc1", "wc2", "wc3", "wc4", "wc5", "wc6", "wc7",
                   "wqc0", "wqc1", "iqc0", "ivc0"):
            p.sem(outer, s_)

        def ccol(i):
            return consts[:, i:i + 1]

        blk = nc.Block()
        blk.__enter__()

        @blk.gpsimd
        def _(g):
            g.memset(ident[:, :], 0.0)
            p.gsync(g)
            g.affine_select(out=ident[:, :], in_=ident[:, :], compare_op=ALU.not_equal,
                            fill=1.0, base=0, pattern=[[-1, 128]], channel_multiplier=1)
            for idx, v_ in [(C_EPS, 1e-5), (C_INV1024, 1.0 / 1024.0),
                            (C_NEGINV1024, -1.0 / 1024.0), (C_INV32, 1.0 / 32.0),
                            (C_INV8, 1.0 / 8.0), (C_NEG1, -1.0), (C_NEGBIG, -1e30),
                            (C_NEGHALF, -0.5), (C_ZERO, 0.0), (C_ONE, 1.0)]:
                g.memset(consts[:, idx:idx + 1], v_)
            g.memset(magic[:, :], 0x5f3759df)
            g.memset(stat[:, :], 0.0)
            p.inc(g.engine_nop(), "ge")
        GE_SETUP = p.v("ge")

        blk.__exit__(None, None, None)

        def ln_stats(blk_, src_ap, m, wait_ld=None, nmr_dst=None, rstd_dst=None, wait_ve=0, wait_se=0):
            musum = stat[:, m:m + 1]
            negmu = stat[:, 2 + m:3 + m]
            sd = stat[:, 4 + m:5 + m]
            rstd = rstd_dst if rstd_dst is not None else stat[:, 6 + m:7 + m]
            nmr = nmr_dst if nmr_dst is not None else stat[:, 8 + m:9 + m]

            @blk_.vector
            def _(v):
                p.wt(v, wait_ld)
                p.w(v, "ve", wait_ve)
                p.w(v, "se", wait_se)
                _v = v.reduce_sum(musum, src_ap, axis=AX.X)
                v.drain()
                _v = v.tensor_scalar_mul(negmu, musum, ccol(C_NEGINV1024))
                p.inc(_v, "ve")
            ve_req = p.v("ve")

            @blk_.scalar
            def _(s):
                p.w(s, "ve", ve_req)
                _a = s.activation(sqscr[:, :], src_ap, AF.Square, bias=negmu,
                             accum_out=musum)
                s.drain()
                _a = s.activation(sd, musum, AF.Ln, bias=ccol(C_EPS), scale=ccol(C_INV1024))
                s.drain()
                _a = s.activation(rstd, sd, AF.Exp, bias=ccol(C_ZERO), scale=ccol(C_NEGHALF))
                s.drain()
                _a = s.activation(nmr, negmu, AF.Identity, bias=ccol(C_ZERO), scale=rstd)
                p.inc(_a, "se")
                s.drain()
            return nmr, rstd, ve_req

        def barrier(blk_, name):
            for dec, sem_ in ((blk_.vector, "ve"), (blk_.scalar, "se"), (blk_.gpsimd, "ge")):
                @dec
                def _(e, sem_=sem_):
                    p.inc(e.engine_nop() if sem_ != "se" else e.drain(), sem_)
            snap = {k: v for k, v in p.cnt.items() if v > 0 and k not in ("bar1", "bar2")}
            for dec in (blk_.tensor, blk_.vector, blk_.scalar, blk_.gpsimd, blk_.sync):
                @dec
                def _(e):
                    for k, v in snap.items():
                        p.w(e, k, v)
                    p.inc(e.nop(), name)
            tgt = p.v(name)
            for dec in (blk_.tensor, blk_.vector, blk_.scalar, blk_.gpsimd, blk_.sync):
                @dec
                def _(e):
                    p.w(e, name, tgt)

        pa = ExitStack()
        q8_buf = sb(pa, "q8_buf", [128, NC, NC, 128], BF16)
        wv_sb = sb(pa, "wv_sb", [128, NC, 1024], BF16)
        wo_sb = sb(pa, "wo_sb", [128, NC, 1024], BF16)
        x_fm = sb(pa, "x_fm", [128, NC, 2048], BF16)
        K_sb = sb(pa, "K_sb", [128, NC, 2048], BF16)
        V_sb = sb(pa, "V_sb", [128, NT, 1024], BF16)
        Q_sb = sb(pa, "Q_sb", [128, NC, 512], BF16)
        P_sb = sb(pa, "P_sb", [128, 1, 2048], BF16)
        PT_sb = sb(pa, "PT_sb", [128, 16, 512], BF16)
        CT_sb = sb(pa, "CT_sb", [128, NC, 512], BF16)
        xin = sb(pa, "xin", [128, 2, 1024], F32)
        xbf = sb(pa, "xbf", [128, 2, 1024], BF16)
        xin2 = xin
        x1buf = sb(pa, "x1buf", [128, 2, 1024], F32)

        blkA = nc.Block()
        blkA.__enter__()

        @blkA.gpsimd
        def _(g):
            for wsb, src in ((wv_sb, wvT_d[:, :, :]), (wo_sb, woT_d[:, :, :])):
                p.dma(g, "wA", 0, wsb[:, :, :], src)
        LDW_A = ("wA0", p.v("wA0"))
        def load_wqk8(wsel, sem_slot, wait_tup):
            @blkA.sync
            def _(s, wsel=wsel, sem_slot=sem_slot, wait_tup=wait_tup):
                p.wt(s, wait_tup)
                p.w(s, "wqc%d" % wsel, 128)
                for ot in range(NC):
                    p.dma(s, "wqk", sem_slot, q8_buf[:, ot, :, :],
                          wqkbf_d[wsel, ot].rearrange("p (c o) -> p c o", o=128))
            return ("wqk%d" % sem_slot, p.v("wqk%d" % sem_slot))

        pst_rel = [0, 0]

        @blkA.vector
        def _(v):
            p.w(v, "ge", GE_SETUP)

        @blkA.scalar
        def _(s):
            p.w(s, "ge", GE_SETUP)

        se_norm = [0] * NT
        te_a1 = [0] * NT
        for i in range(NT):
            m = i % 2

            @blkA.sync
            def _(s, i=i, m=m):
                if i >= 2:
                    p.w(s, "se", se_norm[i - 2])
                p.dma(s, "xld", m, xin[:, m, :], x_in[128 * i:128 * (i + 1), :])
            ldxv = ("xld%d" % m, p.v("xld%d" % m))

            nmr, rstd, ve_ln = ln_stats(blkA, xin[:, m, :], m, wait_ld=ldxv,
                                        wait_se=(se_norm[i - 2] if i >= 2 else 0))

            @blkA.scalar
            def _(s, i=i, m=m, nmr=nmr, rstd=rstd, ve_ln=ve_ln):
                p.w(s, "ve", ve_ln)
                if i >= 2:
                    p.w(s, "te", te_a1[i - 2])
                _a = s.activation(xbf[:, m, :], xin[:, m, :], AF.Identity, bias=nmr, scale=rstd)
                p.inc(_a, "se")
            se_norm[i] = p.v("se")

            @blkA.gpsimd
            def _(g, i=i, sev=se_norm[i]):
                p.w(g, "se", sev)
                wsel, ot = (1, i) if i < 8 else (0, i - 8)
                p.dma(g, "wqc", wsel, wqkbf_d[wsel, ot], wqkT_d[wsel, ot])

            @blkA.tensor
            def _(t, i=i, m=m, pr=pst_rel[m]):
                p.w(t, "se", se_norm[i])
                if i == 0:
                    p.w(t, "ge", GE_SETUP)
                p.w(t, "ve", pr)
                for j in range(8):
                    _mm = t.transpose(pst[:, 1024 * m + 128 * j:1024 * m + 128 * (j + 1)],
                                xbf[:, m, 128 * j:128 * (j + 1)], ident[:, :])
                p.inc(_mm, "te")
            te_tr = p.v("te")
            te_a1[i] = te_tr

            @blkA.vector
            def _(v, i=i, m=m, te_tr=te_tr):
                p.w(v, "te", te_tr)
                _v = v.tensor_copy(x_fm[:, 0:8, 128 * i:128 * (i + 1)],
                              pst[:, 1024 * m:1024 * (m + 1)].rearrange("p (c t) -> p c t", t=128))
                p.inc(_v, "ve")
            pst_rel[m] = p.v("ve")
        VE_XFM = p.v("ve")

        if parts == "A1x":
            @blkA.gpsimd
            def _(g):
                p.w(g, "ve", VE_XFM)
                p.dma(g, "xs", 0, y_out[0:128, :], xbf[:, 0, :])
                p.dma(g, "xs", 0, y_out[128:256, :], xbf[:, 1, :])
                p.dma(g, "xs", 0, y_out[256:384, :], xin[:, 0, :])
                p.dma(g, "xs", 0, y_out[384:512, :], xin[:, 1, :])
                p.dma(g, "xs", 0, y_out[512:640, 0:12], stat[:, :])

            @blkA.sync
            def _(s):
                p.w(s, "xs0", p.v("xs0"))
            blkA.__exit__(None, None, None)
            pa.close()
            return nc

        if parts == "A1":
            @blkA.gpsimd
            def _(g):
                p.w(g, "ve", VE_XFM)
                p.dma(g, "xs", 0, y_out[:, :].flatten().rearrange("(q c t) -> q c t", q=128, c=8),
                      x_fm[:, :, :])

            @blkA.sync
            def _(s):
                p.w(s, "xs0", p.v("xs0"))
            blkA.__exit__(None, None, None)
            pa.close()
            return nc

        LDK_A2 = load_wqk8(1, 0, None)
        evK = [0] * NC
        evV = [0] * NT
        for r in range(NT):
            if r % 2 == 0:
                ot = r // 2

                @blkA.tensor
                def _(t, ot=ot):
                    if ot == 0:
                        p.wt(t, LDK_A2)
                        p.w(t, "ve", VE_XFM)
                    if ot >= 1:
                        p.w(t, "ve", evK[ot - 1])
                    for c in range(NC):
                        for nb in range(4):
                            _mm = t.matmul(ps_s[:, 512 * nb:512 * (nb + 1)],
                                     q8_buf[:, ot, c, :],
                                     x_fm[:, c, 512 * nb:512 * (nb + 1)],
                                     start=(c == 0), stop=(c == NC - 1))
                    p.inc(_mm, "te")
                teK = p.v("te")

                @blkA.vector
                def _(v, ot=ot, teK=teK):
                    p.w(v, "te", teK)
                    _v = v.tensor_copy(K_sb[:, ot, :], ps_s[:, :])
                    p.inc(_v, "ve")
                evK[ot] = p.v("ve")

            tt = r

            @blkA.gpsimd
            def _(g, tt=tt, sev=evV[tt - 1] if tt >= 1 else 0):
                p.w(g, "se", sev)
                p.dma(g, "iqc", 0, inqkbf_d[tt], inqkT_d[tt])

            @blkA.tensor
            def _(t, tt=tt):
                if tt == 0:
                    p.wt(t, LDW_A)
                    p.w(t, "ve", VE_XFM)
                if tt >= 1:
                    p.w(t, "se", evV[tt - 1])
                for c in range(NC):
                    for half in range(2):
                        _mm = t.matmul(ps_a[:, 512 * half:512 * (half + 1)],
                                 x_fm[:, c, 128 * tt:128 * (tt + 1)],
                                 wv_sb[:, c, 512 * half:512 * (half + 1)],
                                 start=(c == 0), stop=(c == NC - 1))
                p.inc(_mm, "te")
            teV = p.v("te")

            @blkA.scalar
            def _(s, tt=tt, teV=teV):
                p.w(s, "te", teV)
                _a = s.copy(V_sb[:, tt, :], ps_a[:, :])
                p.inc(_a, "se")
            evV[tt] = p.v("se")

        if parts in ("A2K", "A2V"):
            src_t = K_sb if parts == "A2K" else V_sb
            @blkA.gpsimd
            def _(g, src_t=src_t):
                p.w(g, "ve", evK[NC - 1])
                p.w(g, "se", evV[NT - 1])
                shp = src_t.shape
                p.dma(g, "xs", 0, y_out[:, :].flatten().rearrange("(q c t) -> q c t", q=128,
                                                                  c=shp[1]),
                      src_t[:, :, :])

            @blkA.sync
            def _(s):
                p.w(s, "xs0", p.v("xs0"))
            blkA.__exit__(None, None, None)
            pa.close()
            return nc

        te_scores_done = 0
        te_ctx_done = 0
        ve_outp_done = 0
        ve_x1rel = [0, 0]
        ldx_x1st = [0, 0]
        LDQ_A3 = load_wqk8(0, 1, ("te", p.v("te")))
        for g_ in range(4):
            seQ = [0] * NC
            for ot in range(NC):
                @blkA.tensor
                def _(t, g_=g_, ot=ot):
                    if g_ == 0 and ot == 0:
                        p.wt(t, LDQ_A3)
                    if g_ == 0 and ot < 2:
                        p.w(t, "se", evV[NT - 1])
                    if ot >= 2:
                        p.w(t, "se", seQ[ot - 2])
                    if g_ > 0 and ot < 2:
                        p.w(t, "ve", ve_outp_done)
                    for c in range(NC):
                        _mm = t.matmul(ps_a[:, 512 * (ot % 2):512 * (ot % 2 + 1)],
                                 q8_buf[:, ot, c, :],
                                 x_fm[:, c, 512 * g_:512 * (g_ + 1)],
                                 start=(c == 0), stop=(c == NC - 1))
                    p.inc(_mm, "te")
                teq = p.v("te")

                ck = 8 * (g_ // 2) + ot
                if g_ % 2 == 0:
                    @blkA.gpsimd
                    def _(g, teq=teq, ck=ck):
                        p.w(g, "te", teq)
                        if ck < 8:
                            p.dma(g, "ivc", 0, invbf_d[:, ck, :], invT_d[:, ck, :])
                        else:
                            p.dma(g, "ivc", 0, outwbf_d[:, ck - 8, :],
                                  outwT_d[:, ck - 8, :])

                @blkA.scalar
                def _(s, ot=ot, teq=teq, g_=g_):
                    p.w(s, "te", teq)
                    if g_ > 0 and ot == 0:
                        p.w(s, "te", te_scores_done)
                    _a = s.activation(Q_sb[:, ot, :], ps_a[:, 512 * (ot % 2):512 * (ot % 2 + 1)],
                                 AF.Identity, bias=ccol(C_ZERO), scale=ccol(C_INV32))
                    p.inc(_a, "se")
                seQ[ot] = p.v("se")

            se_exp = [0] * 4
            ve_recip = [0] * 4
            te_ptr = [0] * 4
            for qt in range(4):
                m = qt % 2
                tq = 4 * g_ + qt

                @blkA.tensor
                def _(t, qt=qt, g_=g_):
                    if qt == 0:
                        p.w(t, "se", seQ[NC - 1])
                        if g_ == 0:
                            p.w(t, "ve", evK[NC - 1])
                    if qt >= 1:
                        p.w(t, "se", se_exp[qt - 1])
                    for c in range(NC):
                        for nb in range(4):
                            _mm = t.matmul(ps_s[:, 512 * nb:512 * (nb + 1)],
                                     Q_sb[:, c, 128 * qt:128 * (qt + 1)],
                                     K_sb[:, c, 512 * nb:512 * (nb + 1)],
                                     start=(c == 0), stop=(c == NC - 1))
                    p.inc(_mm, "te")
                te_sc = p.v("te")
                if qt == 3:
                    te_scores_done = te_sc

                @blkA.scalar
                def _(s, m=m, te_sc=te_sc, qt=qt, ve_rp=(ve_recip[qt - 2] if qt >= 2 else 0)):
                    p.w(s, "te", te_sc)
                    p.w(s, "ve", ve_rp)
                    if qt >= 1:
                        p.w(s, "te", te_ptr[qt - 1])
                    _a = s.activation(P_sb[:, 0, :], ps_s[:, :], AF.Exp,
                                 bias=0.0, scale=1.0,
                                 accum_out=rowsum[:, m:m + 1])
                    p.inc(_a, "se")
                se_exp[qt] = p.v("se")

                @blkA.vector
                def _(v, m=m, tq=tq, se_e=se_exp[qt]):
                    p.w(v, "se", se_e)
                    _v = v.reciprocal(recipA[:, tq:tq + 1], rowsum[:, m:m + 1])
                    p.inc(_v, "ve")
                ve_recip[qt] = p.v("ve")

                for half in range(2):
                    @blkA.tensor
                    def _(t, m=m, half=half, se_e=se_exp[qt], pr=pst_rel[half]):
                        p.w(t, "se", se_e)
                        p.w(t, "ve", pr)
                        for kk in range(8):
                            kc = 8 * half + kk
                            _mm = t.transpose(pst[:, 1024 * half + 128 * kk:1024 * half + 128 * (kk + 1)],
                                        P_sb[:, 0, 128 * kc:128 * (kc + 1)], ident[:, :])
                        p.inc(_mm, "te")
                    te_ptr[qt] = p.v("te")

                    @blkA.vector
                    def _(v, half=half, qt=qt, te_p=te_ptr[qt]):
                        p.w(v, "te", te_p)
                        _v = v.tensor_copy(PT_sb[:, 8 * half:8 * (half + 1), 128 * qt:128 * (qt + 1)],
                                      pst[:, 1024 * half:1024 * (half + 1)]
                                      .rearrange("p (c t) -> p c t", t=128))
                        p.inc(_v, "ve")
                    pst_rel[half] = p.v("ve")

            se_ctx = [0] * NC
            for dt_ in range(NC):
                @blkA.tensor
                def _(t, dt_=dt_, pr0=pst_rel[0], pr1=pst_rel[1]):
                    if dt_ == 0:
                        p.w(t, "ve", pr0)
                        p.w(t, "ve", pr1)
                    if dt_ >= 2:
                        p.w(t, "se", se_ctx[dt_ - 2])
                    for kc in range(16):
                        _mm = t.matmul(ps_a[:, 512 * (dt_ % 2):512 * (dt_ % 2 + 1)],
                                 V_sb[:, kc, 128 * dt_:128 * (dt_ + 1)],
                                 PT_sb[:, kc, :],
                                 start=(kc == 0), stop=(kc == 15))
                    p.inc(_mm, "te")
                te_c = p.v("te")
                if dt_ == NC - 1:
                    te_ctx_done = te_c

                @blkA.scalar
                def _(s, dt_=dt_, te_c=te_c):
                    p.w(s, "te", te_c)
                    _a = s.copy(CT_sb[:, dt_, :], ps_a[:, 512 * (dt_ % 2):512 * (dt_ % 2 + 1)])
                    p.inc(_a, "se")
                se_ctx[dt_] = p.v("se")

            ve_outp = [0] * 4
            for tt in range(4):
                tq = 4 * g_ + tt
                m = tt % 2
                first = (g_ == 0 and tt < 2)

                @blkA.sync
                def _(s, tq=tq, m=m, first=first):
                    if first:
                        p.w(s, "se", se_norm[14 + m])
                    else:
                        p.w(s, "ve", ve_x1rel[m])
                    p.dma(s, "xld", m, xin2[:, m, :], x_in[128 * tq:128 * (tq + 1), :])
                ldx2 = ("xld%d" % m, p.v("xld%d" % m))

                @blkA.tensor
                def _(t, tt=tt, g_=g_):
                    if tt == 0:
                        p.w(t, "se", se_ctx[NC - 1])
                        if g_ == 0:
                            p.wt(t, LDW_A)
                    if tt >= 1:
                        p.w(t, "ve", ve_outp[tt - 1])
                    for c in range(NC):
                        for half in range(2):
                            _mm = t.matmul(ps_a[:, 512 * half:512 * (half + 1)],
                                     CT_sb[:, c, 128 * tt:128 * (tt + 1)],
                                     wo_sb[:, c, 512 * half:512 * (half + 1)],
                                     start=(c == 0), stop=(c == NC - 1))
                    p.inc(_mm, "te")
                te_o = p.v("te")

                @blkA.vector
                def _(v, tq=tq, m=m, te_o=te_o, ldx2=ldx2, first=first, ve_r=ve_recip[3]):
                    p.w(v, "te", te_o)
                    p.wt(v, ldx2)
                    p.w(v, "ve", ve_r)
                    if not first:
                        p.wt(v, ldx_x1st[m])
                    _v = v.scalar_tensor_tensor(x1buf[:, m, :], ps_a[:, :], recipA[:, tq:tq + 1],
                                           xin2[:, m, :], ALU.mult, ALU.add)
                    p.inc(_v, "ve")
                ve_outp[tt] = p.v("ve")
                ve_x1rel[m] = ve_outp[tt]

                @blkA.sync
                def _(s, tq=tq, m=m, ve_o=ve_outp[tt]):
                    p.w(s, "ve", ve_o)
                    dst = y_out if parts == "A" else a2a_in
                    p.dma(s, "xs", m, dst[128 * tq:128 * (tq + 1), :], x1buf[:, m, :])
                ldx_x1st[m] = ("xs%d" % m, p.v("xs%d" % m))
            ve_outp_done = ve_outp[3]

        if parts == "A":
            @blkA.sync
            def _(s):
                p.wt(s, ldx_x1st[0])
                p.wt(s, ldx_x1st[1])
            blkA.__exit__(None, None, None)
            pa.close()
            return nc

        @blkA.gpsimd
        def _(g):
            p.wt(g, ldx_x1st[0])
            p.wt(g, ldx_x1st[1])
            with allow_ici_all_to_all():
                p.inc(g.collective_compute(
                    "AllToAll", ALU.bypass, replica_groups=[list(range(8))],
                    ins=[a2a_in.ap().opt()], outs=[a2a_out.ap().opt()]), "cc")
        CC1 = p.v("cc")

        @blkA.sync
        def _(s):
            p.w(s, "cc", CC1)
            for sh in range(8):
                p.dma(s, "x1p", 0, x1p[:, :].rearrange("(q s) d -> s q d", s=8)[sh],
                      a2a_out[256 * sh:256 * (sh + 1), :])
        LDX_X1P = ("x1p0", p.v("x1p0"))

        barrier(blkA, "bar1")
        blkA.__exit__(None, None, None)
        pa.close()
        pst_rel = [0, 0]

        pbP = ExitStack()
        tok_fm = sb(pbP, "tok_fm", [128, NC, 2048], BF16)
        gw_sb = sb(pbP, "gw_sb", [128, NC, 8], BF16)
        gb_sb = sb(pbP, "gb_sb", [8, 1], F32)
        b2_sb = sb(pbP, "b2_sb", [8, 1024], BF16)
        qb8 = sb(pbP, "qb8", [128, NC], F32)
        kbc = sb(pbP, "kbc", [128, NC], F32)
        combine = sb(pbP, "combine", [128, NT, 8], F32)
        combine_bf = sb(pbP, "combine_bf", [128, NT, 8], BF16)
        combT_sb = sb(pbP, "combT_sb", [8, NT, 128], BF16)
        zeros_bf = sb(pbP, "zeros_bf", [128, 1024], BF16)
        mask17 = sb(pbP, "mask17", [17, 256], BF16)
        ident32 = sb(pbP, "ident32", [128, 128], F32)
        onesrow = sb(pbP, "onesrow", [1, 128], F32)

        blkP = nc.Block()
        blkP.__enter__()

        @blkP.gpsimd
        def _(g):
            p.w(g, "bar1", p.v("bar1"))
            g.memset(ident32[:, :], 0.0)
            p.gsync(g)
            g.affine_select(out=ident32[:, :], in_=ident32[:, :], compare_op=ALU.not_equal,
                            fill=1.0, base=0, pattern=[[-1, 128]], channel_multiplier=1)
            g.memset(mask17[:, :], 0.0)
            for half, scl in ((0, 1.0), (128, MASK_L)):
                p.gsync(g)
                g.affine_select(out=mask17[0:16, half:half + 128], in_=mask17[0:16, half:half + 128],
                                compare_op=ALU.is_gt, fill=scl, base=-7,
                                pattern=[[1, 128]], channel_multiplier=-8)
                p.gsync(g)
                g.affine_select(out=mask17[0:16, half:half + 128], in_=mask17[0:16, half:half + 128],
                                compare_op=ALU.is_ge, fill=0.0, base=0,
                                pattern=[[1, 128]], channel_multiplier=-8)
            p.gsync(g)
            g.affine_select(out=mask17[:, 0:128], in_=mask17[:, 0:128],
                            compare_op=ALU.is_ge, fill=1.0, base=15,
                            pattern=[[0, 128]], channel_multiplier=-1)
            g.affine_select(out=mask17[:, 128:256], in_=mask17[:, 128:256],
                            compare_op=ALU.is_ge, fill=-MASK_L, base=15,
                            pattern=[[0, 128]], channel_multiplier=-1)
            g.memset(onesrow[:, :], 1.0)
            g.memset(zeros_bf[:, :], 0.0)
            p.dma(g, "gwb", 0, gw_sb[:, :, :], gwT_d[:, :, :])
            p.dma(g, "gwb", 0, b2_sb[:, :], b2_d[:, :])
            p.inc(g.engine_nop(), "ge")
        GE_PB = p.v("ge")

        @blkP.sync
        def _(s):
            p.dma(s, "lds", 0, gb_sb[:, :], gb_d[:, :])
            p.dma(s, "lds", 0, qb8[:, :], inqkb_d[0])
            p.dma(s, "lds", 0, kbc[:, :], inqkb_d[1])
        LDX_PB = ("lds0", p.v("lds0"))

        blkP.__exit__(None, None, None)

        pb1 = ExitStack()
        inqk_buf = sb(pb1, "inqk_buf", [128, 2, NC, 128], BF16)
        inv_sb = sb(pb1, "inv_sb", [128, NC, 1024], BF16)
        outwT_sb = sb(pb1, "outwT_sb", [128, NC, 1024], BF16)
        src_fm = sb(pb1, "src_fm", [128, NC, 1024], BF16)
        q_fm = sb(pb1, "q_fm", [128, NC, 1024], BF16)
        k_fm = sb(pb1, "k_fm", [128, NC, 1024], BF16)
        v_tm = sb(pb1, "v_tm", [128, 8, 1024], BF16)
        E_sb = sb(pb1, "E_sb", [128, 2, 2048], BF16)
        A_sb = E_sb
        AT_sb = sb(pb1, "AT_sb", [128, 16, 128], BF16)
        ctx_t = sb(pb1, "ctx_t", [128, 1, NC, 128], BF16)
        bxin = sb(pb1, "bxin", [128, 1, 1024], F32)
        srcb = sb(pb1, "srcb", [128, 2, 1024], F32)
        tokf = sb(pb1, "tokf", [128, 2, 1024], F32)
        tokb = sb(pb1, "tokb", [128, 2, 1024], BF16)
        sbf = sb(pb1, "sbf", [128, 1, 1024], BF16)
        bc_outb = sb(pb1, "bc_outb", [128, 1024], F32)
        bc_vb = sb(pb1, "bc_vb", [128, 1024], F32)
        bc_g1 = sb(pb1, "bc_g1", [128, 1024], F32)
        bc_b1 = sb(pb1, "bc_b1", [128, 1024], F32)
        seg16 = sb(pb1, "seg16", [128, 16], F32)
        vstage1 = sb(pb1, "vstage1", [1, 1024], F32)
        rec16 = sb(pb1, "rec16", [128, 16], F32)
        rbuf = sb(pb1, "rbuf", [128, 8], F32)

        blkB = nc.Block()
        blkB.__enter__()

        @blkB.sync
        def _(s):
            p.w(s, "bar1", p.v("bar1"))
            p.w(s, "ivc0", 256)
            p.dma(s, "wV", 0, inv_sb[:, :, :], invbf_d[:, :, :])
            p.dma(s, "wV", 0, outwT_sb[:, :, :], outwbf_d[:, :, :])
        LDW_INV = ("wV0", p.v("wV0"))

        w2cast_k = [0]

        def issue_w2cast(blk_, wait_sem, wait_val):
            k = w2cast_k[0]
            if k >= 64:
                return
            w2cast_k[0] = k + 1

            @blk_.gpsimd
            def _(g, k=k, wait_sem=wait_sem, wait_val=wait_val):
                p.w(g, wait_sem, wait_val)
                e_, j = divmod(k, 8)
                p.dma(g, "wc", e_, w2bf_d[e_, 512 * j:512 * (j + 1), :],
                      w2T_d[e_, 512 * j:512 * (j + 1), :])

        inqk_rel = [0, 0]

        def load_inqk(idx, slot):
            res = []

            @blkB.sync
            def _(s, idx=idx, slot=slot, rel=inqk_rel[slot]):
                p.w(s, "te", rel)
                if idx == 0:
                    p.w(s, "iqc0", 256)
                res.append(p.dma(s, "iqa", slot,
                                 inqk_buf[:, slot, :, :],
                                 inqkbf_d[idx].rearrange("p (c o) -> p c o", o=128)))
            return res

        def build_bcasts(blk_, specs, stage, wait0_ve=0):
            te_prev = 0
            ve_prev = 0
            out_ves = []
            for j, (vec_ap, bct) in enumerate(specs):
                @blk_.sync
                def _(s, j=j, vec_ap=vec_ap, te_prev=te_prev, stage=stage):
                    if j == 0:
                        p.w(s, "bar1", p.v("bar1"))
                    p.w(s, "te", te_prev)
                    p.dma(s, "vst", 0, stage[0:1, :], vec_ap)
                ldx_v = ("vst0", p.v("vst0"))

                @blk_.tensor
                def _(t, j=j, ldx_v=ldx_v, ve_prev=ve_prev, wait0_ve=wait0_ve, stage=stage):
                    p.wt(t, ldx_v)
                    if j == 0 and wait0_ve:
                        p.w(t, "ve", wait0_ve)
                    if j >= 1:
                        p.w(t, "ve", ve_prev)
                    for half in range(2):
                        _mm = t.matmul(ps_a[:, 512 * half:512 * (half + 1)], onesrow[:, :],
                                 stage[0:1, 512 * half:512 * (half + 1)],
                                 start=True, stop=True)
                    p.inc(_mm, "te")
                te_prev = p.v("te")

                @blk_.vector
                def _(v, bct=bct, te_prev=te_prev):
                    p.w(v, "te", te_prev)
                    _v = v.tensor_copy(bct[:, :], ps_a[:, :])
                    p.inc(_v, "ve")
                ve_prev = p.v("ve")
                out_ves.append(ve_prev)
            return ve_prev

        _r1 = lambda apx: apx.rearrange("(a d) -> a d", a=1)
        VE_BC = build_bcasts(blkB, [
            (_r1(outb_d[:]), bc_outb),
            (_r1(inb_d[2048:3072]), bc_vb),
            (_r1(lng_d[0]), bc_g1),
            (_r1(lng_d[1]), bc_b1),
        ], vstage1)

        @blkB.vector
        def _(v):
            p.wt(v, LDX_PB)
            _v = v.tensor_scalar_mul(qb8[:, :], qb8[:, :], ccol(C_INV8))
            p.inc(_v, "ve")
        VE_QB8 = p.v("ve")

        ve_tok_ready = [0] * NT
        se_tokb = [0] * NT
        ve_tokev = [0] * NT
        te_toktr = [0] * NT
        ldx_spill = [0] * NT
        toktr_queue = []

        se_exp_b = {}
        ve_an = {}
        te_at = {}
        ve_atv = {}
        ve_ctxev = {}
        ve_z1 = {}
        ve_outev = {}
        te_op_prev = {}
        se_n1_prev = {}
        ldx_at = {}
        se_srcb = {}
        H0_ATTN_TE = 0

        def emit_tok_transpose(ti):
            m = ti % 2
            bank = ti % 2

            @blkB.tensor
            def _(t, ti=ti, m=m, bank=bank, pr=pst_rel[bank]):
                p.w(t, "se", se_tokb[ti])
                p.w(t, "ve", pr)
                for j in range(8):
                    _mm = t.transpose(pst[:, 1024 * bank + 128 * j:1024 * bank + 128 * (j + 1)],
                                tokb[:, m, 128 * j:128 * (j + 1)], ident[:, :])
                p.inc(_mm, "te")
            te_toktr[ti] = p.v("te")

            @blkB.vector
            def _(v, ti=ti, bank=bank, te_t=te_toktr[ti]):
                p.w(v, "te", te_t)
                _v = v.tensor_copy(tok_fm[:, 0:8, 128 * ti:128 * (ti + 1)],
                              pst[:, 1024 * bank:1024 * (bank + 1)]
                              .rearrange("p (c t) -> p c t", t=128))
                p.inc(_v, "ve")
            pst_rel[bank] = p.v("ve")
            ve_tokev[ti] = p.v("ve")

        for H in range(2):
            tiles = list(range(8 * H, 8 * H + 8))

            se_nrm = {}
            te_bln = {}
            for ti in tiles:
                m = ti % 2

                @blkB.sync
                def _(s, ti=ti, m=m, H=H):
                    if ti == 0:
                        p.wt(s, LDX_X1P)
                    if ti >= 8 * H + 1:
                        p.w(s, "se", se_nrm[ti - 1])
                    if H == 1 and ti == 8:
                        p.w(s, "se", se_srcb.get(7, 0))
                    p.dma(s, "bx", 0, bxin[:, 0, :], x1p[128 * ti:128 * (ti + 1), :])
                ldxv = ("bx0", p.v("bx0"))

                nmr, rstd, ve_ln = ln_stats(blkB, bxin[:, 0, :], m, wait_ld=ldxv,
                                        nmr_dst=nmr2[:, ti:ti + 1], rstd_dst=rstd2[:, ti:ti + 1],
                                        wait_se=se_nrm.get(ti - 2, 0))

                @blkB.scalar
                def _(s, ti=ti, m=m, nmr=nmr, rstd=rstd, H=H, ve_ln=ve_ln):
                    p.w(s, "ve", ve_ln)
                    if ti >= 8 * H + 1:
                        p.w(s, "te", te_bln[ti - 1])
                    _a = s.activation(sbf[:, 0, :], bxin[:, 0, :], AF.Identity, bias=nmr, scale=rstd)
                    p.inc(_a, "se")
                se_nrm[ti] = p.v("se")
                issue_w2cast(blkB, "se", se_nrm[ti])

                @blkB.tensor
                def _(t, ti=ti, m=m, pr=pst_rel[m]):
                    p.w(t, "se", se_nrm[ti])
                    if ti == 0:
                        p.w(t, "ve", VE_BC)
                        p.w(t, "ge", GE_PB)
                    p.w(t, "ve", pr)
                    for j in range(8):
                        _mm = t.transpose(pst[:, 1024 * m + 128 * j:1024 * m + 128 * (j + 1)],
                                    sbf[:, 0, 128 * j:128 * (j + 1)], ident[:, :])
                    p.inc(_mm, "te")
                te_s = p.v("te")
                te_bln[ti] = te_s

                @blkB.vector
                def _(v, ti=ti, m=m, te_s=te_s, H=H):
                    p.w(v, "te", te_s)
                    if H == 1 and ti == 8:
                        p.w(v, "te", H0_ATTN_TE)
                    tl = ti - 8 * H
                    _v = v.tensor_copy(src_fm[:, 0:8, 128 * tl:128 * (tl + 1)],
                                  pst[:, 1024 * m:1024 * (m + 1)].rearrange("p (c t) -> p c t", t=128))
                    p.inc(_v, "ve")
                pst_rel[m] = p.v("ve")
            VE_SRC = p.v("ve")

            se_qk = {}
            for idx in range(16):
                isq = idx < 8
                ot = idx % 8
                reg = idx % 2
                slot = idx % 2
                ldw_qk = load_inqk(idx, slot)

                @blkB.tensor
                def _(t, idx=idx, reg=reg, slot=slot, H=H, ldw_qk=ldw_qk):
                    for tup in ldw_qk:
                        p.wt(t, tup)
                    if idx == 0:
                        p.w(t, "ve", VE_SRC)
                        if H == 1:
                            p.w(t, "se", se_exp_b.get(7, 0))
                    if idx >= 2:
                        p.w(t, "se", se_qk[idx - 2])
                    for c in range(NC):
                        for half in range(2):
                            _mm = t.matmul(ps_s[:, 1024 * reg + 512 * half:1024 * reg + 512 * (half + 1)],
                                     inqk_buf[:, slot, c, :],
                                     src_fm[:, c, 512 * half:512 * (half + 1)],
                                     start=(c == 0), stop=(c == NC - 1))
                    p.inc(_mm, "te")
                te_q = p.v("te")
                inqk_rel[slot] = te_q

                @blkB.scalar
                def _(s, isq=isq, ot=ot, reg=reg, te_q=te_q, idx=idx, H=H):
                    p.w(s, "te", te_q)
                    if idx == 0:
                        p.w(s, "ve", VE_QB8)
                        if H == 1:
                            p.w(s, "te", H0_ATTN_TE)
                    dst = (q_fm if isq else k_fm)[:, ot, :]
                    if isq:
                        _a = s.activation(dst, ps_s[:, 1024 * reg:1024 * (reg + 1)], AF.Identity,
                                     bias=qb8[:, ot:ot + 1], scale=ccol(C_INV8))
                    else:
                        _a = s.activation(dst, ps_s[:, 1024 * reg:1024 * (reg + 1)], AF.Identity,
                                     bias=kbc[:, ot:ot + 1], scale=1.0)
                    p.inc(_a, "se")
                se_qk[idx] = p.v("se")
                issue_w2cast(blkB, "se", se_qk[idx])

            ve_v = {}
            for tl in range(8):
                @blkB.tensor
                def _(t, tl=tl):
                    if tl == 0:
                        p.w(t, "ve", VE_SRC)
                        p.wt(t, LDW_INV)
                    if tl >= 1:
                        p.w(t, "ve", ve_v[tl - 1])
                    for c in range(NC):
                        for half in range(2):
                            _mm = t.matmul(ps_a[:, 512 * half:512 * (half + 1)],
                                     src_fm[:, c, 128 * tl:128 * (tl + 1)],
                                     inv_sb[:, c, 512 * half:512 * (half + 1)],
                                     start=(c == 0), stop=(c == NC - 1))
                    p.inc(_mm, "te")
                te_v = p.v("te")

                @blkB.vector
                def _(v, tl=tl, te_v=te_v, H=H):
                    p.w(v, "te", te_v)
                    if H == 1 and tl == 0:
                        p.w(v, "te", H0_ATTN_TE)
                    _v = v.tensor_add(v_tm[:, tl, :], ps_a[:, :], bc_vb[:, :])
                    p.inc(_v, "ve")
                ve_v[tl] = p.v("ve")
            VE_IN_V = p.v("ve")

            def emit_bx(ti, first):
                @blkB.sync
                def _(s, ti=ti, first=first, H=H):
                    if first:
                        p.w(s, "se", se_nrm[8 * H + 7])
                    else:
                        p.w(s, "se", se_srcb.get(ti - 1, 0))
                    p.dma(s, "bx", 0, bxin[:, 0, :], x1p[128 * ti:128 * (ti + 1), :])
                ldx_at[ti] = ("bx0", p.v("bx0"))

            def emit_head1(ti):
                tl = ti - 8 * H
                par = ti % 2

                @blkB.tensor
                def _(t, tl=tl, ti=ti):
                    if tl == 0:
                        p.w(t, "se", se_qk[15])
                        p.w(t, "ve", VE_IN_V)
                    if ti >= 1:
                        p.w(t, "se", se_exp_b.get(ti - 1, 0))
                    for h in range(16):
                        po = 64 * (h % 2)
                        cp = h // 2
                        col = 512 * (h // 4) + 128 * (h % 4)
                        _mm = t.matmul(ps_s[:, col:col + 128],
                                 q_fm[po:po + 64, cp, 128 * tl:128 * (tl + 1)],
                                 k_fm[po:po + 64, cp, 128 * tl:128 * (tl + 1)],
                                 start=True, stop=False)
                        _mm = t.matmul(ps_s[:, col:col + 128], mask17[:, 0:128], mask17[:, 128:256],
                                 start=False, stop=True)
                    p.inc(_mm, "te")
                te_sc = p.v("te")

                @blkB.scalar
                def _(s, te_sc=te_sc, ti=ti, par=par):
                    p.w(s, "te", te_sc)
                    if ti >= 2:
                        p.w(s, "te", te_at.get(ti - 2, 0))
                    for g2 in range(4):
                        _a = s.activation(E_sb[:, par, 512 * g2:512 * (g2 + 1)],
                                     ps_s[:, 512 * g2:512 * (g2 + 1)],
                                     AF.Exp, bias=0.0, scale=1.0)
                    p.inc(_a, "se")
                se_exp_b[ti] = p.v("se")

            def emit_head2(ti):
                par = ti % 2

                @blkB.vector
                def _(v, se_e=se_exp_b[ti], ti=ti, par=par):
                    p.w(v, "se", se_e)
                    _v = v.tensor_reduce(seg16[:, :],
                                    E_sb[:, par, :].rearrange("p (h t) -> p h t", t=128),
                                    axis=AX.X, op=ALU.add)
                    v.drain()
                    _v = v.reciprocal(rec16[:, :], seg16[:, :])
                    v.drain()
                    _v = v.tensor_tensor(A_sb[:, par, :].rearrange("p (h t) -> p h t", t=128),
                                    E_sb[:, par, :].rearrange("p (h t) -> p h t", t=128),
                                    rec16[:, :].unsqueeze(-1).broadcast_to([128, 16, 128]),
                                    ALU.mult)
                    p.inc(_v, "ve")
                ve_an[ti] = p.v("ve")

            def emit_tail1(ti):
                tl = ti - 8 * H
                m = ti % 2
                par = ti % 2
                for half in range(2):
                    @blkB.tensor
                    def _(t, half=half, ve_a=ve_an[ti], pr=pst_rel[half], par=par):
                        p.w(t, "ve", ve_a)
                        p.w(t, "ve", pr)
                        for kk in range(8):
                            hh = 8 * half + kk
                            _mm = t.transpose(pst[:, 1024 * half + 128 * kk:1024 * half + 128 * (kk + 1)],
                                        A_sb[:, par, 128 * hh:128 * (hh + 1)], ident[:, :])
                        p.inc(_mm, "te")
                    te_at[ti] = p.v("te")

                    @blkB.vector
                    def _(v, half=half, te_a=te_at[ti]):
                        p.w(v, "te", te_a)
                        _v = v.tensor_copy(AT_sb[:, 8 * half:8 * (half + 1), :],
                                      pst[:, 1024 * half:1024 * (half + 1)]
                                      .rearrange("p (c t) -> p c t", t=128))
                        p.inc(_v, "ve")
                    pst_rel[half] = p.v("ve")
                    ve_atv[(ti, half)] = p.v("ve")

                @blkB.tensor
                def _(t, tl=tl, ti=ti, m=m):
                    p.w(t, "ve", ve_atv[(ti, 1)])
                    if ti >= 1:
                        p.w(t, "ve", ve_ctxev.get(ti - 1, 0))
                        p.w(t, "ve", ve_z1.get(ti - 1, 0))
                    for h in range(16):
                        po = 64 * (h % 2)
                        col = 512 * (h // 8) + 128 * ((h // 2) % 4)
                        _mm = t.matmul(ps_a[po:po + 64, col:col + 128],
                                 v_tm[:, tl, 64 * h:64 * (h + 1)],
                                 AT_sb[:, h, :],
                                 start=True, stop=True)
                    p.inc(_mm, "te")
                te_cx = p.v("te")

                @blkB.vector
                def _(v, ti=ti, te_cx=te_cx):
                    p.w(v, "te", te_cx)
                    if ti >= 1:
                        p.w(v, "te", te_op_prev.get(ti - 1, 0))
                    for half in range(2):
                        _v = v.tensor_copy(ctx_t[:, 0, 4 * half:4 * (half + 1), :],
                                      ps_a[:, 512 * half:512 * (half + 1)]
                                      .rearrange("p (c t) -> p c t", t=128))
                    p.inc(_v, "ve")
                ve_ctxev[ti] = p.v("ve")

                @blkB.tensor
                def _(t, m=m, ve_c=ve_ctxev[ti]):
                    p.w(t, "ve", ve_c)
                    for c in range(NC):
                        for half in range(2):
                            _mm = t.matmul(ps_a[:, 512 * half:512 * (half + 1)],
                                     ctx_t[:, 0, c, :],
                                     outwT_sb[:, c, 512 * half:512 * (half + 1)],
                                     start=(c == 0), stop=(c == NC - 1))
                    p.inc(_mm, "te")
                te_op_prev[ti] = p.v("te")

            def emit_tail2(ti):
                m = ti % 2

                @blkB.scalar
                def _(s, ti=ti, m=m, ldx_v=ldx_at[ti]):
                    p.wt(s, ldx_v)
                    _a = s.activation(srcb[:, m, :], bxin[:, 0, :], AF.Identity,
                                 bias=nmr2[:, ti:ti + 1], scale=rstd2[:, ti:ti + 1])
                    p.inc(_a, "se")
                se_srcb[ti] = p.v("se")
                issue_w2cast(blkB, "se", se_srcb[ti])
                if ti + 1 <= tiles[-1]:
                    emit_bx(ti + 1, False)

                @blkB.vector
                def _(v, m=m, te_op=te_op_prev[ti], se_sr=se_srcb[ti]):
                    p.w(v, "te", te_op)
                    p.w(v, "se", se_sr)
                    _v = v.tensor_add(srcb[:, m, :], srcb[:, m, :], ps_a[:, 0:1024])
                    v.drain()
                    _v = v.tensor_add(srcb[:, m, :], srcb[:, m, :], bc_outb[:, :])
                    p.inc(_v, "ve")
                ve_z1[ti] = p.v("ve")
                ve_outev[ti] = ve_z1[ti]

                nmr, rstd, ve_ln = ln_stats(blkB, srcb[:, m, :], m, wait_ve=ve_z1[ti],
                                        wait_se=se_n1_prev.get(ti - 2, 0))

                @blkB.scalar
                def _(s, m=m, nmr=nmr, rstd=rstd, ti=ti, ve_ln=ve_ln):
                    p.w(s, "ve", ve_ln)
                    if ti >= 2:
                        p.wt(s, ldx_spill[ti - 2])
                    _a = s.activation(tokf[:, m, :], srcb[:, m, :], AF.Identity, bias=nmr, scale=rstd)
                    p.inc(_a, "se")
                se_n1 = p.v("se")
                se_n1_prev[ti] = se_n1

                @blkB.vector
                def _(v, m=m, se_n1=se_n1, ti=ti):
                    p.w(v, "se", se_n1)
                    _v = v.tensor_mul(tokf[:, m, :], tokf[:, m, :], bc_g1[:, :])
                    v.drain()
                    _v = v.tensor_add(tokf[:, m, :], tokf[:, m, :], bc_b1[:, :])
                    p.inc(_v, "ve")
                ve_tok_ready[ti] = p.v("ve")

                @blkB.scalar
                def _(s, m=m, ti=ti, ve_t=ve_tok_ready[ti]):
                    p.w(s, "ve", ve_t)
                    if ti >= 2:
                        p.w(s, "te", te_toktr[ti - 2])
                    _a = s.activation(tokb[:, m, :], tokf[:, m, :], AF.Identity,
                                 bias=0.0, scale=1.0)
                    p.inc(_a, "se")
                se_tokb[ti] = p.v("se")

                @blkB.sync
                def _(s, ti=ti, m=m, ve_t=ve_tok_ready[ti]):
                    p.w(s, "ve", ve_t)
                    p.dma(s, "sp", m, tokspill[128 * ti:128 * (ti + 1), :], tokf[:, m, :])
                ldx_spill[ti] = ("sp%d" % m, p.v("sp%d" % m))

                toktr_queue.append(ti)

            emit_bx(tiles[0], True)
            for ti in tiles:
                if len(toktr_queue) >= 2:
                    emit_tok_transpose(toktr_queue.pop(0))
                emit_head1(ti)
                if ti > tiles[0]:
                    emit_tail1(ti - 1)
                emit_head2(ti)
                if ti > tiles[0]:
                    emit_tail2(ti - 1)
            if toktr_queue:
                emit_tok_transpose(toktr_queue.pop(0))
            emit_tail1(tiles[-1])
            emit_tail2(tiles[-1])

            if H == 0:
                H0_ATTN_TE = p.v("te")

        while toktr_queue:
            emit_tok_transpose(toktr_queue.pop(0))
        VE_TOKFM = p.v("ve")

        barrier(blkB, "bar2")
        blkB.__exit__(None, None, None)
        pb1.close()

        pbG = ExitStack()
        lfm = sb(pbG, "lfm", [8, 2048], F32)
        ltm = sb(pbG, "ltm", [128, NT, 8], F32)
        gsc = sb(pbG, "gsc", [128, 66], F32)
        grt = sb(pbG, "grt", [128, 3, 128], F32)
        blkG = nc.Block()
        blkG.__enter__()

        se_lfm = [0] * 4
        for nb in range(4):
            @blkG.tensor
            def _(t, nb=nb):
                if nb == 0:
                    p.w(t, "bar2", p.v("bar2"))
                    p.w(t, "gwb0", p.v("gwb0"))
                if nb >= 2:
                    p.w(t, "se", se_lfm[nb - 2])
                for c in range(NC):
                    _mm = t.matmul(ps_a[0:8, 512 * (nb % 2):512 * (nb % 2) + 512],
                             gw_sb[:, c, :], tok_fm[:, c, 512 * nb:512 * (nb + 1)],
                             start=(c == 0), stop=(c == NC - 1))
                p.inc(_mm, "te")
            te_g = p.v("te")

            @blkG.scalar
            def _(s, nb=nb, te_g=te_g):
                p.w(s, "te", te_g)
                if nb == 0:
                    p.wt(s, LDX_PB)
                _a = s.activation(lfm[0:8, 512 * nb:512 * (nb + 1)],
                             ps_a[0:8, 512 * (nb % 2):512 * (nb % 2) + 512],
                             AF.Identity, bias=gb_sb[:, 0:1], scale=1.0)
                p.inc(_a, "se")
            se_lfm[nb] = p.v("se")

        @blkG.tensor
        def _(t):
            p.w(t, "se", se_lfm[3])
            for i in range(NT):
                _mm = t.transpose(ps_s[:, 8 * i:8 * (i + 1)], lfm[0:8, 128 * i:128 * (i + 1)],
                            ident32[0:8, 0:8])
            p.inc(_mm, "te")
        TE_LT = p.v("te")

        @blkG.vector
        def _(v):
            p.w(v, "te", TE_LT)
            _v = v.tensor_copy(ltm[:, :, :], ps_s[:, 0:128].rearrange("p (i e) -> p i e", e=8))
            p.inc(_v, "ve")
        VE_LTM = p.v("ve")

        ltm3 = ltm[:, :, :]
        ltm_f = ltm[:, :, :].rearrange("p i e -> p (i e)")

        def b16(ap2d):
            return ap2d.unsqueeze(-1).broadcast_to([128, 16, 8])

        @blkG.vector
        def _(v):
            p.w(v, "ve", VE_LTM)
            _v = v.tensor_reduce(gsc[:, 0:16], ltm3, axis=AX.X, op=ALU.max)
            v.drain()
            _v = v.tensor_tensor(grt[:, 0, :].rearrange("p (i e) -> p i e", e=8),
                            ltm3, b16(gsc[:, 0:16]), ALU.is_equal)
            v.drain()
            _v = v.scalar_tensor_tensor(grt[:, 0, :], grt[:, 0, :], ccol(C_NEGBIG),
                                   ltm_f, ALU.mult, ALU.add)
            v.drain()
            _v = v.tensor_reduce(gsc[:, 16:32],
                            grt[:, 0, :].rearrange("p (i e) -> p i e", e=8),
                            axis=AX.X, op=ALU.max)
            v.drain()
            _v = v.tensor_tensor(grt[:, 1, :].rearrange("p (i e) -> p i e", e=8),
                            ltm3, b16(gsc[:, 16:32]), ALU.is_ge)
            v.drain()
            _v = v.tensor_tensor(grt[:, 0, :].rearrange("p (i e) -> p i e", e=8),
                            ltm3, b16(gsc[:, 0:16]), ALU.subtract)
            p.inc(_v, "ve")
        ve_rt = p.v("ve")

        @blkG.scalar
        def _(s, ve_rt=ve_rt):
            p.w(s, "ve", ve_rt)
            _a = s.activation(grt[:, 2, :], grt[:, 0, :], AF.Exp, bias=0.0, scale=1.0)
            p.inc(_a, "se")
        se_e8 = p.v("se")

        @blkG.vector
        def _(v, se_e8=se_e8):
            p.w(v, "se", se_e8)
            _v = v.tensor_mul(grt[:, 1, :], grt[:, 2, :], grt[:, 1, :])
            v.drain()
            _v = v.tensor_reduce(gsc[:, 32:48],
                            grt[:, 1, :].rearrange("p (i e) -> p i e", e=8),
                            axis=AX.X, op=ALU.add)
            v.drain()
            _v = v.reciprocal(gsc[:, 48:64], gsc[:, 32:48])
            v.drain()
            _v = v.tensor_tensor(combine[:, :, :],
                            grt[:, 1, :].rearrange("p (i e) -> p i e", e=8),
                            b16(gsc[:, 48:64]), ALU.mult)
            v.drain()
            _v = v.tensor_copy(combine_bf[:, :, :], combine[:, :, :])
            p.inc(_v, "ve")
        VE_COMB = p.v("ve")

        @blkG.tensor
        def _(t, pr0=pst_rel[0], pr1=pst_rel[1]):
            p.w(t, "ve", VE_COMB)
            p.w(t, "ve", pr0)
            p.w(t, "ve", pr1)
            for i in range(NT):
                _mm = t.transpose(pst[0:8, 128 * i:128 * (i + 1)], combine_bf[:, i, :], ident[:, :])
            p.inc(_mm, "te")
        TE_CT = p.v("te")

        @blkG.vector
        def _(v):
            p.w(v, "te", TE_CT)
            _v = v.tensor_copy(combT_sb[:, :, :], pst[0:8, 0:2048].rearrange("p (i t) -> p i t", t=128))
            p.inc(_v, "ve")
        VE_CT = p.v("ve")
        pst_rel = [VE_CT, VE_CT]

        barrier(blkG, "bar2")
        blkG.__exit__(None, None, None)
        pbG.close()

        pb2 = ExitStack()
        h_sb = sb(pb2, "h_sb", [128, 16, 1024], BF16)
        w2sb = sb(pb2, "w2sb", [128, 16, 1024], BF16)
        moe_acc = sb(pb2, "moe_acc", [128, 2, 8, 1024], F32)
        w1buf = sb(pb2, "w1buf", [128, 4, NC, 128], BF16)
        b1c = sb(pb2, "b1c", [128, 32], F32)
        fxb = sb(pb2, "fxb", [128, 1, 1024], F32)
        fnb = sb(pb2, "fnb", [128, 1, 1024], F32)
        ftk = sb(pb2, "ftk", [128, 1, 1024], F32)
        bc_g2 = sb(pb2, "bc_g2", [128, 1024], F32)
        bc_b2 = sb(pb2, "bc_b2", [128, 1024], F32)
        vstage2 = sb(pb2, "vstage2", [1, 1024], F32)

        blkM = nc.Block()
        blkM.__enter__()

        VE_BC2 = build_bcasts(blkM, [
            (_r1(lng_d[2]), bc_g2),
            (_r1(lng_d[3]), bc_b2),
        ], vstage2, wait0_ve=VE_CT)

        prev_y_te = 0
        prev_y_ve = 0
        prev_hev14 = None
        fin_se_prev = [0]

        W1SEQ = [(e_, 16 * fh_ + ft_) for _bk in range(2) for e_ in range(NE)
                 for fh_ in range(2) for ft_ in range(16)]
        w1_gi = [0]
        w1_ld = {}

        def issue_w1(gate_te, n=1):
            gis = list(range(w1_gi[0], min(w1_gi[0] + n, len(W1SEQ))))
            if not gis:
                return
            w1_gi[0] = gis[-1] + 1

            @blkM.gpsimd
            def _(g, gate_te=gate_te, gis=gis):
                p.w(g, "te", gate_te)
                if gis[0] == 0:
                    p.w(g, "bar2", p.v("bar2"))
                for gi in gis:
                    e_, t_ = W1SEQ[gi]
                    src = w1T_d[e_, t_].rearrange("p (c o) -> p c o", o=128)
                    s_ = gi % 4
                    w1_ld[gi] = [
                        p.dma(g, "w1a", s_, w1buf[:, s_, 0:4, :], src[:, 0:4, :]),
                        p.dma(g, "w1b", s_, w1buf[:, s_, 4:8, :], src[:, 4:8, :]),
                    ]
        sto_rel = [None, None]
        last_spill = [ldx_spill[14], ldx_spill[15]]
        if last_spill[0][0] != "sp0":
            last_spill = [ldx_spill[15], ldx_spill[14]]

        def emit_final(blki, tt8):
            ti = 8 * blki + tt8
            m = tt8 % 2
            vm = ve_moe[tt8]

            @blkM.sync
            def _(s, ti=ti, m=m, sr=sto_rel[0], spv=last_spill[m]):
                p.wt(s, sr)
                p.wt(s, spv)
                p.dma(s, "fl", 0, ftk[:, 0, :], tokspill[128 * ti:128 * (ti + 1), :])
                p.dma(s, "fl", 0, fxb[:, 0, :], x1p[128 * ti:128 * (ti + 1), :])
            ldx_f = ("fl0", p.v("fl0"))

            @blkM.vector
            def _(v, blki=blki, tt8=tt8, ldx_f=ldx_f, vm=vm):
                p.wt(v, ldx_f)
                p.w(v, "ve", vm)
                _v = v.tensor_add(ftk[:, 0, :], ftk[:, 0, :], moe_acc[:, blki, tt8, :])
                _v = v.tensor_add(fxb[:, 0, :], fxb[:, 0, :], bc_b2[:, :])
                p.inc(_v, "ve")
            ve_fz = p.v("ve")

            nmr, rstd, ve_ln = ln_stats(blkM, ftk[:, 0, :], m, wait_ve=ve_fz,
                                        wait_se=fin_se_prev[0])

            @blkM.scalar
            def _(s, nmr=nmr, rstd=rstd, ve_ln=ve_ln):
                p.w(s, "ve", ve_ln)
                _a = s.activation(fnb[:, 0, :], ftk[:, 0, :], AF.Identity, bias=nmr, scale=rstd)
                p.inc(_a, "se")
            se_fn = p.v("se")
            fin_se_prev[0] = se_fn

            @blkM.vector
            def _(v, se_fn=se_fn):
                p.w(v, "se", se_fn)
                _v = v.tensor_mul(fnb[:, 0, :], fnb[:, 0, :], bc_g2[:, :])
                v.drain()
                _v = v.tensor_add(fnb[:, 0, :], fnb[:, 0, :], fxb[:, 0, :])
                p.inc(_v, "ve")
            ve_fo = p.v("ve")

            @blkM.sync
            def _(s, ti=ti, ve_fo=ve_fo):
                p.w(s, "ve", ve_fo)
                p.dma(s, "st", 0, y_out[128 * ti:128 * (ti + 1), :], fnb[:, 0, :])
            sto_rel[0] = ("st0", p.v("st0"))

        for blki in range(2):
            ve_ini = {}
            for tt8 in range(8):
                @blkM.tensor
                def _(t, tt8=tt8, blki=blki):
                    if tt8 == 0:
                        p.w(t, "bar2", p.v("bar2"))
                        p.w(t, "ve", VE_BC2)
                        p.w(t, "gwb0", p.v("gwb0"))
                        if blki == 1:
                            p.w(t, "ve", prev_y_ve)
                    if tt8 >= 1:
                        p.w(t, "ve", ve_ini[tt8 - 1])
                    for half in range(2):
                        _mm = t.matmul(ps_a[:, 512 * half:512 * (half + 1)],
                                 combT_sb[:, 8 * blki + tt8, :],
                                 b2_sb[:, 512 * half:512 * (half + 1)],
                                 start=True, stop=True)
                    p.inc(_mm, "te")
                te_i = p.v("te")

                @blkM.vector
                def _(v, tt8=tt8, te_i=te_i, blki=blki):
                    p.w(v, "te", te_i)
                    _v = v.tensor_copy(moe_acc[:, blki, tt8, :], ps_a[:, :])
                    p.inc(_v, "ve")
                ve_ini[tt8] = p.v("ve")
            VE_INI = p.v("ve")

            ve_moe = {t_: VE_INI for t_ in range(8)}

            if blki == 0:
                issue_w1(0, 4)

            for e in range(NE):
                for fh in range(2):
                    gi0 = ((blki * NE + e) * 2 + fh) * 16
                    te_h = {}
                    hev_rel = {}

                    if fh == 0:
                        @blkM.gpsimd
                        def _(g, e=e, blki=blki, pyt=prev_y_te):
                            if e == 0 and blki == 0:
                                p.w(g, "bar2", p.v("bar2"))
                            p.w(g, "te", pyt)
                            p.dma(g, "b1", 0, b1c[:, :], b1_d[e])

                    @blkM.sync
                    def _(s, e=e, fh=fh, blki=blki, pyt=prev_y_te):
                        if e == 0 and fh == 0 and blki == 0:
                            p.w(s, "bar2", p.v("bar2"))
                        p.w(s, "te", pyt)
                        p.w(s, "wc%d" % e, 128)
                        for ftt in range(16):
                            p.dma(s, "w2", 0, w2sb[:, ftt, :],
                                  w2bf_d[e, 2048 * fh + 128 * ftt:
                                         2048 * fh + 128 * (ftt + 1), :])
                    ldw2seg = ("w20", p.v("w20"))

                    for ft in range(16):
                        gi = gi0 + ft

                        @blkM.tensor
                        def _(t, ft=ft, gi=gi, blki=blki, ldwv=w1_ld[gi], pyv=prev_y_ve,
                              b1v=("b10", p.v("b10")), ph=prev_hev14):
                            for tup in ldwv:
                                p.wt(t, tup)
                            if ft == 0:
                                p.wt(t, b1v)
                                if ph is not None:
                                    p.w(t, ph[0], ph[1])
                            if ft == 1:
                                p.w(t, "ve", pyv)
                            if ft >= 2:
                                rel = hev_rel[ft - 2]
                                p.w(t, rel[0], rel[1])
                            r_ = (ft + 1) % 2
                            for c in range(NC):
                                for sub in range(2):
                                    _mm = t.matmul(ps_s[:, 1024 * r_ + 512 * sub:
                                                  1024 * r_ + 512 * (sub + 1)],
                                             w1buf[:, gi % 4, c, :],
                                             tok_fm[:, c, 1024 * blki + 512 * sub:
                                                    1024 * blki + 512 * (sub + 1)],
                                             start=(c == 0), stop=(c == NC - 1))
                            p.inc(_mm, "te")
                        te_h[ft] = p.v("te")
                        issue_w1(te_h[ft])

                        r_ = (ft + 1) % 2

                        @blkM.vector
                        def _(v, fh=fh, ft=ft, r_=r_, te_v=te_h[ft], pyt=prev_y_te):
                            p.w(v, "te", te_v)
                            p.w(v, "te", pyt)
                            _v = v.scalar_tensor_tensor(h_sb[:, ft, :],
                                                   ps_s[:, 1024 * r_:1024 * (r_ + 1)],
                                                   b1c[:, 16 * fh + ft:16 * fh + ft + 1],
                                                   zeros_bf[:, :], ALU.add, ALU.max)
                            p.inc(_v, "ve")
                        hev_rel[ft] = ("ve", p.v("ve"))

                    ve_yev = {}
                    for tt8 in range(8):
                        @blkM.tensor
                        def _(t, tt8=tt8, h14=hev_rel[14], h15=hev_rel[15], ldwv=ldw2seg):
                            if tt8 == 0:
                                p.w(t, h14[0], h14[1])
                                p.w(t, h15[0], h15[1])
                                p.wt(t, ldwv)
                            if tt8 >= 2:
                                p.w(t, "ve", ve_yev[tt8 - 2])
                            reg = ps_a if tt8 % 2 == 0 else ps_s
                            for fs in range(16):
                                for half in range(2):
                                    _mm = t.matmul(reg[:, 512 * half:512 * (half + 1)],
                                             h_sb[:, fs, 128 * tt8:128 * (tt8 + 1)],
                                             w2sb[:, fs, 512 * half:512 * (half + 1)],
                                             start=(fs == 0), stop=(fs == 15))
                            p.inc(_mm, "te")
                        te_y = p.v("te")

                        @blkM.vector
                        def _(v, e=e, tt8=tt8, blki=blki, te_y=te_y, vm=ve_moe[tt8]):
                            p.w(v, "te", te_y)
                            p.w(v, "ve", vm)
                            reg = ps_a if tt8 % 2 == 0 else ps_s
                            _v = v.scalar_tensor_tensor(moe_acc[:, blki, tt8, :], reg[:, 0:1024],
                                                   combine[:, 8 * blki + tt8, e:e + 1],
                                                   moe_acc[:, blki, tt8, :], ALU.mult, ALU.add)
                            p.inc(_v, "ve")
                        ve_yev[tt8] = p.v("ve")
                        ve_moe[tt8] = p.v("ve")
                        if e == NE - 1 and fh == 1:
                            emit_final(blki, tt8)

                    prev_y_te = p.v("te")
                    prev_y_ve = ve_yev[7]
                    prev_hev14 = hev_rel[14]

        @blkM.sync
        def _(s):
            p.wt(s, sto_rel[0])
        blkM.__exit__(None, None, None)
        pb2.close()
        pbP.close()

    return nc


def _prep_inputs(inputs):
    W_qkv = np.asarray(inputs["L0_attn_W_qkv"], np.float32)
    share = {
        "wvT": np.ascontiguousarray(W_qkv[2::3].T.reshape(8, 128, D).transpose(1, 0, 2)),
        "wqkT": np.ascontiguousarray(np.stack([W_qkv[i::3].T for i in range(2)])
                                     .reshape(2, D, 8, 128).transpose(0, 2, 1, 3)
                                     .reshape(2, 8, 8, 128, 128).transpose(0, 1, 3, 2, 4)
                                     .reshape(2, 8, 128, D)),
        "woT": np.ascontiguousarray(np.asarray(inputs["L0_attn_W_out"], np.float32).T
                                    .reshape(8, 128, D).transpose(1, 0, 2)),
        "inqkT": np.ascontiguousarray(np.asarray(inputs["L0_ffn_in_proj_w"], np.float32).T[:, 0:2048]
                                      .reshape(D, 16, 128).transpose(1, 0, 2)
                                      .reshape(16, 8, 128, 128).transpose(0, 2, 1, 3)
                                      .reshape(16, 128, D)),
        "invT": np.ascontiguousarray(np.asarray(inputs["L0_ffn_in_proj_w"], np.float32).T[:, 2048:3072]
                                     .reshape(8, 128, D).transpose(1, 0, 2)),
        "inb": np.ascontiguousarray(np.asarray(inputs["L0_ffn_in_proj_b"], np.float32)),
        "inqkb": np.ascontiguousarray(
            np.asarray(inputs["L0_ffn_in_proj_b"], np.float32)[0:2048]
            .reshape(2, 8, 128).transpose(0, 2, 1)),
        "outwT": np.ascontiguousarray(np.asarray(inputs["L0_ffn_out_w"], np.float32).T
                                      .reshape(8, 128, D).transpose(1, 0, 2)),
        "outb": np.ascontiguousarray(np.asarray(inputs["L0_ffn_out_b"], np.float32)),
        "gwT": np.ascontiguousarray(np.asarray(inputs["L0_ffn_gate_w"], np.float32).T
                                    .reshape(8, 128, NE).transpose(1, 0, 2)),
        "gb": np.ascontiguousarray(np.asarray(inputs["L0_ffn_gate_b"], np.float32).reshape(NE, 1)),
        "w1T": np.ascontiguousarray(
            np.asarray(inputs["L0_ffn_w1"], np.float32).transpose(0, 2, 1)
            .reshape(NE, D, 32, 128).transpose(0, 2, 1, 3)
            .reshape(NE, 32, 8, 128, 128).transpose(0, 1, 3, 2, 4)
            .reshape(NE, 32, 128, D)),
        "b1": np.ascontiguousarray(np.asarray(inputs["L0_ffn_b1"], np.float32)
                                   .reshape(NE, 32, 128).transpose(0, 2, 1)),
        "w2T": np.ascontiguousarray(np.asarray(inputs["L0_ffn_w2"], np.float32).transpose(0, 2, 1)),
        "b2": np.ascontiguousarray(np.asarray(inputs["L0_ffn_b2"], np.float32)),
        "lng": np.ascontiguousarray(np.stack([
            np.asarray(inputs["L0_ffn_ln1_g"], np.float32),
            np.asarray(inputs["L0_ffn_ln1_b"], np.float32),
            np.asarray(inputs["L0_ffn_ln2_g"], np.float32),
            np.asarray(inputs["L0_ffn_ln2_b"], np.float32)])),
    }
    x = np.asarray(inputs["L0_attn_x"], np.float32)
    in_maps = []
    for c in range(8):
        m = dict(share)
        m["x"] = np.ascontiguousarray(x[c])
        in_maps.append(m)
    return in_maps


def kernel(**inputs):
    in_maps = _prep_inputs(inputs)
    nc = build_program("AB")
    res = run_bass_kernel_spmd(nc, in_maps, core_ids=list(range(8)))
    out = np.empty((B, S, D), np.float32)
    for c in range(8):
        yc = np.asarray(res.results[c]["y"], np.float32)
        out[:, 256 * c:256 * (c + 1), :] = yc.reshape(256, 8, D).transpose(1, 0, 2)
    return out
